# version 08
# baseline (speedup 1.0000x reference)
import math
import sys
from contextlib import ExitStack

import numpy as np

if "/opt/trn_rl_repo" not in sys.path:
    sys.path.insert(0, "/opt/trn_rl_repo")
import concourse.bass as bass
import concourse.mybir as mybir
from concourse.bass_utils import run_bass_kernel_spmd

F32 = mybir.dt.float32
BF16 = mybir.dt.bfloat16
AF = mybir.ActivationFunctionType
ALU = mybir.AluOpType
AX = mybir.AxisListType

D = 2048
DC = 16
V = 1536
VT = 12
S0 = 2048
H = 16
DH = 128
QP = 512
FF0 = 8192
E0 = 2
E1 = 8
I1 = 1408
LN_EPS = 1e-5
RMS_EPS = 1e-6
NEG = -1e30
INVSD = 1.0 / math.sqrt(DH)
NCORES = 8

NG_V = [(0, 512), (512, 512), (1024, 512)]



def _retile_lhst(wT, blk):
    K, M = wT.shape
    kc = K // 128
    mb = M // blk
    a = wT.reshape(kc, 128, mb, blk).transpose(2, 1, 0, 3)
    return np.ascontiguousarray(a.reshape(mb, 128, kc * blk))


def _rope_perm():
    p = np.empty(DH, dtype=np.int64)
    p[0:64] = np.arange(0, DH, 2)
    p[64:128] = np.arange(1, DH, 2)
    return p


def stage_shared(I):
    import ml_dtypes
    bf = ml_dtypes.bfloat16
    f32 = np.float32
    asc = np.ascontiguousarray
    s = {}
    s["w_dq"] = asc(np.asarray(I["L0_attn_W_dq"], dtype=f32))
    s["w_uq"] = asc(np.asarray(I["L0_attn_W_uq"], dtype=f32))
    s["w_dkv"] = asc(np.asarray(I["L0_attn_W_dkv"], dtype=f32))
    s["w_ukv"] = asc(np.asarray(I["L0_attn_W_ukv"], dtype=f32))
    w_oT = np.asarray(I["L0_attn_W_o"], dtype=f32).T
    s["wo_t"] = asc(w_oT.reshape(16, 128, 16, 128).transpose(2, 1, 0, 3)
                    .reshape(16, 128, 2048))
    s["qgb"] = asc(np.stack([np.asarray(I["L0_attn_q_ln_g"], dtype=f32),
                             np.asarray(I["L0_attn_q_ln_b"], dtype=f32)]))
    s["kvgb"] = asc(np.stack([np.asarray(I["L0_attn_kv_ln_g"], dtype=f32),
                              np.asarray(I["L0_attn_kv_ln_b"], dtype=f32)]))
    s["gate_wT"] = asc(np.asarray(I["L0_ffn_gate_w"], dtype=f32).T)
    wg = np.asarray(I["L0_ffn_w_gate"], dtype=f32)
    wu = np.asarray(I["L0_ffn_w_up"], dtype=f32)
    wd = np.asarray(I["L0_ffn_w_down"], dtype=f32)
    s["wg_t"] = np.stack([_retile_lhst(asc(wg[e].T), 128) for e in range(E0)])
    s["wu_t"] = np.stack([_retile_lhst(asc(wu[e].T), 128) for e in range(E0)])
    s["wd_t"] = np.stack([_retile_lhst(asc(wd[e].T), 128) for e in range(E0)])
    s["lnw"] = asc(np.asarray(I["L1_attn_ln_w"], dtype=f32))
    perm = _rope_perm()
    wq = np.asarray(I["L1_attn_wq"], dtype=f32)
    wq_p = np.concatenate([wq[:, h * DH:(h + 1) * DH][:, perm] for h in range(H)], axis=1)
    wk = np.asarray(I["L1_attn_wk"], dtype=f32)
    wk_p = np.concatenate([wk[:, kh * DH:(kh + 1) * DH][:, perm] for kh in range(4)], axis=1)
    s["wq_t"] = _retile_lhst(asc(wq_p), DH)
    s["wk_t"] = _retile_lhst(asc(wk_p), DH)
    s["wv"] = asc(np.asarray(I["L1_attn_wv"], dtype=f32))
    wo1 = np.asarray(I["L1_attn_wo"], dtype=f32)
    s["wo1_t"] = asc(wo1.reshape(16, 128, 16, 128).transpose(2, 1, 0, 3)
                     .reshape(16, 128, 2048))
    s["rwT"] = asc(np.asarray(I["L1_ffn_router_w"], dtype=f32).T)
    w1 = np.asarray(I["L1_ffn_w1"], dtype=f32)
    w2 = np.asarray(I["L1_ffn_w2"], dtype=f32)
    s["w1_t"] = np.stack([_retile_lhst(asc(w1[e].T), 128) for e in range(E1)])
    s["w2_t"] = np.stack([_retile_lhst(asc(w2[e].T), 128) for e in range(E1)])
    s["ident_bf"] = np.eye(128, dtype=bf)
    s["ident_f32"] = np.eye(128, dtype=f32)
    s["onesrow_bf"] = np.ones((1, 128), dtype=bf)
    s["onesrow_f32"] = np.ones((1, 128), dtype=f32)
    s["onescol_f32"] = np.ones((128, 1), dtype=f32)
    s["onescol_bf"] = np.ones((128, 1), dtype=bf)
    q_ = np.arange(128)[None, :]
    k_ = np.arange(128)[:, None]
    triL = np.where(q_ > k_, NEG, 0.0).astype(f32)
    triU = np.where(q_ < k_, NEG, 0.0).astype(f32)
    full = np.full((128, 128), NEG, dtype=f32)
    zero = np.zeros((128, 128), dtype=f32)
    l1m = np.zeros((8, 128, 512), dtype=f32)
    for j in range(8):
        for i in range(4):
            dji = j - i
            t = full if (dji < 0 or dji > 4) else (triL if dji == 0 else (triU if dji == 4 else zero))
            l1m[j, :, i * 128:(i + 1) * 128] = t
    s["l1_mask"] = l1m.astype(bf)
    return s


def stage_core(I, core):
    import ml_dtypes
    bf = ml_dtypes.bfloat16
    f32 = np.float32
    b, par = core // 2, core % 2
    x = np.asarray(I["L0_attn_x"], dtype=f32)
    d = {}
    if par == 0:
        xq = np.zeros((V, D), dtype=f32)
        xq[512:, :] = x[b, 0:1024, :]
        xkv = np.zeros((S0, D), dtype=f32)
        xkv[0:1024, :] = x[b, 0:1024, :]
    else:
        xq = x[b, 512:2048, :]
        xkv = x[b, :, :]
    d["xqT"] = np.ascontiguousarray(xq.T)
    d["xkvT"] = np.ascontiguousarray(xkv.T)

    q_ = np.arange(128)[None, :]
    k_ = np.arange(128)[:, None]
    tri = np.where(k_ > q_, NEG, 0.0).astype(f32)
    fullm = np.full((128, 128), NEG, dtype=f32)
    zerom = np.zeros((128, 128), dtype=f32)
    l0_bnd = np.zeros((3, 8, 128, 512), dtype=f32)
    for g in range(3):
        for sl in range(8):
            kc = (4 * g - 4 + sl) if sl < 4 else (4 * g + sl)
            if kc < 0:
                continue
            for i in range(4):
                qt = 4 * g + i
                bnd = (qt - 4) if par == 0 else (qt + 4)
                t = tri if kc == bnd else (fullm if kc > bnd else zerom)
                l0_bnd[g, sl, :, i * 128:(i + 1) * 128] = t
    d["l0_bnd"] = l0_bnd.astype(bf)
    l0_mcol = np.ones((3, 16), dtype=f32)
    for g in range(3):
        nk = 4 * g + 8
        for kc in range(16):
            if kc >= nk:
                l0_mcol[g, kc] = 0.0
            elif par == 0:
                if g == 0:
                    l0_mcol[g, kc] = 1.0 if kc == 0 else 0.0
                elif kc >= 4 * g:
                    l0_mcol[g, kc] = 0.0
    d["l0_mcol"] = np.ascontiguousarray(
        np.broadcast_to(l0_mcol[:, :, None], (3, 16, 128)).astype(f32))
    d["l0_mcol_bf"] = d["l0_mcol"].astype(bf)

    l1_mcol = np.ones((2, 8), dtype=f32)
    if par == 0:
        l1_mcol[0, 0:4] = 0.0
    d["l1_mcol"] = np.ascontiguousarray(
        np.broadcast_to(l1_mcol[:, :, None], (2, 8, 128)).astype(f32))
    d["l1_mcol_bf"] = d["l1_mcol"].astype(bf)

    inv = 1.0 / (10000.0 ** (np.arange(0, DH, 2, dtype=np.float64) / DH))
    gpos = np.arange(V, dtype=np.float64) + (512.0 if par == 1 else -512.0)
    ang = inv[:, None] * gpos[None, :]
    cos = np.cos(ang).astype(f32)
    sin = np.sin(ang).astype(f32)
    if par == 0:
        cos[:, 0:512] = 1.0
        sin[:, 0:512] = 0.0
    d["ropec"] = np.ascontiguousarray(cos)
    d["ropes"] = np.ascontiguousarray(sin)
    return d



ENGS = ("sy", "pe", "ve", "se", "gp")
_ENG_ATTR = {"sy": "sync", "pe": "tensor", "ve": "vector", "se": "scalar", "gp": "gpsimd"}


class MatNode:
    __slots__ = ("eng", "seq", "val")

    def __init__(self, eng, seq):
        self.eng = eng
        self.seq = seq
        self.val = None


class Rec:
    __slots__ = ("fn", "ewaits", "dwaits", "nodes", "dinc")

    def __init__(self, fn):
        self.fn = fn
        self.ewaits = []
        self.dwaits = []
        self.nodes = []
        self.dinc = None


class Buf:
    __slots__ = ("w", "r", "name")

    def __init__(self, name=""):
        self.w = None
        self.r = {}
        self.name = name


class Em:
    def __init__(self, c):
        self.c = c
        self.q = {e: [] for e in ENGS}
        self.esem = {e: c.sem("E_" + e) for e in ENGS}
        self.mats = {e: {} for e in ENGS}
        self.flushed_mats = {e: [] for e in ENGS}
        self.base = {e: 0 for e in ENGS}
        self.dsem = {}
        self.settled = {}
        self.last_dma = {}
        self.pending = {}
        self.fence_seen = {}

    def fence(self):
        toks = []
        for e in ENGS:
            loc = len(self.q[e]) - 1
            while loc >= 0 and self.q[e][loc].dinc is not None:
                loc -= 1
            if loc >= 0:
                node = self.mats[e].get(loc)
                if node is None:
                    node = MatNode(e, loc)
                    self.mats[e][loc] = node
                    self.q[e][loc].nodes.append(node)
                toks.append(("E", e, self.base[e] + loc))
        for name, sem in self.dsem.items():
            if sem.v > self.fence_seen.get(name, 0):
                toks.append(("D", sem, sem.v))
                self.fence_seen[name] = sem.v
        self.pending = {e: list(toks) for e in ENGS}

    def _fence_deps(self, rec, eng):
        toks = self.pending.pop(eng, None)
        if toks:
            for tk in toks:
                self._dep(rec, eng, tk)

    def seal(self, bufs, slot):
        sem = self.stream(slot)
        tok = ("D", sem, sem.v)
        for b in bufs:
            b.w = tok

    def _etok(self, eng):
        return ("E", eng, self.base[eng] + len(self.q[eng]) - 1)

    def stream(self, name):
        if name not in self.dsem:
            self.dsem[name] = self.c.sem("D_" + name)
        return self.dsem[name]

    def _dep(self, rec, eng, tok):
        if tok is None:
            return
        kind = tok[0]
        if kind == "D":
            rec.dwaits.append((tok[1], tok[2]))
            return
        peng, s = tok[1], tok[2]
        if peng == eng:
            if peng == "pe":
                return
            rec.ewaits.append(("DRAIN", peng, s))
            return
        loc = s - self.base[peng]
        if loc < 0:
            hist = self.flushed_mats[peng]
            cover = None
            for hs, hv in hist:
                if hs >= s:
                    cover = (self.esem[peng], hv)
                    break
            if cover is None:
                return
            rec.dwaits.append(cover)
            return
        prec = self.q[peng][loc]
        if prec.dinc is not None:
            rec.dwaits.append(prec.dinc)
            return
        node = self.mats[peng].get(loc)
        if node is None:
            node = MatNode(peng, loc)
            self.mats[peng][loc] = node
            self.q[peng][loc].nodes.append(node)
        rec.ewaits.append(("SEM", peng, node))

    def op(self, eng, fn, reads=(), writes=()):
        rec = Rec(fn)
        self.q[eng].append(rec)
        self._fence_deps(rec, eng)
        tok = self._etok(eng)
        for b in reads:
            self._dep(rec, eng, b.w)
        for b in writes:
            self._dep(rec, eng, b.w)
            for rt in b.r.values():
                self._dep(rec, eng, rt)
        for b in reads:
            b.r[eng] = tok
        for b in writes:
            b.w = tok
            b.r = {}
        return tok

    def dma(self, eng, slot, out, in_, reads=(), writes=(), **kw):
        sem = self.stream(slot)

        def fn(e, out=out, in_=in_, kw=kw):
            return e.dma_start(out=out, in_=in_, **kw)
        rec = Rec(fn)
        self.q[eng].append(rec)
        self._fence_deps(rec, eng)
        for b in reads:
            self._dep(rec, eng, b.w)
        for b in writes:
            if not (b.w is not None and b.w[0] == "D" and b.w[1] is sem):
                self._dep(rec, eng, b.w)
            for rt in b.r.values():
                if not (rt[0] == "D" and rt[1] is sem):
                    self._dep(rec, eng, rt)
        sem.v += 16
        rec.dinc = (sem, sem.v)
        self.last_dma[eng] = (sem, sem.v)
        tok = ("D", sem, sem.v)
        etok = self._etok(eng)
        for b in reads:
            b.r["D:" + slot] = tok
        for b in writes:
            b.w = tok
            b.r = {}
        return tok

    def barrier_tok(self, eng):
        n = self.base[eng] + len(self.q[eng])
        if n == 0:
            return None
        return ("E", eng, n - 1)

    def flush(self, blk):
        for e in ENGS:
            nodes = sorted(self.mats[e].values(), key=lambda n: n.seq)
            sem = self.esem[e]
            for n in nodes:
                sem.v += 1
                n.val = sem.v
                self.flushed_mats[e].append((self.base[e] + n.seq, n.val))
        wm = {e: {} for e in ENGS}

        tail = []
        for name, sem in self.dsem.items():
            if self.settled.get(name, 0) < sem.v:
                tail.append((sem, sem.v))
                self.settled[name] = sem.v
        if tail:
            def fn_settle(e, tail=tail):
                for sem, val in tail[:-1]:
                    e.wait_ge(sem.h, val)
                return e.wait_ge(tail[-1][0].h, tail[-1][1])
            rec = Rec(fn_settle)
            self.q["sy"].append(rec)

        for e in ENGS:
            recs = self.q[e]
            if not recs:
                continue

            def body(eng, recs=recs, e=e, wm=wm[e], base=self.base[e]):
                drained_upto = -1
                for ri, rec in enumerate(recs):
                    need_drain = False
                    for w in rec.ewaits:
                        if w[0] == "DRAIN":
                            s = w[2]
                            if drained_upto < s:
                                need_drain = True
                        else:
                            _, peng, node = w
                            sem = self.esem[peng]
                            if wm.get(id(sem), 0) < node.val:
                                eng.wait_ge(sem.h, node.val)
                                wm[id(sem)] = node.val
                    for sem, val in rec.dwaits:
                        if wm.get(id(sem), 0) < val:
                            eng.wait_ge(sem.h, val)
                            wm[id(sem)] = val
                    if need_drain:
                        eng.drain()
                        drained_upto = base + ri - 1
                    instr = rec.fn(eng)
                    for n in rec.nodes:
                        instr.then_inc(self.esem[e].h, 1)
                    if rec.dinc is not None:
                        instr.then_inc(rec.dinc[0].h, 16)
            getattr(blk, _ENG_ATTR[e])(body)
        for e in ENGS:
            self.base[e] += len(self.q[e])
            self.q[e] = []
            self.mats[e] = {}


class Ctx:
    def __init__(self):
        self.nc = bass.Bass("TRN2", target_bir_lowering=False, debug=False)
        self.es = ExitStack()
        self.sems = {}
        self.t = {}
        self.sb = {}
        self.nsb = 0

    def sem(self, name):
        if name not in self.sems:
            h = self.es.enter_context(self.nc.semaphore(name))

            class S:
                __slots__ = ("h", "v")
            s = S()
            s.h = h
            s.v = 0
            self.sems[name] = s
        return self.sems[name]

    def din(self, name, shape, dtype=F32):
        ap = self.nc.dram_tensor(name, list(shape), dtype, kind="ExternalInput").ap()
        self.t[name] = ap
        return ap

    def dout(self, name, shape, dtype=F32):
        ap = self.nc.dram_tensor(name, list(shape), dtype, kind="ExternalOutput").ap()
        self.t[name] = ap
        return ap

    def dint(self, name, shape, dtype=F32):
        ap = self.nc.dram_tensor(name, list(shape), dtype).ap()
        self.t[name] = ap
        return ap

    def sbuf(self, shape, dtype=F32, scope=None, name=None, side="left"):
        self.nsb += 1
        nm = name or f"sb{self.nsb}"
        return (scope or self.es).enter_context(
            self.nc.sbuf_tensor(nm, list(shape), dtype, side=side))

    def psum(self, shape, dtype=F32, scope=None, name=None):
        self.nsb += 1
        nm = name or f"ps{self.nsb}"
        return (scope or self.es).enter_context(self.nc.psum_tensor(nm, list(shape), dtype))



def ln_rows(c, em, src, srcbuf, nch, width, colstride, dim, eps, scope, rms=False):
    ng = [(o, min(512, width - o)) for o in range(0, width, 512)]
    rows = c.sbuf((1, 4 * width), scope=scope, side="right")
    inner = ExitStack()
    st_ps = c.psum((1, width), scope=inner) if not rms else None
    sq_ps = c.psum((1, width), scope=inner)
    b_st = Buf("st")
    b_sq = Buf("sq")
    tmp = [c.sbuf((128, width), scope=inner, side="right") for _ in range(2)]
    b_tmp = [Buf("t0"), Buf("t1")]
    onescol = c.sb["onescol_f32"]
    b_ones = c.sb["b_onescol_f32"]
    for ch in range(nch):
        co = ch * colstride
        tb = b_tmp[ch % 2]
        tt = tmp[ch % 2]
        em.op("se", lambda e, tt=tt, co=co, w=width: e.activation(
            tt[:, :], src[:, co:co + w], AF.Square), reads=[srcbuf], writes=[tb])
        for (o, w) in ng:
            if not rms:
                em.op("pe", lambda e, o=o, w=w, co=co, ch=ch: e.matmul(
                    st_ps[0:1, o:o + w], onescol[:, :], src[:, co + o:co + o + w],
                    start=(ch == 0), stop=(ch == nch - 1)),
                    reads=[srcbuf, b_ones], writes=[b_st])
            em.op("pe", lambda e, o=o, w=w, ch=ch, tt=tt: e.matmul(
                sq_ps[0:1, o:o + w], onescol[:, :], tt[:, o:o + w],
                start=(ch == 0), stop=(ch == nch - 1)),
                reads=[tb, b_ones], writes=[b_sq])
    return _ln_rows_finish(c, em, rows, st_ps, b_st, sq_ps, b_sq, inner, width,
                           ng, dim, eps, scope, rms)[:4]


def _ln_rows_finish(c, em, rows, st_ps, b_st, sq_ps, b_sq, inner, width, ng,
                    dim, eps, scope, rms):
    b_rows = Buf("rows")
    if not rms:
        em.op("ve", lambda e: e.tensor_scalar(rows[0:1, 0:width], st_ps[0:1, :],
                                              1.0 / dim, None, ALU.mult),
              reads=[b_st], writes=[b_rows])
        em.op("ve", lambda e: e.scalar_tensor_tensor(
            rows[0:1, width:2 * width], rows[0:1, 0:width], -1.0,
            rows[0:1, 0:width], ALU.mult, ALU.mult), reads=[b_rows], writes=[b_rows])
        em.op("ve", lambda e: e.scalar_tensor_tensor(
            rows[0:1, width:2 * width], sq_ps[0:1, :], 1.0 / dim,
            rows[0:1, width:2 * width], ALU.mult, ALU.add),
            reads=[b_sq, b_rows], writes=[b_rows])
    else:
        em.op("ve", lambda e: e.tensor_scalar(rows[0:1, width:2 * width], sq_ps[0:1, :],
                                              1.0 / dim, None, ALU.mult),
              reads=[b_sq], writes=[b_rows])
    em.op("ve", lambda e: e.tensor_scalar(rows[0:1, width:2 * width],
                                          rows[0:1, width:2 * width],
                                          float(eps), None, ALU.add),
          reads=[b_rows], writes=[b_rows])
    em.op("se", lambda e: e.activation(rows[0:1, 2 * width:3 * width],
                                       rows[0:1, width:2 * width], AF.Sqrt),
          reads=[b_rows], writes=[b_rows])
    em.op("ve", lambda e: e.reciprocal(rows[0:1, 3 * width:4 * width],
                                       rows[0:1, 2 * width:3 * width]),
          reads=[b_rows], writes=[b_rows])
    inner.close()
    inner2 = ExitStack()
    rs_ps = c.psum((128, width), scope=inner2)
    b_rsps = Buf("rsps")
    onesrow = c.sb["onesrow_f32"]
    b_or = c.sb["b_onesrow_f32"]
    for o, w in ng:
        em.op("pe", lambda e, o=o, w=w: e.matmul(
            rs_ps[:, o:o + w], onesrow[:, :],
            rows[0:1, 3 * width + o:3 * width + o + w],
            start=True, stop=True), reads=[b_rows, b_or], writes=[b_rsps])
    rs_sb = c.sbuf((128, width), scope=scope, side="right")
    b_rs = Buf("rs")
    em.op("ve", lambda e: e.tensor_copy(rs_sb[:, :], rs_ps[:, :]),
          reads=[b_rsps], writes=[b_rs])
    mrs_sb = None
    b_mrs = None
    if not rms:
        em.op("ve", lambda e: e.tensor_tensor(rows[0:1, width:2 * width],
                                              rows[0:1, 0:width],
                                              rows[0:1, 3 * width:4 * width], ALU.mult),
              reads=[b_rows], writes=[b_rows])
        for o, w in ng:
            em.op("pe", lambda e, o=o, w=w: e.matmul(
                rs_ps[:, o:o + w], onesrow[:, :], rows[0:1, width + o:width + o + w],
                start=True, stop=True), reads=[b_rows, b_or], writes=[b_rsps])
        mrs_sb = c.sbuf((128, width), scope=scope, side="right")
        b_mrs = Buf("mrs")
        em.op("ve", lambda e: e.tensor_copy(mrs_sb[:, :], rs_ps[:, :]),
              reads=[b_rsps], writes=[b_mrs])
    inner2.close()
    return rs_sb, b_rs, mrs_sb, b_mrs, b_rows


def ln_stream(c, em, src_row_ap, dst, dstbuf, nch, width, dim, eps, scope,
              rms=False, dst_stride=None, extra=None, fold=None):
    if dst_stride is None:
        dst_stride = width
    ng = [(o, min(512, width - o)) for o in range(0, width, 512)]
    stg = [c.sbuf((128, width), F32, scope=scope, side="right") for _ in range(2)]
    b_stg = [Buf("stg0"), Buf("stg1")]
    rows = c.sbuf((1, 4 * width), scope=scope, side="right")
    if fold is not None:
        slab_fns, rnew_fn, r_bufs, slab_bufs = fold
        fstg = [c.sbuf((128, width), F32, scope=scope, side="right")
                for _ in range(len(slab_fns))]
        b_fstg = [Buf("f%d" % i) for i in range(len(slab_fns))]
    inner = ExitStack()
    st_ps = c.psum((1, width), scope=inner) if not rms else None
    sq_ps = c.psum((1, width), scope=inner)
    b_st = Buf("st")
    b_sq = Buf("sq")
    tmp = [c.sbuf((128, width), scope=inner, side="right") for _ in range(2)]
    b_tmp = [Buf(), Buf()]
    onescol = c.sb["onescol_f32"]
    b_ones = c.sb["b_onescol_f32"]
    for ch in range(nch):
        ss, bs = stg[ch % 2], b_stg[ch % 2]
        tt, tb = tmp[ch % 2], b_tmp[ch % 2]
        em.dma("sy" if ch % 2 == 0 else "se",
               "lnA" if ch % 2 == 0 else "lnB", ss[:, :], src_row_ap(ch),
               writes=[bs], reads=[r_bufs[ch]] if fold is not None else ())
        if fold is not None:
            for si, sf in enumerate(slab_fns):
                fs, bfs = fstg[si], b_fstg[si]
                em.dma("sy" if si % 2 == 0 else "se",
                       "fs" + "ABCD"[2 * si + ch % 2], fs[:, :], sf(ch),
                       writes=[bfs])
                em.op("ve", lambda e, ss=ss, fs=fs: e.tensor_tensor(
                    ss[:, :], ss[:, :], fs[:, :], ALU.add),
                    reads=[bfs], writes=[bs])
            em.dma("se", "rnwA" if ch % 2 == 0 else "rnwB", rnew_fn(ch),
                   ss[:, :], reads=[bs], writes=[r_bufs[ch]])
        em.op("se", lambda e, tt=tt, ss=ss: e.activation(
            tt[:, :], ss[:, :], AF.Square), reads=[bs], writes=[tb])
        for (o, w) in ng:
            if not rms:
                em.op("pe", lambda e, o=o, w=w, ch=ch, ss=ss: e.matmul(
                    st_ps[0:1, o:o + w], onescol[:, :], ss[:, o:o + w],
                    start=(ch == 0), stop=(ch == nch - 1)),
                    reads=[bs, b_ones], writes=[b_st])
            em.op("pe", lambda e, o=o, w=w, ch=ch, tt=tt: e.matmul(
                sq_ps[0:1, o:o + w], onescol[:, :], tt[:, o:o + w],
                start=(ch == 0), stop=(ch == nch - 1)),
                reads=[tb, b_ones], writes=[b_sq])
    rs_sb, b_rs, mrs_sb, b_mrs, b_rows = _ln_rows_finish(
        c, em, rows, st_ps, b_st, sq_ps, b_sq, inner, width, ng, dim, eps,
        scope, rms)
    for ch in range(nch):
        ss, bs = stg[ch % 2], b_stg[ch % 2]
        co = ch * dst_stride
        em.dma("sy" if ch % 2 == 0 else "se",
               "lnA" if ch % 2 == 0 else "lnB", ss[:, :], src_row_ap(ch),
               writes=[bs], reads=[r_bufs[ch]] if fold is not None else ())
        if extra is not None:
            extra(ch, ss, bs)
        if rms:
            em.op("ve", lambda e, ss=ss, co=co: e.tensor_tensor(
                dst[:, co:co + width], ss[:, :], rs_sb[:, :], ALU.mult),
                reads=[bs, b_rs], writes=[dstbuf])
        else:
            em.op("ve", lambda e, ss=ss: e.tensor_tensor(
                ss[:, :], ss[:, :], rs_sb[:, :], ALU.mult),
                reads=[bs, b_rs], writes=[bs])
            em.op("ve", lambda e, ss=ss, co=co: e.tensor_tensor(
                dst[:, co:co + width], ss[:, :], mrs_sb[:, :], ALU.subtract),
                reads=[bs, b_mrs], writes=[dstbuf])
    return rs_sb, b_rs, mrs_sb, b_mrs, rows, b_rows


def ln_apply(c, em, src, srcbuf, dst, dstbuf, nch, width, colstride,
             rs_sb, b_rs, mrs_sb, b_mrs, scope):
    tmp = [c.sbuf((128, width), scope=scope, side="right") for _ in range(2)]
    btmp = [Buf(), Buf()]
    for ch in range(nch):
        co = ch * colstride
        if mrs_sb is None:
            em.op("ve", lambda e, co=co: e.tensor_tensor(
                dst[:, co:co + width], src[:, co:co + width], rs_sb[:, :], ALU.mult),
                reads=[srcbuf, b_rs], writes=[dstbuf])
        else:
            tb, tt = btmp[ch % 2], tmp[ch % 2]
            em.op("ve", lambda e, co=co, tt=tt: e.tensor_tensor(
                tt[:, :], src[:, co:co + width], rs_sb[:, :], ALU.mult),
                reads=[srcbuf, b_rs], writes=[tb])
            em.op("ve", lambda e, co=co, tt=tt: e.tensor_tensor(
                dst[:, co:co + width], tt[:, :], mrs_sb[:, :], ALU.subtract),
                reads=[tb, b_mrs], writes=[dstbuf])


def load_cast(c, em, dram_src, dst, dstbuf, col0, ncols, stage, b_stage,
              slot, scale_col=None, b_scale=None, scratch=None):
    em.dma("sy", slot, stage[:, 0:ncols], dram_src, writes=[b_stage])
    if scale_col is None:
        em.op("ve", lambda e: e.tensor_copy(dst[:, col0:col0 + ncols], stage[:, 0:ncols]),
              reads=[b_stage], writes=[dstbuf])
    else:
        em.op("ve", lambda e: e.tensor_scalar(dst[:, col0:col0 + ncols],
                                              stage[:, 0:ncols],
                                              scale_col, None, ALU.mult),
              reads=[b_stage, b_scale], writes=[dstbuf])
    if scratch is not None:
        em.dma("gp", slot + "_sc", scratch, dst[:, col0:col0 + ncols], reads=[dstbuf])



def emit_p1(c, em, qn_dram, y1a_ap, y1b_ap):
    t = c.t
    sb = c.sb
    nc = c.nc
    ident = sb["ident_bf"]
    b_ident = sb["b_ident_bf"]
    onesrow_bf = sb["onesrow_bf"]
    b_orb = sb["b_onesrow_bf"]
    onesrow_f32 = sb["onesrow_f32"]
    b_orf = sb["b_onesrow_f32"]
    onescol_bf = sb["onescol_bf"]
    b_ocb = sb["b_onescol_bf"]
    onescol = sb["onescol_f32"]
    b_ones = sb["b_onescol_f32"]

    p1 = ExitStack()
    with p1:
        ckv_bf = c.sbuf((128, 4 * S0), BF16, scope=p1)
        b_ckv = Buf("ckv")
        qT_bf = c.sbuf((128, H * V), BF16, scope=p1)
        b_qT = Buf("qT")
        mcol_f = c.sbuf((128, 48), F32, scope=p1)
        b_mcf = Buf("mcf")
        mcol_b = c.sbuf((128, 48), BF16, scope=p1)
        b_mcb = Buf("mcb")
        gq_sb = c.sbuf((128, 8), F32, scope=p1)
        b_gq = Buf("gq")
        gkv_sb = c.sbuf((128, 8), F32, scope=p1)
        b_gkv = Buf("gkv")

        if True:
            for g in range(3):
                for kc in range(16):
                    em.dma("sy", "msk", mcol_f[:, g * 16 + kc:g * 16 + kc + 1],
                           t["l0_mcol"][g, kc, :], writes=[b_mcf])
                    em.dma("sy", "msk", mcol_b[:, g * 16 + kc:g * 16 + kc + 1],
                           t["l0_mcol_bf"][g, kc, :], writes=[b_mcb])
            for i in range(2):
                for m in range(4):
                    em.dma("sy", "msk", gq_sb[:, i * 4 + m:i * 4 + m + 1],
                           t["qgb"][i, m * 128:(m + 1) * 128], writes=[b_gq])
                    em.dma("sy", "msk", gkv_sb[:, i * 4 + m:i * 4 + m + 1],
                           t["kvgb"][i, m * 128:(m + 1) * 128], writes=[b_gkv])
            em.seal([b_mcf, b_mcb, b_gq, b_gkv], "msk")
            em.fence()

        sq_scope = ExitStack()
        with sq_scope:
            cq_bf = c.sbuf((128, 4 * V), BF16, scope=sq_scope, side="right")
            b_cq = Buf("cq")
            cqraw = c.sbuf((128, 4 * V), F32, scope=sq_scope, side="right")
            b_cqr = Buf("cqraw")
            s2 = ExitStack()
            with s2:
                xq_bf = c.sbuf((128, DC * V), BF16, scope=s2, side="right")
                b_xq = Buf("xq")
                if True:
                    lnsc = ExitStack()
                    with lnsc:
                        ln_stream(c, em, lambda ch: t["xqT"][ch * 128:(ch + 1) * 128, :],
                                  xq_bf, b_xq, DC, V, D, LN_EPS, lnsc)
                        em.fence()

                s3 = ExitStack()
                with s3:
                    dq_bf = c.sbuf((128, DC * QP), BF16, scope=s3, side="right")
                    b_dq = Buf("dq")
                    wst = c.sbuf((128, QP), F32, scope=s3, side="right")
                    b_wst = Buf("wst")
                    if True:
                        ps_scope = ExitStack()
                        with ps_scope:
                            for dc in range(DC):
                                load_cast(c, em, t["w_dq"][dc * 128:(dc + 1) * 128, :],
                                          dq_bf, b_dq, dc * QP, QP, wst, b_wst, "wA")
                            ps = [c.psum((128, 512), scope=ps_scope) for _ in range(2)]
                            b_ps = [Buf("cqps0"), Buf("cqps1")]
                            k = 0
                            for m in range(4):
                                for (o, w) in NG_V:
                                    p, bp = ps[k % 2], b_ps[k % 2]
                                    k += 1
                                    for dc in range(DC):
                                        em.op("pe", lambda e, p=p, m=m, o=o, w=w, dc=dc: e.matmul(
                                            p[:, 0:w],
                                            dq_bf[:, dc * QP + m * 128:dc * QP + (m + 1) * 128],
                                            xq_bf[:, dc * V + o:dc * V + o + w],
                                            start=(dc == 0), stop=(dc == DC - 1)),
                                            reads=[b_dq, b_xq], writes=[bp])
                                    em.op("ve", lambda e, p=p, m=m, o=o, w=w: e.tensor_copy(
                                        cqraw[:, m * V + o:m * V + o + w], p[:, 0:w]),
                                        reads=[bp], writes=[b_cqr])
                            em.fence()
            if True:
                lnsc = ExitStack()
                with lnsc:
                    rs, b_rs, mrs, b_mrs = ln_rows(c, em, cqraw, b_cqr, 4, V, V,
                                                   QP, LN_EPS, lnsc)
                    tmp = [c.sbuf((128, V), scope=lnsc, side="right") for _ in range(2)]
                    btmp = [Buf(), Buf()]
                    for m in range(4):
                        tb, tt = btmp[m % 2], tmp[m % 2]
                        em.op("ve", lambda e, m=m, tt=tt: e.tensor_tensor(
                            tt[:, :], cqraw[:, m * V:(m + 1) * V], rs[:, :], ALU.mult),
                            reads=[b_cqr, b_rs], writes=[tb])
                        em.op("ve", lambda e, tt=tt: e.tensor_tensor(
                            tt[:, :], tt[:, :], mrs[:, :], ALU.subtract),
                            reads=[tb, b_mrs], writes=[tb])
                        em.op("ve", lambda e, m=m, tt=tt: e.tensor_scalar(
                            cq_bf[:, m * V:(m + 1) * V], tt[:, :],
                            gq_sb[:, m:m + 1], gq_sb[:, 4 + m:4 + m + 1],
                            ALU.mult, op1=ALU.add),
                            reads=[tb, b_gq], writes=[b_cq])
                    em.fence()
            s4 = ExitStack()
            with s4:
                uq_bf = c.sbuf((128, 4 * D), BF16, scope=s4, side="right")
                b_uq = Buf("uq")
                wst = c.sbuf((128, D), F32, scope=s4, side="right")
                b_wst = Buf("wstq")
                sqt = [c.sbuf((128, V), BF16, scope=s4, side="right") for _ in range(2)]
                b_sqt = [Buf(), Buf()]
                qnrow = [c.sbuf((1, V), F32, scope=s4, side="right") for _ in range(2)]
                b_qnr = [Buf(), Buf()]
                if True:
                    ps_scope = ExitStack()
                    with ps_scope:
                        q2_ps = c.psum((1, V), scope=ps_scope)
                        b_q2 = Buf("q2ps")
                        for kc in range(4):
                            load_cast(c, em, t["w_uq"][kc * 128:(kc + 1) * 128, :],
                                      uq_bf, b_uq, kc * D, D, wst, b_wst, "wA")
                        ps = [c.psum((128, 512), scope=ps_scope) for _ in range(2)]
                        b_ps = [Buf(), Buf()]
                        k = 0
                        for h in range(H):
                            for (o, w) in NG_V:
                                p, bp = ps[k % 2], b_ps[k % 2]
                                k += 1
                                for kc in range(4):
                                    em.op("pe", lambda e, p=p, h=h, o=o, w=w, kc=kc: e.matmul(
                                        p[:, 0:w],
                                        uq_bf[:, kc * D + h * 128:kc * D + (h + 1) * 128],
                                        cq_bf[:, kc * V + o:kc * V + o + w],
                                        start=(kc == 0), stop=(kc == 3)),
                                        reads=[b_uq, b_cq], writes=[bp])
                                em.op("ve", lambda e, p=p, h=h, o=o, w=w: e.tensor_copy(
                                    qT_bf[:, h * V + o:h * V + o + w], p[:, 0:w]),
                                    reads=[bp], writes=[b_qT])
                            stq, b_st = sqt[h % 2], b_sqt[h % 2]
                            em.op("se", lambda e, h=h, stq=stq: e.activation(
                                stq[:, :], qT_bf[:, h * V:(h + 1) * V], AF.Square),
                                reads=[b_qT], writes=[b_st])
                            for (o, w) in NG_V:
                                em.op("pe", lambda e, o=o, w=w, stq=stq: e.matmul(
                                    q2_ps[0:1, o:o + w], onescol_bf[:, :], stq[:, o:o + w],
                                    start=True, stop=True),
                                    reads=[b_st, b_ocb], writes=[b_q2])
                            qr, b_qr = qnrow[h % 2], b_qnr[h % 2]
                            em.op("se", lambda e, qr=qr: e.activation(
                                qr[0:1, :], q2_ps[0:1, :], AF.Sqrt),
                                reads=[b_q2], writes=[b_qr])
                            em.dma("sy", "qnstA" if h % 2 == 0 else "qnstB", qn_dram[h, :], qr[0:1, :], reads=[b_qr])
                        em.fence()

        s5 = ExitStack()
        with s5:
            dkv_bf = c.sbuf((128, DC * QP), BF16, scope=s5, side="right")
            b_dkv = Buf("dkv")
            wst = c.sbuf((128, QP), F32, scope=s5, side="right")
            b_wst = Buf("wstkv")
            if True:
                for dc in range(DC):
                    load_cast(c, em, t["w_dkv"][dc * 128:(dc + 1) * 128, :],
                              dkv_bf, b_dkv, dc * QP, QP, wst, b_wst, "wA")
                em.fence()
            def _ckv_tg(tg):
                s6 = ExitStack()
                with s6:
                    xkv_bf = c.sbuf((128, DC * 512), BF16, scope=s6, side="right")
                    b_xkb = Buf("xkvb")
                    ckraw = c.sbuf((128, 4 * 512), F32, scope=s6, side="right")
                    b_ckr = Buf("ckraw")
                    if True:
                        lnsc = ExitStack()
                        with lnsc:
                            _ = ln_stream(c, em,
                                          lambda ch: t["xkvT"][ch * 128:(ch + 1) * 128,
                                                               tg * 512:(tg + 1) * 512],
                                          xkv_bf, b_xkb, DC, 512, D, LN_EPS, lnsc)
                            ps = [c.psum((128, 512), scope=lnsc) for _ in range(2)]
                            b_ps = [Buf(), Buf()]
                            for m in range(4):
                                p, bp = ps[m % 2], b_ps[m % 2]
                                for dc in range(DC):
                                    em.op("pe", lambda e, p=p, m=m, dc=dc: e.matmul(
                                        p[:, :],
                                        dkv_bf[:, dc * QP + m * 128:dc * QP + (m + 1) * 128],
                                        xkv_bf[:, dc * 512:(dc + 1) * 512],
                                        start=(dc == 0), stop=(dc == DC - 1)),
                                        reads=[b_dkv, b_xkb], writes=[bp])
                                em.op("ve", lambda e, p=p, m=m: e.tensor_copy(
                                    ckraw[:, m * 512:(m + 1) * 512], p[:, :]),
                                    reads=[bp], writes=[b_ckr])
                            em.fence()
                    if True:
                        lnsc = ExitStack()
                        with lnsc:
                            rs, b_rs, mrs, b_mrs = ln_rows(c, em, ckraw, b_ckr, 4, 512,
                                                           512, QP, LN_EPS, lnsc)
                            tmp = [c.sbuf((128, 512), scope=lnsc, side="right") for _ in range(2)]
                            btmp = [Buf(), Buf()]
                            for m in range(4):
                                tb, tt = btmp[m % 2], tmp[m % 2]
                                em.op("ve", lambda e, m=m, tt=tt: e.tensor_tensor(
                                    tt[:, :], ckraw[:, m * 512:(m + 1) * 512], rs[:, :],
                                    ALU.mult), reads=[b_ckr, b_rs], writes=[tb])
                                em.op("ve", lambda e, tt=tt: e.tensor_tensor(
                                    tt[:, :], tt[:, :], mrs[:, :], ALU.subtract),
                                    reads=[tb, b_mrs], writes=[tb])
                                em.op("ve", lambda e, m=m, tt=tt, tg=tg: e.tensor_scalar(
                                    ckv_bf[:, m * S0 + tg * 512:m * S0 + (tg + 1) * 512],
                                    tt[:, :], gkv_sb[:, m:m + 1], gkv_sb[:, 4 + m:4 + m + 1],
                                    ALU.mult, op1=ALU.add),
                                    reads=[tb, b_gkv], writes=[b_ckv])
                            em.fence()

            for tg in range(4):
                _ckv_tg(tg)
        s7 = ExitStack()
        with s7:
            bnd_sb = c.sbuf((128, 24 * 512), BF16, scope=s7, side="right")
            b_bnd = Buf("bnd")
            ot_all = c.sbuf((128, 8 * V), BF16, scope=s7, side="right")
            b_otall = Buf("otall")
            if True:
                for g in range(3):
                    for sl in range(8):
                        em.dma("sy", "msk", bnd_sb[:, (g * 8 + sl) * 512:(g * 8 + sl + 1) * 512],
                               t["l0_bnd"][g, sl, :, :], writes=[b_bnd])
                em.seal([b_bnd], "msk")
                em.fence()

            def _hgrp1(hg):
                s8 = ExitStack()
                with s8:
                    kT_bf = c.sbuf((128, 4 * S0), BF16, scope=s8, side="right")
                    b_kT = Buf("kT")
                    v_bf = c.sbuf((128, 16 * 512), BF16, scope=s8, side="right")
                    b_v = Buf("v")
                    wst3 = c.sbuf((128, 512), F32, scope=s8, side="right")
                    b_wst3 = Buf("wst3")
                    pt_bf = c.sbuf((128, 16 * 512), BF16, scope=s8, side="right")
                    b_pt = Buf("pt")
                    k2tmp = c.sbuf((128, S0), BF16, scope=s8, side="right")
                    b_k2t = Buf("k2t")
                    krow = c.sbuf((1, 64), F32, scope=s8, side="right")
                    b_krow = Buf("krow")
                    qrow = c.sbuf((1, V), F32, scope=s8, side="right")
                    b_qrow = Buf("qrow")
                    negm = [c.sbuf((1, V), BF16, scope=s8, side="right") for _ in range(2)]
                    b_negm2 = [Buf("negm0"), Buf("negm1")]
                    rb_sb = c.sbuf((128, 512), F32, scope=s8, side="right")
                    b_rb = Buf("rb")
                    rrow = c.sbuf((1, 512), F32, scope=s8, side="right")
                    b_rrow = Buf("rrow")

                    ukv_bf = c.sbuf((128, 4 * 1024), BF16, scope=s8, side="right")
                    b_ukv = Buf("ukv")
                    if True:
                        kvsc = ExitStack()
                        with kvsc:
                            for kc in range(4):
                                load_cast(c, em,
                                          t["w_ukv"][kc * 128:(kc + 1) * 128,
                                                     hg * 512:(hg + 1) * 512],
                                          ukv_bf, b_ukv, kc * 1024, 512,
                                          wst3, b_wst3, "wA")
                                load_cast(c, em,
                                          t["w_ukv"][kc * 128:(kc + 1) * 128,
                                                     2048 + hg * 512:2048 + (hg + 1) * 512],
                                          ukv_bf, b_ukv, kc * 1024 + 512, 512,
                                          wst3, b_wst3, "wA")

                            pk = [c.psum((128, 512), scope=kvsc) for _ in range(2)]
                            b_pk = [Buf(), Buf()]
                            k2_ps = c.psum((1, S0), scope=kvsc)
                            b_k2 = Buf("k2ps")
                            ki = 0
                            for hh in range(4):
                                h = hg * 4 + hh
                                for ng in range(4):
                                    p, bp = pk[ki % 2], b_pk[ki % 2]
                                    ki += 1
                                    for m in range(4):
                                        em.op("pe", lambda e, p=p, hh=hh, ng=ng, m=m: e.matmul(
                                            p[:, :],
                                            ukv_bf[:, m * 1024 + hh * 128:m * 1024 + (hh + 1) * 128],
                                            ckv_bf[:, m * S0 + ng * 512:m * S0 + (ng + 1) * 512],
                                            start=(m == 0), stop=(m == 3)),
                                            reads=[b_ukv, b_ckv], writes=[bp])
                                    em.op("ve", lambda e, p=p, hh=hh, ng=ng: e.tensor_copy(
                                        kT_bf[:, hh * S0 + ng * 512:hh * S0 + (ng + 1) * 512],
                                        p[:, :]), reads=[bp], writes=[b_kT])
                                em.op("se", lambda e, hh=hh: e.activation(
                                    k2tmp[:, :], kT_bf[:, hh * S0:(hh + 1) * S0], AF.Square),
                                    reads=[b_kT], writes=[b_k2t])
                                for o in range(0, S0, 512):
                                    em.op("pe", lambda e, o=o: e.matmul(
                                        k2_ps[0:1, o:o + 512], onescol_bf[:, :],
                                        k2tmp[:, o:o + 512],
                                        start=True, stop=True),
                                        reads=[b_k2t, b_ocb], writes=[b_k2])
                                em.op("ve", lambda e, hh=hh: e.reduce_max(
                                    krow[0:1, hh * 16:hh * 16 + 1],
                                    k2_ps[0:1, :], axis=AX.X), reads=[b_k2], writes=[b_krow])
                                em.op("se", lambda e, hh=hh: e.activation(
                                    krow[0:1, hh * 16 + 1:hh * 16 + 2],
                                    krow[0:1, hh * 16:hh * 16 + 1], AF.Sqrt),
                                    reads=[b_krow], writes=[b_krow])
                                em.op("ve", lambda e, hh=hh: e.tensor_scalar(
                                    krow[0:1, hh * 16 + 2:hh * 16 + 3],
                                    krow[0:1, hh * 16 + 1:hh * 16 + 2],
                                    -1.0, None, ALU.mult), reads=[b_krow], writes=[b_krow])
                                em.dma("sy", "qnld", qrow[0:1, :], qn_dram[h, :],
                                       writes=[b_qrow])
                                em.op("ve", lambda e, hh=hh: e.tensor_scalar(
                                    negm[hh % 2][0:1, :], qrow[0:1, :],
                                    krow[0:1, hh * 16 + 2:hh * 16 + 3],
                                    None, ALU.mult),
                                    reads=[b_qrow, b_krow], writes=[b_negm2[hh % 2]])
                            for kc in range(16):
                                p, bp = pk[ki % 2], b_pk[ki % 2]
                                ki += 1
                                for m in range(4):
                                    em.op("pe", lambda e, p=p, kc=kc, m=m: e.matmul(
                                        p[:, :],
                                        ckv_bf[:, m * S0 + kc * 128:m * S0 + (kc + 1) * 128],
                                        ukv_bf[:, m * 1024 + 512:m * 1024 + 1024],
                                        start=(m == 0), stop=(m == 3)),
                                        reads=[b_ukv, b_ckv], writes=[bp])
                                em.op("ve", lambda e, p=p, kc=kc: e.tensor_copy(
                                    v_bf[:, kc * 512:(kc + 1) * 512], p[:, :]),
                                    reads=[bp], writes=[b_v])
                            em.fence()

                    if True:
                        atsc = ExitStack()
                        with atsc:
                            stp = [c.psum((128, 3 * 512), scope=atsc) for _ in range(2)]
                            b_stp = [Buf("stA"), Buf("stB")]
                            ot_ps = c.psum((128, 512), scope=atsc)
                            b_otp = Buf("otps")
                            row_ps = c.psum((128, 512), scope=atsc)
                            b_row = Buf("rowps")
                            for hh in range(4):
                                h = hg * 4 + hh
                                for g in range(3):
                                    nk = 4 * g + 8
                                    q0 = h * V + g * 512
                                    nbat = (nk + 2) // 3
                                    for bi in range(nbat):
                                        kcs = list(range(bi * 3, min(bi * 3 + 3, nk)))
                                        p, bp = stp[bi % 2], b_stp[bi % 2]
                                        for ii, kc in enumerate(kcs):
                                            sl = None
                                            if g > 0 and 4 * g - 4 <= kc <= 4 * g - 1:
                                                sl = kc - (4 * g - 4)
                                            elif 4 * g + 4 <= kc <= 4 * g + 7:
                                                sl = 4 + (kc - (4 * g + 4))
                                            em.op("pe", lambda e, p=p, ii=ii, kc=kc, hh=hh, q0=q0: e.matmul(
                                                p[:, ii * 512:(ii + 1) * 512],
                                                kT_bf[:, hh * S0 + kc * 128:hh * S0 + (kc + 1) * 128],
                                                qT_bf[:, q0:q0 + 512],
                                                start=True, stop=False),
                                                reads=[b_kT, b_qT], writes=[bp])
                                            em.op("pe", lambda e, p=p, ii=ii, hh=hh, g=g,
                                                  last=(sl is None): e.matmul(
                                                p[:, ii * 512:(ii + 1) * 512], onesrow_bf[:, :],
                                                negm[hh % 2][0:1, g * 512:(g + 1) * 512],
                                                start=False, stop=last),
                                                reads=[b_negm2[hh % 2], b_orb], writes=[bp])
                                            if sl is not None:
                                                em.op("pe", lambda e, p=p, ii=ii, g=g, sl=sl: e.matmul(
                                                    p[:, ii * 512:(ii + 1) * 512], ident[:, :],
                                                    bnd_sb[:, (g * 8 + sl) * 512:(g * 8 + sl + 1) * 512],
                                                    start=False, stop=True),
                                                    reads=[b_bnd, b_ident], writes=[bp])
                                        em.op("se", lambda e, p=p, kcs=kcs: e.activation(
                                            pt_bf[:, kcs[0] * 512:(kcs[0] + len(kcs)) * 512],
                                            p[:, 0:len(kcs) * 512], AF.Exp, scale=INVSD),
                                            reads=[bp], writes=[b_pt])
                                        for kc in kcs:
                                            em.op("ve", lambda e, kc=kc, g=g: e.tensor_scalar(
                                                pt_bf[:, kc * 512:(kc + 1) * 512],
                                                pt_bf[:, kc * 512:(kc + 1) * 512],
                                                mcol_f[:, g * 16 + kc:g * 16 + kc + 1],
                                                None, ALU.mult),
                                                reads=[b_mcf], writes=[b_pt])
                                    for kc in range(nk):
                                        em.op("pe", lambda e, kc=kc, hh=hh, nk=nk: e.matmul(
                                            ot_ps[:, :],
                                            v_bf[:, kc * 512 + hh * 128:kc * 512 + (hh + 1) * 128],
                                            pt_bf[:, kc * 512:(kc + 1) * 512],
                                            start=(kc == 0), stop=(kc == nk - 1)),
                                            reads=[b_v, b_pt], writes=[b_otp])
                                        em.op("pe", lambda e, kc=kc, g=g, nk=nk: e.matmul(
                                            row_ps[0:1, :],
                                            mcol_b[:, g * 16 + kc:g * 16 + kc + 1],
                                            pt_bf[:, kc * 512:(kc + 1) * 512],
                                            start=(kc == 0), stop=(kc == nk - 1)),
                                            reads=[b_mcb, b_pt], writes=[b_row])
                                    em.op("ve", lambda e: e.reciprocal(rrow[0:1, :], row_ps[0:1, :]),
                                          reads=[b_row], writes=[b_rrow])
                                    em.op("pe", lambda e: e.matmul(
                                        row_ps[:, :], onesrow_f32[:, :], rrow[0:1, :],
                                        start=True, stop=True),
                                        reads=[b_rrow, b_orf], writes=[b_row])
                                    em.op("ve", lambda e: e.tensor_copy(rb_sb[:, :], row_ps[:, :]),
                                          reads=[b_row], writes=[b_rb])
                                    em.op("ve", lambda e, hh=hh, g=g, hg=hg: e.tensor_tensor(
                                        ot_all[:, ((hg % 2) * 4 + hh) * V + g * 512:
                                               ((hg % 2) * 4 + hh) * V + (g + 1) * 512],
                                        ot_ps[:, :], rb_sb[:, :], ALU.mult),
                                        reads=[b_otp, b_rb], writes=[b_otall])
                            em.fence()

                    em.fence()


            def _wo_half1(half, y_ap):
                wsc = ExitStack()
                with wsc:
                    wst = [c.sbuf((128, 1024), F32, scope=wsc, side="right")
                           for _ in range(2)]
                    b_wst = [Buf(), Buf()]
                    wbf = [c.sbuf((128, 1024), BF16, scope=wsc, side="right")
                           for _ in range(2)]
                    b_wbf = [Buf(), Buf()]
                    ystg = [c.sbuf((128, 512), F32, scope=wsc, side="right")
                            for _ in range(4)]
                    b_ystg = [Buf(), Buf(), Buf(), Buf()]
                    yp = [c.psum((128, 512), scope=wsc) for _ in range(2)]
                    b_yp = [Buf(), Buf()]
                    k = 0
                    for j in range(16):
                        ws, bws = wst[j % 2], b_wst[j % 2]
                        wb, bwb = wbf[j % 2], b_wbf[j % 2]
                        jeng = "sy" if j % 2 == 0 else "se"
                        em.dma(jeng, "woA" if j % 2 == 0 else "woB", ws[:, :],
                               t["wo_t"][j, :, half * 1024:(half + 1) * 1024],
                               writes=[bws])
                        em.op("ve", lambda e, ws=ws, wb=wb: e.tensor_copy(
                            wb[:, :], ws[:, :]), reads=[bws], writes=[bwb])
                        for (o, w) in NG_V:
                            p, bp = yp[k % 2], b_yp[k % 2]
                            ys, b_ys = ystg[k % 4], b_ystg[k % 4]
                            k += 1
                            for ic in range(8):
                                em.op("pe", lambda e, p=p, o=o, w=w, ic=ic, wb=wb: e.matmul(
                                    p[:, 0:w],
                                    wb[:, ic * 128:(ic + 1) * 128],
                                    ot_all[:, ic * V + o:ic * V + o + w],
                                    start=(ic == 0), stop=(ic == 7)),
                                    reads=[bwb, b_otall], writes=[bp])
                            em.op("ve", lambda e, p=p, ys=ys, w=w: e.tensor_copy(
                                ys[:, 0:w], p[:, 0:w]), reads=[bp], writes=[b_ys])
                            em.dma(jeng, "yst" + "ABCD"[(k - 1) % 4],
                                   y_ap[j * 128:(j + 1) * 128, o:o + w],
                                   ys[:, 0:w], reads=[b_ys])
                    em.fence()

            for half in range(2):
                _hgrp1(2 * half)
                _hgrp1(2 * half + 1)
                _wo_half1(half, y1a_ap if half == 0 else y1b_ap)

CONSTS = [("ident_bf", (128, 128), BF16),
          ("ident_f32", (128, 128), F32),
          ("onesrow_bf", (1, 128), BF16),
          ("onesrow_f32", (1, 128), F32),
          ("onescol_f32", (128, 1), F32),
          ("onescol_bf", (128, 1), BF16)]


def declare_consts(c):
    for nm, shape, dt in CONSTS:
        c.din(nm, shape, dt)


def emit_const_loads(c, em):
    sb = c.sb
    with c.nc.Block() as blk:
        for nm, shape, dt in CONSTS:
            sb[nm] = c.sbuf(shape, dt, name="c_" + nm)
            sb["b_" + nm] = Buf(nm)
            em.dma("sy", "cst", sb[nm][:, :], c.t[nm][:, :], writes=[sb["b_" + nm]])
        em.seal([sb["b_" + nm] for nm, _, _ in CONSTS], "cst")
        em.flush(blk)


def declare_inputs_p1(c):
    c.din("xqT", (D, V))
    c.din("xkvT", (D, S0))
    c.din("w_dq", (D, QP))
    c.din("w_uq", (QP, D))
    c.din("w_dkv", (D, QP))
    c.din("w_ukv", (QP, 2 * D))
    c.din("wo_t", (16, 128, D))
    c.din("qgb", (2, QP))
    c.din("kvgb", (2, QP))
    c.din("l0_bnd", (3, 8, 128, 512), BF16)
    c.din("l0_mcol", (3, 16, 128))
    c.din("l0_mcol_bf", (3, 16, 128), BF16)


P1_INPUTS = ["xqT", "xkvT", "w_dq", "w_uq", "w_dkv", "w_ukv", "wo_t", "qgb",
             "kvgb", "l0_bnd", "l0_mcol", "l0_mcol_bf"]


def finish(c, em):
    with c.nc.Block() as blk:
        def _(eng):
            for s in em.dsem.values():
                eng.wait_ge(s.h, s.v)
        blk.sync(_)


def build_p1():
    c = Ctx()
    declare_consts(c)
    declare_inputs_p1(c)
    r_in = c.din("r_in", (D, V))
    r_out = c.dout("r_out", (D, V))
    qn_dram = c.dint("qn_scratch", (H, V))
    with c.es:
        em = Em(c)
        emit_const_loads(c, em)
        with c.nc.Block() as blk:
            for dc in range(DC):
                em.dma("sy", "rcp", r_out[dc * 128:(dc + 1) * 128, :],
                       r_in[dc * 128:(dc + 1) * 128, :])
            em.flush(blk)
        with c.nc.Block() as blk:
            emit_p1(c, em, r_out, qn_dram)
            em.flush(blk)
        finish(c, em)
        c.nc.finalize()
    return c



NG_768 = [(0, 512), (512, 256)]


def emit_p2(c, em, r_ap, r_bufs, fold_fns, y2a_ap, y2b_ap):
    t = c.t
    sb = c.sb
    nc = c.nc
    onesrow_f32 = sb["onesrow_f32"]
    b_orf = sb["b_onesrow_f32"]
    ident_f = sb["ident_f32"]
    b_idf = sb["b_ident_f32"]

    p2 = ExitStack()
    with p2:
        x2_bf = c.sbuf((128, DC * V), BF16, scope=p2, side="right")
        b_x2 = Buf("x2")
        cb_bf = c.sbuf((128, 2 * V), BF16, scope=p2, side="right")
        b_cb = Buf("cb")

        rsc = ExitStack()
        gw_f = c.sbuf((128, DC * 2), F32, scope=rsc, side="right")
        b_gwf = Buf("gwf")
        b_raw = Buf("rawps")
        sgcol = c.sbuf((2, 1), F32, scope=rsc, side="right")
        b_sgc = Buf("sgc")
        onescol = sb["onescol_f32"]
        b_ones = sb["b_onescol_f32"]
        if True:
            for dc in range(DC):
                em.dma("sy", "gwl", gw_f[:, dc * 2:(dc + 1) * 2],
                       t["gate_wT"][dc * 128:(dc + 1) * 128, :], writes=[b_gwf])
            em.seal([b_gwf], "gwl")
            sgsc = ExitStack()
            sg_ps = c.psum((1, 2), scope=sgsc)
            b_sgp = Buf("sgp")
            for dc in range(DC):
                em.op("pe", lambda e, dc=dc: e.matmul(
                    sg_ps[0:1, :], onescol[:, :], gw_f[:, dc * 2:(dc + 1) * 2],
                    start=(dc == 0), stop=(dc == DC - 1)),
                    reads=[b_gwf, b_ones], writes=[b_sgp])
            sgrow = c.sbuf((1, 2), F32, scope=rsc, side="right")
            b_sgrow = Buf("sgrow")
            em.op("ve", lambda e: e.tensor_copy(sgrow[0:1, :], sg_ps[0:1, :]),
                  reads=[b_sgp], writes=[b_sgrow])
            sgc_ps = c.psum((2, 1), scope=sgsc)
            b_sgcp = Buf("sgcp")
            em.op("pe", lambda e: e.matmul(
                sgc_ps[:, :], sgrow[0:1, :], onesrow_f32[0:1, 0:1],
                start=True, stop=True), reads=[b_sgrow, b_orf], writes=[b_sgcp])
            em.op("ve", lambda e: e.tensor_copy(sgcol[:, :], sgc_ps[:, :]),
                  reads=[b_sgcp], writes=[b_sgc])
            em.fence()
            sgsc.close()

        rawT_holder = []

        def p2_extra(ch, ss, bs):
            rawT_ps = rawT_holder[0]
            for (o, w) in NG_V:
                em.op("pe", lambda e, ch=ch, o=o, w=w, ss=ss: e.matmul(
                    rawT_ps[:, o:o + w], gw_f[:, ch * 2:(ch + 1) * 2],
                    ss[:, o:o + w],
                    start=(ch == 0), stop=(ch == DC - 1)),
                    reads=[bs, b_gwf], writes=[b_raw])

        lsb = c.sbuf((128, 2 * VT), F32, scope=rsc, side="right")
        b_lsb2 = Buf("lsb2")
        if True:
            lnsc0 = ExitStack()
            with lnsc0:
                rs2, b_rs2, mrs2, b_mrs2, _, _ = ln_stream(
                    c, em, lambda ch: r_ap[ch * 128:(ch + 1) * 128, :],
                    x2_bf, b_x2, DC, V, D, LN_EPS, rsc,
                    fold=(fold_fns, lambda ch: r_ap[ch * 128:(ch + 1) * 128, :],
                          r_bufs, None))
                em.fence()
        if True:
            lnsc = ExitStack()
            with lnsc:
                rstg = [c.sbuf((128, V), F32, scope=lnsc, side="right") for _ in range(2)]
                b_rstg = [Buf(), Buf()]
                rawT_ps = c.psum((2, V), scope=lnsc)
                rawT_holder.append(rawT_ps)
                for ch in range(DC):
                    ss, bs = rstg[ch % 2], b_rstg[ch % 2]
                    em.dma("sy" if ch % 2 == 0 else "se",
                           "lnA" if ch % 2 == 0 else "lnB", ss[:, :],
                           r_ap[ch * 128:(ch + 1) * 128, :], writes=[bs],
                           reads=[r_bufs[ch]])
                    p2_extra(ch, ss, bs)
                lT = c.sbuf((2, V), F32, scope=lnsc, side="right")
                b_lT = Buf("lT")
                em.op("ve", lambda e: e.tensor_tensor(
                    lT[:, :], rawT_ps[:, :], rs2[0:2, :], ALU.mult),
                    reads=[b_raw, b_rs2], writes=[b_lT])
                tmpT = c.sbuf((2, V), F32, scope=lnsc, side="right")
                b_tmpT = Buf("tmpT")
                em.op("ve", lambda e: e.tensor_scalar(
                    tmpT[:, :], mrs2[0:2, :], sgcol[:, :], None, ALU.mult),
                    reads=[b_mrs2, b_sgc], writes=[b_tmpT])
                em.op("ve", lambda e: e.tensor_tensor(
                    lT[:, :], lT[:, :], tmpT[:, :], ALU.subtract),
                    reads=[b_tmpT], writes=[b_lT])
                tk_ps = c.psum((128, 2 * VT), scope=lnsc)
                b_tk = Buf("tkps")
                for tc in range(VT):
                    em.op("pe", lambda e, tc=tc: e.matmul(
                        tk_ps[:, tc * 2:(tc + 1) * 2],
                        lT[:, tc * 128:(tc + 1) * 128],
                        ident_f[0:2, 0:2], start=True, stop=True),
                        reads=[b_lT, b_idf], writes=[b_tk])
                em.op("ve", lambda e: e.tensor_copy(lsb[:, :], tk_ps[:, :]),
                      reads=[b_tk], writes=[b_lsb2])
                em.fence()

        if True:
            with rsc:
                rws = c.sbuf((128, 8 * VT), F32, scope=rsc, side="right")
                b_rws = Buf("rws")
                csb = c.sbuf((128, 2 * VT), F32, scope=rsc, side="right")
                b_csb = Buf("csb")
                for tc in range(VT):
                    em.op("ve", lambda e, tc=tc: e.tensor_tensor(
                        rws[:, tc * 8:tc * 8 + 1], lsb[:, tc * 2:tc * 2 + 1],
                        lsb[:, tc * 2 + 1:tc * 2 + 2], ALU.subtract),
                        reads=[b_lsb2], writes=[b_rws])
                for tc in range(VT):
                    em.op("se", lambda e, tc=tc: e.activation(
                        rws[:, tc * 8 + 1:tc * 8 + 2], rws[:, tc * 8:tc * 8 + 1],
                        AF.Sigmoid), reads=[b_rws], writes=[b_rws])
                    em.op("ve", lambda e, tc=tc: e.tensor_scalar(
                        rws[:, tc * 8 + 2:tc * 8 + 3], rws[:, tc * 8:tc * 8 + 1],
                        0.0, None, ALU.is_ge), reads=[b_rws], writes=[b_rws])
                    em.op("ve", lambda e, tc=tc: e.tensor_tensor(
                        csb[:, tc * 2:tc * 2 + 1], rws[:, tc * 8 + 1:tc * 8 + 2],
                        rws[:, tc * 8 + 2:tc * 8 + 3], ALU.mult),
                        reads=[b_rws], writes=[b_csb])
                    em.op("ve", lambda e, tc=tc: e.tensor_scalar(
                        rws[:, tc * 8 + 3:tc * 8 + 4], rws[:, tc * 8 + 1:tc * 8 + 2],
                        -1.0, 1.0, ALU.mult, op1=ALU.add),
                        reads=[b_rws], writes=[b_rws])
                    em.op("ve", lambda e, tc=tc: e.tensor_scalar(
                        rws[:, tc * 8 + 4:tc * 8 + 5], rws[:, tc * 8 + 2:tc * 8 + 3],
                        -1.0, 1.0, ALU.mult, op1=ALU.add),
                        reads=[b_rws], writes=[b_rws])
                    em.op("ve", lambda e, tc=tc: e.tensor_tensor(
                        csb[:, tc * 2 + 1:tc * 2 + 2], rws[:, tc * 8 + 3:tc * 8 + 4],
                        rws[:, tc * 8 + 4:tc * 8 + 5], ALU.mult),
                        reads=[b_rws], writes=[b_csb])
                crow_ps = c.psum((1, V), scope=rsc)
                b_crp = Buf("crp")
                crow = c.sbuf((1, V), F32, scope=rsc, side="right")
                b_crow = Buf("crow")
                cb_ps = c.psum((128, 512), scope=rsc)
                b_cbp = Buf("cbp")
                for e_ in range(2):
                    for tc in range(VT):
                        em.op("pe", lambda e, e_=e_, tc=tc: e.transpose(
                            crow_ps[0:1, tc * 128:(tc + 1) * 128],
                            csb[:, tc * 2 + e_:tc * 2 + e_ + 1], ident_f[:, :]),
                            reads=[b_csb, b_idf], writes=[b_crp])
                    em.op("ve", lambda e: e.tensor_copy(crow[0:1, :], crow_ps[0:1, :]),
                          reads=[b_crp], writes=[b_crow])
                    for (o, w) in NG_V:
                        em.op("pe", lambda e, o=o, w=w: e.matmul(
                            cb_ps[:, 0:w], onesrow_f32[:, :], crow[0:1, o:o + w],
                            start=True, stop=True), reads=[b_crow, b_orf], writes=[b_cbp])
                        em.op("ve", lambda e, e_=e_, o=o, w=w: e.tensor_copy(
                            cb_bf[:, e_ * V + o:e_ * V + o + w], cb_ps[:, 0:w]),
                            reads=[b_cbp], writes=[b_cb])
                em.fence()

        h_bf = c.sbuf((128, 32 * V), BF16, scope=p2, side="right")
        b_h = Buf("h")
        def _p2_hh(hh):
            def _p2_he(e_):
                if True:
                    gsc = ExitStack()
                    with gsc:
                        wst = c.sbuf((128, 2048), F32, scope=gsc, side="right")
                        b_wst = Buf("wst")
                        wstu = c.sbuf((128, 2048), F32, scope=gsc, side="right")
                        b_wstu = Buf("wstu")
                        wgbf = [c.sbuf((128, 2048), BF16, scope=gsc, side="right")
                                for _ in range(2)]
                        b_wgbf = [Buf(), Buf()]
                        wubf = [c.sbuf((128, 2048), BF16, scope=gsc, side="right")
                                for _ in range(2)]
                        b_wubf = [Buf(), Buf()]
                        sg = c.sbuf((128, V), BF16, scope=gsc, side="right")
                        b_sg = Buf("sg")
                        gps = c.psum((128, V), scope=gsc)
                        b_gps = Buf("gps")
                        ups = c.psum((128, V), scope=gsc)
                        b_ups = Buf("ups")
                        for fc in range(32):
                            gfc = hh * 32 + fc
                            wg_b, bwg = wgbf[fc % 2], b_wgbf[fc % 2]
                            wu_b, bwu = wubf[fc % 2], b_wubf[fc % 2]
                            em.dma("sy", "wgA" if fc % 2 == 0 else "wgB",
                                   wst[:, :], t["wg_t"][e_, gfc], writes=[b_wst])
                            em.op("ve", lambda e, wg_b=wg_b: e.tensor_copy(
                                wg_b[:, :], wst[:, :]), reads=[b_wst], writes=[bwg])
                            em.dma("se", "wuA" if fc % 2 == 0 else "wuB",
                                   wstu[:, :], t["wu_t"][e_, gfc], writes=[b_wstu])
                            em.op("ve", lambda e, wu_b=wu_b: e.tensor_copy(
                                wu_b[:, :], wstu[:, :]), reads=[b_wstu], writes=[bwu])
                            for dc in range(DC):
                                for (o, w) in NG_V:
                                    em.op("pe", lambda e, dc=dc, o=o, w=w, wg_b=wg_b: e.matmul(
                                        gps[:, o:o + w],
                                        wg_b[:, dc * 128:(dc + 1) * 128],
                                        x2_bf[:, dc * V + o:dc * V + o + w],
                                        start=(dc == 0), stop=(dc == DC - 1)),
                                        reads=[bwg, b_x2], writes=[b_gps])
                            for dc in range(DC):
                                for (o, w) in NG_V:
                                    em.op("pe", lambda e, dc=dc, o=o, w=w, wu_b=wu_b: e.matmul(
                                        ups[:, o:o + w],
                                        wu_b[:, dc * 128:(dc + 1) * 128],
                                        x2_bf[:, dc * V + o:dc * V + o + w],
                                        start=(dc == 0), stop=(dc == DC - 1)),
                                        reads=[bwu, b_x2], writes=[b_ups])
                            em.op("se", lambda e: e.activation(
                                sg[:, :], gps[:, :], AF.Silu), reads=[b_gps], writes=[b_sg])
                            em.op("ve", lambda e, fc=fc: e.tensor_tensor(
                                h_bf[:, fc * V:(fc + 1) * V], sg[:, :], ups[:, :],
                                ALU.mult), reads=[b_sg, b_ups], writes=[b_h])
                            em.op("ve", lambda e, fc=fc, e_=e_: e.tensor_tensor(
                                h_bf[:, fc * V:(fc + 1) * V], h_bf[:, fc * V:(fc + 1) * V],
                                cb_bf[:, e_ * V:(e_ + 1) * V], ALU.mult),
                                reads=[b_cb], writes=[b_h])
                        em.fence()
                if True:
                    ysc = ExitStack()
                    with ysc:
                        wdst2 = [c.sbuf((128, 2048), F32, scope=ysc, side="right")
                                 for _ in range(2)]
                        b_wdst2 = [Buf("wdstA"), Buf("wdstB")]
                        wdbf2 = [c.sbuf((128, 4096), BF16, scope=ysc, side="right")
                                 for _ in range(2)]
                        b_wdbf2 = [Buf("wd0"), Buf("wd1")]
                        ys = c.sbuf((128, V), F32, scope=ysc, side="right")
                        b_ys = Buf("ys")
                        yold = c.sbuf((128, V), F32, scope=ysc, side="right")
                        b_yold = Buf("yold")
                        yps = [c.psum((128, V), scope=ysc) for _ in range(2)]
                        b_yps = [Buf(), Buf()]
                        y_ap = y2a_ap if e_ == 0 else y2b_ap
                        for dc in range(DC):
                            wdbf, b_wdbf = wdbf2[dc % 2], b_wdbf2[dc % 2]
                            eng = "sy" if dc % 2 == 0 else "se"
                            for hf in range(2):
                                wdst, b_wdst = wdst2[hf], b_wdst2[hf]
                                em.dma(eng,
                                       "wdl" + "ABCD"[2 * hf + dc % 2], wdst[:, :],
                                       t["wd_t"][e_, dc,
                                                 :, hh * 4096 + hf * 2048:
                                                 hh * 4096 + (hf + 1) * 2048],
                                       writes=[b_wdst])
                                em.op("ve", lambda e, wdbf=wdbf, wdst=wdst,
                                      hf=hf, b=None: e.tensor_copy(
                                    wdbf[:, hf * 2048:(hf + 1) * 2048], wdst[:, :]),
                                    reads=[b_wdst], writes=[b_wdbf])
                            if hh == 1:
                                em.dma(eng, "yolA" if dc % 2 == 0 else "yolB",
                                       yold[:, :], y_ap[dc * 128:(dc + 1) * 128, :],
                                       writes=[b_yold])
                            yp, byp = yps[dc % 2], b_yps[dc % 2]
                            for fc in range(32):
                                for (o, w) in NG_V:
                                    em.op("pe", lambda e, yp=yp, fc=fc, o=o, w=w,
                                          wdbf=wdbf, b=None: e.matmul(
                                        yp[:, o:o + w],
                                        wdbf[:, fc * 128:(fc + 1) * 128],
                                        h_bf[:, fc * V + o:fc * V + o + w],
                                        start=(fc == 0), stop=(fc == 31)),
                                        reads=[b_wdbf, b_h], writes=[byp])
                            if hh == 0:
                                em.op("ve", lambda e, yp=yp: e.tensor_copy(
                                    ys[:, :], yp[:, :]), reads=[byp], writes=[b_ys])
                            else:
                                em.op("ve", lambda e, yp=yp: e.tensor_tensor(
                                    ys[:, :], yp[:, :], yold[:, :], ALU.add),
                                    reads=[byp, b_yold], writes=[b_ys])
                            em.dma(eng, "ysA" if dc % 2 == 0 else "ysB",
                                   y_ap[dc * 128:(dc + 1) * 128, :],
                                   ys[:, :], reads=[b_ys])
                        em.fence()

            for e_ in range(2):
                _p2_he(e_)
        for hh in range(2):
            _p2_hh(hh)

DEBUG_P3 = False


def emit_p3(c, em, r_ap, r_bufs, fold_fns, y3_ap, qn_dram):
    t = c.t
    sb = c.sb
    nc = c.nc
    ident = sb["ident_bf"]
    b_ident = sb["b_ident_bf"]
    onesrow_bf = sb["onesrow_bf"]
    b_orb = sb["b_onesrow_bf"]
    onesrow_f32 = sb["onesrow_f32"]
    b_orf = sb["b_onesrow_f32"]
    onescol_bf = sb["onescol_bf"]
    b_ocb = sb["b_onescol_bf"]

    p3 = ExitStack()
    with p3:
        xn_bf = c.sbuf((128, DC * V), BF16, scope=p3, side="right")
        b_xn = Buf("xn")
        lnw_sb = c.sbuf((128, DC), F32, scope=p3, side="right")
        b_lnw = Buf("lnw")
        cos_sb = c.sbuf((64, V), F32, scope=p3, side="right")
        b_cos = Buf("cos")
        sin_sb = c.sbuf((64, V), F32, scope=p3, side="right")
        b_sin = Buf("sin")
        l1m_sb = c.sbuf((128, 8 * 512), BF16, scope=p3, side="right")
        b_l1m = Buf("l1m")
        mcol_f = c.sbuf((128, 16), F32, scope=p3, side="right")
        b_mcf = Buf("mcf3")
        mcol_b = c.sbuf((128, 16), BF16, scope=p3, side="right")
        b_mcb = Buf("mcb3")
        kT_bf = c.sbuf((128, 4 * V), BF16, scope=p3, side="right")
        b_kT = Buf("kT3")
        ot3_all = c.sbuf((128, H * 1024), BF16, scope=p3, side="right")
        b_otall3 = Buf("otall3")
        v_bf = c.sbuf((128, VT * 512), BF16, scope=p3, side="right")
        b_v = Buf("v3")
        krow = c.sbuf((1, 64), F32, scope=p3, side="right")
        b_krow = Buf("krow3")

        if True:
            for dc in range(DC):
                em.dma("sy", "msk", lnw_sb[:, dc:dc + 1],
                       t["lnw"][dc * 128:(dc + 1) * 128], writes=[b_lnw])
            em.dma("sy", "msk", cos_sb[:, :], t["ropec"][:, :], writes=[b_cos])
            em.dma("sy", "msk", sin_sb[:, :], t["ropes"][:, :], writes=[b_sin])
            for j in range(8):
                em.dma("sy", "msk", l1m_sb[:, j * 512:(j + 1) * 512],
                       t["l1_mask"][j, :, :], writes=[b_l1m])
            for qg in range(2):
                for kc in range(8):
                    em.dma("sy", "msk", mcol_f[:, qg * 8 + kc:qg * 8 + kc + 1],
                           t["l1_mcol"][qg, kc, :], writes=[b_mcf])
                    em.dma("sy", "msk", mcol_b[:, qg * 8 + kc:qg * 8 + kc + 1],
                           t["l1_mcol_bf"][qg, kc, :], writes=[b_mcb])
            em.seal([b_lnw, b_cos, b_sin, b_l1m, b_mcf, b_mcb], "msk")
            em.fence()

        if True:
            lnsc = ExitStack()
            with lnsc:
                _ = ln_stream(c, em, lambda ch: r_ap[ch * 128:(ch + 1) * 128, :],
                              xn_bf, b_xn, DC, V, D, RMS_EPS, lnsc, rms=True,
                              fold=(fold_fns,
                                    lambda ch: r_ap[ch * 128:(ch + 1) * 128, :],
                                    r_bufs, None))
                for dc in range(DC):
                    em.op("ve", lambda e, dc=dc: e.tensor_scalar(
                        xn_bf[:, dc * V:(dc + 1) * V], xn_bf[:, dc * V:(dc + 1) * V],
                        lnw_sb[:, dc:dc + 1], None, ALU.mult),
                        reads=[b_lnw], writes=[b_xn])
                em.fence()

        if True:
            kvsc = ExitStack()
            with kvsc:
                wkst = c.sbuf((128, 2048), F32, scope=kvsc, side="right")
                b_wkst = Buf("wkst")
                wkbf = c.sbuf((128, 2048), BF16, scope=kvsc, side="right")
                b_wkbf = Buf("wkbf")
                rtmp = [c.sbuf((64, V), F32, scope=kvsc, side="right") for _ in range(2)]
                b_rt = [Buf(), Buf()]
                k2t = c.sbuf((128, V), BF16, scope=kvsc, side="right")
                b_k2t = Buf("k2t3")
                kps = c.psum((128, V), scope=kvsc)
                b_kps = Buf("kps")
                k2_ps = c.psum((1, V), scope=kvsc)
                b_k2 = Buf("k2ps3")
                for kh in range(4):
                    em.dma("sy", "wkA" if kh % 2 == 0 else "wkB", wkst[:, :],
                           t["wk_t"][kh], writes=[b_wkst])
                    em.op("ve", lambda e: e.tensor_copy(wkbf[:, :], wkst[:, :]),
                          reads=[b_wkst], writes=[b_wkbf])
                    for dc in range(DC):
                        for (o, w) in NG_V:
                            em.op("pe", lambda e, dc=dc, o=o, w=w: e.matmul(
                                kps[:, o:o + w], wkbf[:, dc * 128:(dc + 1) * 128],
                                xn_bf[:, dc * V + o:dc * V + o + w],
                                start=(dc == 0), stop=(dc == DC - 1)),
                                reads=[b_wkbf, b_xn], writes=[b_kps])
                    ra, bra = rtmp[0], b_rt[0]
                    rb, brb = rtmp[1], b_rt[1]
                    em.op("ve", lambda e, ra=ra: e.tensor_tensor(
                        ra[:, :], kps[0:64, :], cos_sb[:, :], ALU.mult),
                        reads=[b_kps, b_cos], writes=[bra])
                    em.op("ve", lambda e, rb=rb: e.tensor_tensor(
                        rb[:, :], kps[64:128, :], sin_sb[:, :], ALU.mult),
                        reads=[b_kps, b_sin], writes=[brb])
                    em.op("ve", lambda e, kh=kh, ra=ra, rb=rb: e.tensor_tensor(
                        kT_bf[0:64, kh * V:(kh + 1) * V], ra[:, :], rb[:, :],
                        ALU.subtract), reads=[bra, brb], writes=[b_kT])
                    em.op("ve", lambda e, ra=ra: e.tensor_tensor(
                        ra[:, :], kps[0:64, :], sin_sb[:, :], ALU.mult),
                        reads=[b_kps, b_sin], writes=[bra])
                    em.op("ve", lambda e, rb=rb: e.tensor_tensor(
                        rb[:, :], kps[64:128, :], cos_sb[:, :], ALU.mult),
                        reads=[b_kps, b_cos], writes=[brb])
                    em.op("ve", lambda e, kh=kh, ra=ra, rb=rb: e.tensor_tensor(
                        kT_bf[64:128, kh * V:(kh + 1) * V], ra[:, :], rb[:, :],
                        ALU.add), reads=[bra, brb], writes=[b_kT])
                    em.op("se", lambda e, kh=kh: e.activation(
                        k2t[:, :], kT_bf[:, kh * V:(kh + 1) * V], AF.Square),
                        reads=[b_kT], writes=[b_k2t])
                    for (o, w) in NG_V:
                        em.op("pe", lambda e, o=o, w=w: e.matmul(
                            k2_ps[0:1, o:o + w], onescol_bf[:, :], k2t[:, o:o + w],
                            start=True, stop=True), reads=[b_k2t, b_ocb], writes=[b_k2])
                    em.op("ve", lambda e, kh=kh: e.reduce_max(
                        krow[0:1, kh * 16:kh * 16 + 1], k2_ps[0:1, :], axis=AX.X),
                        reads=[b_k2], writes=[b_krow])
                    em.op("se", lambda e, kh=kh: e.activation(
                        krow[0:1, kh * 16 + 1:kh * 16 + 2],
                        krow[0:1, kh * 16:kh * 16 + 1], AF.Sqrt),
                        reads=[b_krow], writes=[b_krow])
                    em.op("ve", lambda e, kh=kh: e.tensor_scalar(
                        krow[0:1, kh * 16 + 2:kh * 16 + 3],
                        krow[0:1, kh * 16 + 1:kh * 16 + 2],
                        -1.0, None, ALU.mult), reads=[b_krow], writes=[b_krow])
                em.fence()
        if True:
            vsc = ExitStack()
            with vsc:
                wvst = c.sbuf((128, 512), F32, scope=vsc, side="right")
                b_wvst = Buf("wvst")
                wvbf = c.sbuf((128, DC * 512), BF16, scope=vsc, side="right")
                b_wvbf = Buf("wvbf")
                for dc in range(DC):
                    load_cast(c, em, t["wv"][dc * 128:(dc + 1) * 128, :],
                              wvbf, b_wvbf, dc * 512, 512, wvst, b_wvst, "wvl")
                vps = [c.psum((128, 512), scope=vsc) for _ in range(2)]
                b_vps = [Buf(), Buf()]
                for tc in range(VT):
                    vp, bvp = vps[tc % 2], b_vps[tc % 2]
                    for dc in range(DC):
                        em.op("pe", lambda e, vp=vp, tc=tc, dc=dc: e.matmul(
                            vp[:, :],
                            xn_bf[:, dc * V + tc * 128:dc * V + (tc + 1) * 128],
                            wvbf[:, dc * 512:(dc + 1) * 512],
                            start=(dc == 0), stop=(dc == DC - 1)),
                            reads=[b_xn, b_wvbf], writes=[bvp])
                    em.op("ve", lambda e, vp=vp, tc=tc: e.tensor_copy(
                        v_bf[:, tc * 512:(tc + 1) * 512], vp[:, :]),
                        reads=[bvp], writes=[b_v])
                em.fence()

        if DEBUG_P3:
            if True:
                for nm, sbuf_t, bufo, wdt in [("dbg_xn", xn_bf, b_xn, DC * V),
                                              ("dbg_kT", kT_bf, b_kT, 4 * V),
                                              ("dbg_v", v_bf, b_v, VT * 512)]:
                    ap = c.dout(nm, (128, wdt), BF16)
                    em.dma("sy", "dbg", ap[:, :], sbuf_t[:, :], reads=[bufo])
                em.fence()

        def _hgrp3(hg):
            s8 = ExitStack()
            with s8:
                wqst = c.sbuf((128, 2048), F32, scope=s8, side="right")
                b_wqst = Buf("wqst")
                wqbf = c.sbuf((128, 2048), BF16, scope=s8, side="right")
                b_wqbf = Buf("wqbf")
                q_bf = c.sbuf((128, 1024), BF16, scope=s8, side="right")
                b_q = Buf("qbf")
                rtq = [c.sbuf((64, 1024), F32, scope=s8, side="right") for _ in range(2)]
                b_rtq = [Buf(), Buf()]
                sqt = c.sbuf((128, 1024), BF16, scope=s8, side="right")
                b_sqt = Buf("sqt")
                qnr = c.sbuf((1, 1024), F32, scope=s8, side="right")
                b_qnr = Buf("qnr")
                qrow = c.sbuf((1, 1024), F32, scope=s8, side="right")
                b_qrow = Buf("qrow3")
                negm = c.sbuf((1, 1024), BF16, scope=s8, side="right")
                b_negm = Buf("negm3")
                pt_bf = c.sbuf((128, 8 * 512), BF16, scope=s8, side="right")
                b_pt = Buf("pt3")
                rb_sb = c.sbuf((128, 512), F32, scope=s8, side="right")
                b_rb = Buf("rb3")
                rrow = c.sbuf((1, 512), F32, scope=s8, side="right")
                b_rrow = Buf("rrow3")

                if True:
                    atsc = ExitStack()
                    with atsc:
                        qps = c.psum((128, 512), scope=atsc)
                        b_qps = Buf("qps")
                        q2_ps = c.psum((1, 512), scope=atsc)
                        b_q2 = Buf("q2ps3")
                        stp = [c.psum((128, 2 * 512), scope=atsc) for _ in range(2)]
                        b_stp = [Buf("stA3"), Buf("stB3")]
                        ot_ps = c.psum((128, 512), scope=atsc)
                        b_otp = Buf("otps3")
                        row_ps = c.psum((128, 512), scope=atsc)
                        b_row = Buf("rowps3")
                        for hh in range(4):
                            h = hg * 4 + hh
                            kh = hg
                            em.dma("sy", "wqA" if hh % 2 == 0 else "wqB", wqst[:, :],
                                   t["wq_t"][h], writes=[b_wqst])
                            em.op("ve", lambda e: e.tensor_copy(wqbf[:, :], wqst[:, :]),
                                  reads=[b_wqst], writes=[b_wqbf])
                            for o in (0, 512):
                                for dc in range(DC):
                                    em.op("pe", lambda e, dc=dc, o=o: e.matmul(
                                        qps[:, :],
                                        wqbf[:, dc * 128:(dc + 1) * 128],
                                        xn_bf[:, dc * V + 512 + o:dc * V + 1024 + o],
                                        start=(dc == 0), stop=(dc == DC - 1)),
                                        reads=[b_wqbf, b_xn], writes=[b_qps])
                                ra, bra = rtq[0], b_rtq[0]
                                rb2, brb = rtq[1], b_rtq[1]
                                em.op("ve", lambda e, ra=ra, o=o: e.tensor_tensor(
                                    ra[:, 0:512], qps[0:64, :],
                                    cos_sb[:, 512 + o:1024 + o], ALU.mult),
                                    reads=[b_qps, b_cos], writes=[bra])
                                em.op("ve", lambda e, rb2=rb2, o=o: e.tensor_tensor(
                                    rb2[:, 0:512], qps[64:128, :],
                                    sin_sb[:, 512 + o:1024 + o], ALU.mult),
                                    reads=[b_qps, b_sin], writes=[brb])
                                em.op("ve", lambda e, ra=ra, rb2=rb2, o=o: e.tensor_tensor(
                                    q_bf[0:64, o:o + 512], ra[:, 0:512], rb2[:, 0:512],
                                    ALU.subtract), reads=[bra, brb], writes=[b_q])
                                em.op("ve", lambda e, ra=ra, o=o: e.tensor_tensor(
                                    ra[:, 0:512], qps[0:64, :],
                                    sin_sb[:, 512 + o:1024 + o], ALU.mult),
                                    reads=[b_qps, b_sin], writes=[bra])
                                em.op("ve", lambda e, rb2=rb2, o=o: e.tensor_tensor(
                                    rb2[:, 0:512], qps[64:128, :],
                                    cos_sb[:, 512 + o:1024 + o], ALU.mult),
                                    reads=[b_qps, b_cos], writes=[brb])
                                em.op("ve", lambda e, ra=ra, rb2=rb2, o=o: e.tensor_tensor(
                                    q_bf[64:128, o:o + 512], ra[:, 0:512], rb2[:, 0:512],
                                    ALU.add), reads=[bra, brb], writes=[b_q])
                            em.op("se", lambda e: e.activation(
                                sqt[:, :], q_bf[:, :], AF.Square),
                                reads=[b_q], writes=[b_sqt])
                            for o in (0, 512):
                                em.op("pe", lambda e, o=o: e.matmul(
                                    q2_ps[0:1, :], onescol_bf[:, :], sqt[:, o:o + 512],
                                    start=True, stop=True), reads=[b_sqt, b_ocb], writes=[b_q2])
                                em.op("se", lambda e, o=o: e.activation(
                                    qnr[0:1, o:o + 512], q2_ps[0:1, :], AF.Sqrt),
                                    reads=[b_q2], writes=[b_qnr])
                            em.op("ve", lambda e, kh=kh: e.tensor_scalar(
                                negm[0:1, :], qnr[0:1, :],
                                krow[0:1, kh * 16 + 2:kh * 16 + 3], None, ALU.mult),
                                reads=[b_qnr, b_krow], writes=[b_negm])
                            if DEBUG_P3 and hg == 1 and hh == 1:
                                ap = c.dout("dbg_q5", (128, 1024), BF16)
                                em.dma("sy", "dbg", ap[:, :], q_bf[:, :], reads=[b_q])
                            for qg in range(2):
                                qo = qg * 512
                                for bi in range(4):
                                    kcs = [bi * 2, bi * 2 + 1]
                                    p, bp = stp[bi % 2], b_stp[bi % 2]
                                    for ii, kc in enumerate(kcs):
                                        kcg = qg * 4 + kc
                                        em.op("pe", lambda e, p=p, ii=ii, kcg=kcg, kh=kh, qo=qo: e.matmul(
                                            p[:, ii * 512:(ii + 1) * 512],
                                            kT_bf[:, kh * V + kcg * 128:kh * V + (kcg + 1) * 128],
                                            q_bf[:, qo:qo + 512],
                                            start=True, stop=False),
                                            reads=[b_kT, b_q], writes=[bp])
                                        em.op("pe", lambda e, p=p, ii=ii, qo=qo: e.matmul(
                                            p[:, ii * 512:(ii + 1) * 512], onesrow_bf[:, :],
                                            negm[0:1, qo:qo + 512],
                                            start=False, stop=False),
                                            reads=[b_negm, b_orb], writes=[bp])
                                        em.op("pe", lambda e, p=p, ii=ii, kc=kc: e.matmul(
                                            p[:, ii * 512:(ii + 1) * 512], ident[:, :],
                                            l1m_sb[:, kc * 512:(kc + 1) * 512],
                                            start=False, stop=True),
                                            reads=[b_l1m, b_ident], writes=[bp])
                                    em.op("se", lambda e, p=p, kcs=kcs: e.activation(
                                        pt_bf[:, kcs[0] * 512:(kcs[0] + 2) * 512],
                                        p[:, 0:1024], AF.Exp, scale=INVSD),
                                        reads=[bp], writes=[b_pt])
                                    for kc in kcs:
                                        em.op("ve", lambda e, kc=kc, qg=qg: e.tensor_scalar(
                                            pt_bf[:, kc * 512:(kc + 1) * 512],
                                            pt_bf[:, kc * 512:(kc + 1) * 512],
                                            mcol_f[:, qg * 8 + kc:qg * 8 + kc + 1],
                                            None, ALU.mult),
                                            reads=[b_mcf], writes=[b_pt])
                                for kc in range(8):
                                    kcg = qg * 4 + kc
                                    em.op("pe", lambda e, kc=kc, kcg=kcg, kh=kh: e.matmul(
                                        ot_ps[:, :],
                                        v_bf[:, kcg * 512 + kh * 128:kcg * 512 + (kh + 1) * 128],
                                        pt_bf[:, kc * 512:(kc + 1) * 512],
                                        start=(kc == 0), stop=(kc == 7)),
                                        reads=[b_v, b_pt], writes=[b_otp])
                                    em.op("pe", lambda e, kc=kc, qg=qg: e.matmul(
                                        row_ps[0:1, :],
                                        mcol_b[:, qg * 8 + kc:qg * 8 + kc + 1],
                                        pt_bf[:, kc * 512:(kc + 1) * 512],
                                        start=(kc == 0), stop=(kc == 7)),
                                        reads=[b_mcb, b_pt], writes=[b_row])
                                em.op("ve", lambda e: e.reciprocal(rrow[0:1, :], row_ps[0:1, :]),
                                      reads=[b_row], writes=[b_rrow])
                                em.op("pe", lambda e: e.matmul(
                                    row_ps[:, :], onesrow_f32[:, :], rrow[0:1, :],
                                    start=True, stop=True),
                                    reads=[b_rrow, b_orf], writes=[b_row])
                                em.op("ve", lambda e: e.tensor_copy(rb_sb[:, :], row_ps[:, :]),
                                      reads=[b_row], writes=[b_rb])
                                em.op("ve", lambda e, h=h, qg=qg: e.tensor_tensor(
                                    ot3_all[:, h * 1024 + qg * 512:h * 1024 + (qg + 1) * 512],
                                    ot_ps[:, :], rb_sb[:, :], ALU.mult),
                                    reads=[b_otp, b_rb], writes=[b_otall3])
                        em.fence()

                em.fence()

        for hg in range(4):
            _hgrp3(hg)

        wsc = ExitStack()
        with wsc:
            wst = [c.sbuf((128, 2048), F32, scope=wsc, side="right") for _ in range(2)]
            b_wst = [Buf(), Buf()]
            wbf = [c.sbuf((128, 2048), BF16, scope=wsc, side="right") for _ in range(2)]
            b_wbf = [Buf(), Buf()]
            ystg = [c.sbuf((128, 512), F32, scope=wsc, side="right") for _ in range(4)]
            b_ystg = [Buf(), Buf(), Buf(), Buf()]
            wops = ExitStack()
            with wops:
                yp = [c.psum((128, 512), scope=wops) for _ in range(2)]
                b_yp = [Buf(), Buf()]
                k = 0
                for j in range(16):
                    ws, bws = wst[j % 2], b_wst[j % 2]
                    wb, bwb = wbf[j % 2], b_wbf[j % 2]
                    jeng = "sy" if j % 2 == 0 else "se"
                    em.dma(jeng, "woA" if j % 2 == 0 else "woB", ws[:, :],
                           t["wo1_t"][j], writes=[bws])
                    em.op("ve", lambda e, ws=ws, wb=wb: e.tensor_copy(
                        wb[:, :], ws[:, :]), reads=[bws], writes=[bwb])
                    for (o, w) in [(0, 512), (512, 512)]:
                        p, bp = yp[k % 2], b_yp[k % 2]
                        ys, bys = ystg[k % 4], b_ystg[k % 4]
                        k += 1
                        for ic in range(16):
                            em.op("pe", lambda e, p=p, o=o, w=w, ic=ic, wb=wb: e.matmul(
                                p[:, 0:w],
                                wb[:, ic * 128:(ic + 1) * 128],
                                ot3_all[:, ic * 1024 + o:ic * 1024 + o + w],
                                start=(ic == 0), stop=(ic == 15)),
                                reads=[bwb, b_otall3], writes=[bp])
                        em.op("ve", lambda e, p=p, ys=ys, w=w: e.tensor_copy(
                            ys[:, 0:w], p[:, 0:w]), reads=[bp], writes=[bys])
                        em.dma(jeng, "yst" + "ABCD"[(k - 1) % 4],
                               y3_ap[j * 128:(j + 1) * 128, o:o + w],
                               ys[:, 0:w], reads=[bys])
                em.fence()

def emit_p4(c, em, r_ap, r_bufs, fold_fns, y_ap):
    t = c.t
    sb = c.sb
    nc = c.nc
    onesrow_f32 = sb["onesrow_f32"]
    b_orf = sb["b_onesrow_f32"]
    ident_f = sb["ident_f32"]
    b_idf = sb["b_ident_f32"]
    W = 1024

    p4 = ExitStack()
    with p4:
        x3_bf = c.sbuf((128, DC * W), BF16, scope=p4, side="right")
        b_x3 = Buf("x3")
        yacc = c.sbuf((128, DC * W), F32, scope=p4, side="right")
        b_yacc = Buf("yacc")
        cb_bf = c.sbuf((128, 8 * W), BF16, scope=p4, side="right")
        b_cb = Buf("cb4")

        rsc = ExitStack()
        rw_f = c.sbuf((128, DC * 8), F32, scope=rsc, side="right")
        b_rwf = Buf("rwf")
        b_raw = Buf("rawps4")
        sgcol = c.sbuf((8, 1), F32, scope=rsc, side="right")
        b_sgc = Buf("sgc4")
        onescol = sb["onescol_f32"]
        b_ones = sb["b_onescol_f32"]
        if True:
            for dc in range(DC):
                em.dma("sy", "rwl", rw_f[:, dc * 8:(dc + 1) * 8],
                       t["rwT"][dc * 128:(dc + 1) * 128, :], writes=[b_rwf])
            em.seal([b_rwf], "rwl")
            sgsc = ExitStack()
            sg_ps = c.psum((1, 8), scope=sgsc)
            b_sgp = Buf("sgp4")
            for dc in range(DC):
                em.op("pe", lambda e, dc=dc: e.matmul(
                    sg_ps[0:1, :], onescol[:, :], rw_f[:, dc * 8:(dc + 1) * 8],
                    start=(dc == 0), stop=(dc == DC - 1)),
                    reads=[b_rwf, b_ones], writes=[b_sgp])
            sgrow = c.sbuf((1, 8), F32, scope=rsc, side="right")
            b_sgrow = Buf("sgrow4")
            em.op("ve", lambda e: e.tensor_copy(sgrow[0:1, :], sg_ps[0:1, :]),
                  reads=[b_sgp], writes=[b_sgrow])
            sgc_ps = c.psum((8, 1), scope=sgsc)
            b_sgcp = Buf("sgcp4")
            em.op("pe", lambda e: e.matmul(
                sgc_ps[:, :], sgrow[0:1, :], onesrow_f32[0:1, 0:1],
                start=True, stop=True), reads=[b_sgrow, b_orf], writes=[b_sgcp])
            em.op("ve", lambda e: e.tensor_copy(sgcol[:, :], sgc_ps[:, :]),
                  reads=[b_sgcp], writes=[b_sgc])
            em.fence()
            sgsc.close()

        rawT_holder = []

        def p4_extra(ch, ss, bs):
            rawT_ps = rawT_holder[0]
            for (o, w) in [(0, 512), (512, 512)]:
                em.op("pe", lambda e, ch=ch, o=o, w=w, ss=ss: e.matmul(
                    rawT_ps[:, o:o + w], rw_f[:, ch * 8:(ch + 1) * 8],
                    ss[:, o:o + w],
                    start=(ch == 0), stop=(ch == DC - 1)),
                    reads=[bs, b_rwf], writes=[b_raw])

        lsb = c.sbuf((128, 64), F32, scope=rsc, side="right")
        b_lsb = Buf("lsb")
        if True:
            lnsc0 = ExitStack()
            with lnsc0:
                rs2, b_rs2, mrs2, b_mrs2, _, _ = ln_stream(
                    c, em, lambda ch: r_ap[ch * 128:(ch + 1) * 128, 512:V],
                    x3_bf, b_x3, DC, W, D, LN_EPS, rsc,
                    fold=(fold_fns,
                          lambda ch: r_ap[ch * 128:(ch + 1) * 128, 512:V],
                          r_bufs, None))
                em.fence()
        if True:
            lnsc = ExitStack()
            with lnsc:
                rstg = [c.sbuf((128, W), F32, scope=lnsc, side="right") for _ in range(2)]
                b_rstg = [Buf(), Buf()]
                rawT_ps = c.psum((8, W), scope=lnsc)
                rawT_holder.append(rawT_ps)
                for ch in range(DC):
                    ss, bs = rstg[ch % 2], b_rstg[ch % 2]
                    em.dma("sy" if ch % 2 == 0 else "se",
                           "lnA" if ch % 2 == 0 else "lnB", ss[:, :],
                           r_ap[ch * 128:(ch + 1) * 128, 512:V], writes=[bs],
                           reads=[r_bufs[ch]])
                    p4_extra(ch, ss, bs)
                lT = c.sbuf((8, W), F32, scope=lnsc, side="right")
                b_lT = Buf("lT4")
                em.op("ve", lambda e: e.tensor_tensor(
                    lT[:, :], rawT_ps[:, :], rs2[0:8, :], ALU.mult),
                    reads=[b_raw, b_rs2], writes=[b_lT])
                tmpT = c.sbuf((8, W), F32, scope=lnsc, side="right")
                b_tmpT = Buf("tmpT4")
                em.op("ve", lambda e: e.tensor_scalar(
                    tmpT[:, :], mrs2[0:8, :], sgcol[:, :], None, ALU.mult),
                    reads=[b_mrs2, b_sgc], writes=[b_tmpT])
                em.op("ve", lambda e: e.tensor_tensor(
                    lT[:, :], lT[:, :], tmpT[:, :], ALU.subtract),
                    reads=[b_tmpT], writes=[b_lT])
                tk_ps = c.psum((128, 64), scope=lnsc)
                b_tk = Buf("tkps4")
                for tc in range(8):
                    em.op("pe", lambda e, tc=tc: e.matmul(
                        tk_ps[:, tc * 8:(tc + 1) * 8],
                        lT[:, tc * 128:(tc + 1) * 128],
                        ident_f[0:8, 0:8], start=True, stop=True),
                        reads=[b_lT, b_idf], writes=[b_tk])
                em.op("ve", lambda e: e.tensor_copy(lsb[:, :], tk_ps[:, :]),
                      reads=[b_tk], writes=[b_lsb])
                em.fence()

        if True:
            with rsc:
                ws = c.sbuf((128, 48), F32, scope=rsc, side="right")
                b_ws = Buf("ws4")
                wm = c.sbuf((128, 8), F32, scope=rsc, side="right")
                b_wm = Buf("wm4")
                csb = c.sbuf((128, 8 * 8), F32, scope=rsc, side="right")
                b_csb = Buf("csb4")
                for tc in range(8):
                    L = lambda: lsb[:, tc * 8:(tc + 1) * 8]
                    em.op("ve", lambda e, tc=tc: e.reduce_max(
                        ws[:, tc * 6:tc * 6 + 1], lsb[:, tc * 8:(tc + 1) * 8],
                        axis=AX.X, negate=True), reads=[b_lsb], writes=[b_ws])
                    em.op("ve", lambda e, tc=tc: e.tensor_scalar(
                        ws[:, tc * 6 + 1:tc * 6 + 2], ws[:, tc * 6:tc * 6 + 1],
                        -1.0, None, ALU.mult), reads=[b_ws], writes=[b_ws])
                    em.op("ve", lambda e, tc=tc: e.tensor_scalar(
                        wm[:, :], lsb[:, tc * 8:(tc + 1) * 8],
                        ws[:, tc * 6 + 1:tc * 6 + 2], None, ALU.is_ge),
                        reads=[b_lsb, b_ws], writes=[b_wm])
                    em.op("ve", lambda e, tc=tc: e.scalar_tensor_tensor(
                        wm[:, :], wm[:, :], NEG, lsb[:, tc * 8:(tc + 1) * 8],
                        ALU.mult, ALU.add), reads=[b_wm, b_lsb], writes=[b_wm])
                    em.op("ve", lambda e, tc=tc: e.reduce_max(
                        ws[:, tc * 6 + 2:tc * 6 + 3], wm[:, :], axis=AX.X),
                        reads=[b_wm], writes=[b_ws])
                    em.op("ve", lambda e, tc=tc: e.tensor_scalar(
                        wm[:, :], lsb[:, tc * 8:(tc + 1) * 8],
                        ws[:, tc * 6 + 2:tc * 6 + 3], None, ALU.is_ge),
                        reads=[b_lsb, b_ws], writes=[b_wm])
                    em.op("se", lambda e, tc=tc: e.activation(
                        csb[:, tc * 8:(tc + 1) * 8], lsb[:, tc * 8:(tc + 1) * 8],
                        AF.Exp, bias=ws[:, tc * 6:tc * 6 + 1]),
                        reads=[b_lsb, b_ws], writes=[b_csb])
                    em.op("ve", lambda e, tc=tc: e.tensor_tensor(
                        csb[:, tc * 8:(tc + 1) * 8], csb[:, tc * 8:(tc + 1) * 8],
                        wm[:, :], ALU.mult), reads=[b_wm], writes=[b_csb])
                    em.op("ve", lambda e, tc=tc: e.reduce_sum(
                        ws[:, tc * 6 + 3:tc * 6 + 4], csb[:, tc * 8:(tc + 1) * 8],
                        axis=AX.X), reads=[b_csb], writes=[b_ws])
                    em.op("ve", lambda e, tc=tc: e.reciprocal(
                        ws[:, tc * 6 + 4:tc * 6 + 5], ws[:, tc * 6 + 3:tc * 6 + 4]),
                        reads=[b_ws], writes=[b_ws])
                    em.op("ve", lambda e, tc=tc: e.tensor_scalar(
                        csb[:, tc * 8:(tc + 1) * 8], csb[:, tc * 8:(tc + 1) * 8],
                        ws[:, tc * 6 + 4:tc * 6 + 5], None, ALU.mult),
                        reads=[b_ws], writes=[b_csb])
                crow_ps = c.psum((1, W), scope=rsc)
                b_crp = Buf("crp4")
                crow = c.sbuf((1, W), F32, scope=rsc, side="right")
                b_crow = Buf("crow4")
                cb_ps = c.psum((128, 512), scope=rsc)
                b_cbp = Buf("cbp4")
                for e_ in range(8):
                    for tc in range(8):
                        em.op("pe", lambda e, e_=e_, tc=tc: e.transpose(
                            crow_ps[0:1, tc * 128:(tc + 1) * 128],
                            csb[:, tc * 8 + e_:tc * 8 + e_ + 1], ident_f[:, :]),
                            reads=[b_csb, b_idf], writes=[b_crp])
                    em.op("ve", lambda e: e.tensor_copy(crow[0:1, :], crow_ps[0:1, :]),
                          reads=[b_crp], writes=[b_crow])
                    for (o, w) in [(0, 512), (512, 512)]:
                        em.op("pe", lambda e, o=o, w=w: e.matmul(
                            cb_ps[:, 0:w], onesrow_f32[:, :], crow[0:1, o:o + w],
                            start=True, stop=True), reads=[b_crow, b_orf], writes=[b_cbp])
                        em.op("ve", lambda e, e_=e_, o=o, w=w: e.tensor_copy(
                            cb_bf[:, e_ * W + o:e_ * W + o + w], cb_ps[:, 0:w]),
                            reads=[b_cbp], writes=[b_cb])
                em.fence()

        def _p4_e(e_):
            if True:
                esc = ExitStack()
                with esc:
                    wstg = c.sbuf((128, 2048), F32, scope=esc, side="right")
                    b_wstg = Buf()
                    wstu = c.sbuf((128, 2048), F32, scope=esc, side="right")
                    b_wstu = Buf()
                    wbfg = [c.sbuf((128, 2048), BF16, scope=esc, side="right")
                            for _ in range(2)]
                    b_wbfg = [Buf(), Buf()]
                    wbfu = [c.sbuf((128, 2048), BF16, scope=esc, side="right")
                            for _ in range(2)]
                    b_wbfu = [Buf(), Buf()]
                    h_bf = c.sbuf((128, 11 * W), BF16, scope=esc, side="right")
                    b_h = Buf("h4")
                    sg = c.sbuf((128, W), BF16, scope=esc, side="right")
                    b_sg = Buf("sg4")
                    htmp = c.sbuf((128, W), BF16, scope=esc, side="right")
                    b_htmp = Buf("ht4")
                    gps = c.psum((128, W), scope=esc)
                    b_gps = Buf("gps4")
                    ups = c.psum((128, W), scope=esc)
                    b_ups = Buf("ups4")
                    for fc in range(11):
                        for which, ps, bps in [(fc, gps, b_gps), (fc + 11, ups, b_ups)]:
                            gate = which < 11
                            wb, bwb = (wbfg if gate else wbfu)[fc % 2], \
                                      (b_wbfg if gate else b_wbfu)[fc % 2]
                            wstx, b_wstx = (wstg, b_wstg) if gate else (wstu, b_wstu)
                            em.dma("sy" if gate else "se",
                                   ("w1A" if fc % 2 == 0 else "w1B") if gate
                                   else ("w1C" if fc % 2 == 0 else "w1D"),
                                   wstx[:, :], t["w1_t"][e_, which], writes=[b_wstx])
                            em.op("ve", lambda e, wb=wb, wstx=wstx: e.tensor_copy(
                                wb[:, :], wstx[:, :]), reads=[b_wstx], writes=[bwb])
                            for dc in range(DC):
                                for (o, w) in [(0, 512), (512, 512)]:
                                    em.op("pe", lambda e, ps=ps, dc=dc, o=o, w=w, wb=wb: e.matmul(
                                        ps[:, o:o + w],
                                        wb[:, dc * 128:(dc + 1) * 128],
                                        x3_bf[:, dc * W + o:dc * W + o + w],
                                        start=(dc == 0), stop=(dc == DC - 1)),
                                        reads=[bwb, b_x3], writes=[bps])
                        em.op("se", lambda e: e.activation(
                            sg[:, :], gps[:, :], AF.Silu), reads=[b_gps], writes=[b_sg])
                        em.op("ve", lambda e: e.tensor_tensor(
                            htmp[:, :], sg[:, :], ups[:, :], ALU.mult),
                            reads=[b_sg, b_ups], writes=[b_htmp])
                        em.op("ve", lambda e, fc=fc, e_=e_: e.tensor_tensor(
                            h_bf[:, fc * W:(fc + 1) * W], htmp[:, :],
                            cb_bf[:, e_ * W:(e_ + 1) * W], ALU.mult),
                            reads=[b_htmp, b_cb], writes=[b_h])
                    w2stl = [c.sbuf((128, 1408), F32, scope=esc, side="right")
                             for _ in range(2)]
                    b_w2stl = [Buf(), Buf()]
                    w2bf = [c.sbuf((128, 1408), BF16, scope=esc, side="right")
                            for _ in range(2)]
                    b_w2bf = [Buf(), Buf()]
                    y2ps = [c.psum((128, W), scope=esc) for _ in range(2)]
                    b_y2ps = [Buf(), Buf()]
                    for dc in range(DC):
                        wb, bwb = w2bf[dc % 2], b_w2bf[dc % 2]
                        w2st, b_w2st = w2stl[dc % 2], b_w2stl[dc % 2]
                        em.dma("sy" if dc % 2 == 0 else "se",
                               "w2A" if dc % 2 == 0 else "w2B",
                               w2st[:, :], t["w2_t"][e_, dc], writes=[b_w2st])
                        em.op("ve", lambda e, wb=wb, w2st=w2st: e.tensor_copy(
                            wb[:, :], w2st[:, :]), reads=[b_w2st], writes=[bwb])
                        yp, byp = y2ps[dc % 2], b_y2ps[dc % 2]
                        for fc in range(11):
                            for (o, w) in [(0, 512), (512, 512)]:
                                em.op("pe", lambda e, yp=yp, fc=fc, o=o, w=w, wb=wb: e.matmul(
                                    yp[:, o:o + w],
                                    wb[:, fc * 128:(fc + 1) * 128],
                                    h_bf[:, fc * W + o:fc * W + o + w],
                                    start=(fc == 0), stop=(fc == 10)),
                                    reads=[bwb, b_h], writes=[byp])
                        if e_ == 0:
                            em.op("ve", lambda e, yp=yp, dc=dc: e.tensor_copy(
                                yacc[:, dc * W:(dc + 1) * W], yp[:, :]),
                                reads=[byp], writes=[b_yacc])
                        else:
                            em.op("ve", lambda e, yp=yp, dc=dc: e.tensor_tensor(
                                yacc[:, dc * W:(dc + 1) * W],
                                yacc[:, dc * W:(dc + 1) * W], yp[:, :], ALU.add),
                                reads=[byp], writes=[b_yacc])
                    em.fence()

        for e_ in range(E1):
            _p4_e(e_)
        if True:
            osc = ExitStack()
            with osc:
                rst = [c.sbuf((128, W), F32, scope=osc, side="right") for _ in range(2)]
                b_rst = [Buf(), Buf()]
                outsb = [c.sbuf((128, D), F32, scope=osc, side="right") for _ in range(2)]
                b_out = [Buf(), Buf()]
                tps = [c.psum((128, 128), scope=osc) for _ in range(2)]
                b_tps = [Buf(), Buf()]
                for dc in range(DC):
                    rs, brs = rst[dc % 2], b_rst[dc % 2]
                    em.dma("sy", "orA" if dc % 2 == 0 else "orB", rs[:, :],
                           r_ap[dc * 128:(dc + 1) * 128, 512:V], writes=[brs],
                           reads=[r_bufs[dc]])
                    em.op("ve", lambda e, rs=rs, dc=dc: e.tensor_tensor(
                        yacc[:, dc * W:(dc + 1) * W], yacc[:, dc * W:(dc + 1) * W],
                        rs[:, :], ALU.add), reads=[brs], writes=[b_yacc])
                kk = 0
                for tc in range(8):
                    ob, bob = outsb[tc % 2], b_out[tc % 2]
                    for dc in range(DC):
                        tp, btp = tps[kk % 2], b_tps[kk % 2]
                        kk += 1
                        em.op("pe", lambda e, tp=tp, dc=dc, tc=tc: e.transpose(
                            tp[:, :], yacc[:, dc * W + tc * 128:dc * W + (tc + 1) * 128],
                            ident_f[:, :]), reads=[b_yacc, b_idf], writes=[btp])
                        em.op("ve", lambda e, tp=tp, ob=ob, dc=dc: e.tensor_copy(
                            ob[:, dc * 128:(dc + 1) * 128], tp[:, :]),
                            reads=[btp], writes=[bob])
                    em.dma("sy", "ostA" if tc % 2 == 0 else "ostB",
                           y_ap[tc * 128:(tc + 1) * 128, :], ob[:, :], reads=[bob])
                em.fence()



P2_INPUTS = ["gate_wT", "wg_t", "wu_t", "wd_t"]
P3_INPUTS = ["lnw", "wq_t", "wk_t", "wv", "wo1_t", "ropec", "ropes", "l1_mask",
             "l1_mcol", "l1_mcol_bf"]
P4_INPUTS = ["rwT", "w1_t", "w2_t"]


def declare_inputs_p2(c):
    c.din("gate_wT", (D, 2))
    c.din("wg_t", (E0, 64, 128, 2048))
    c.din("wu_t", (E0, 64, 128, 2048))
    c.din("wd_t", (E0, 16, 128, 8192))


def declare_inputs_p3(c):
    c.din("lnw", (D,))
    c.din("wq_t", (H, 128, 2048))
    c.din("wk_t", (4, 128, 2048))
    c.din("wv", (D, 512))
    c.din("wo1_t", (16, 128, D))
    c.din("ropec", (64, V))
    c.din("ropes", (64, V))
    c.din("l1_mask", (8, 128, 512), BF16)
    c.din("l1_mcol", (2, 8, 128))
    c.din("l1_mcol_bf", (2, 8, 128), BF16)


def declare_inputs_p4(c):
    c.din("rwT", (D, 8))
    c.din("w1_t", (E1, 22, 128, 2048))
    c.din("w2_t", (E1, 16, 128, 1408))


def build_p2():
    c = Ctx()
    declare_consts(c)
    declare_inputs_p2(c)
    r_in = c.din("r_in", (D, V))
    r_out = c.dout("r_out", (D, V))
    with c.es:
        em = Em(c)
        emit_const_loads(c, em)
        with c.nc.Block() as blk:
            for dc in range(DC):
                em.dma("sy", "rcp", r_out[dc * 128:(dc + 1) * 128, :],
                       r_in[dc * 128:(dc + 1) * 128, :])
            em.flush(blk)
        with c.nc.Block() as blk:
            emit_p2(c, em, r_out)
            em.flush(blk)
        finish(c, em)
        c.nc.finalize()
    return c


def build_p3():
    c = Ctx()
    declare_consts(c)
    declare_inputs_p3(c)
    r_in = c.din("r_in", (D, V))
    r_out = c.dout("r_out", (D, V))
    qn_dram = c.dint("qn_scratch", (H, V))
    with c.es:
        em = Em(c)
        emit_const_loads(c, em)
        with c.nc.Block() as blk:
            for dc in range(DC):
                em.dma("sy", "rcp", r_out[dc * 128:(dc + 1) * 128, :],
                       r_in[dc * 128:(dc + 1) * 128, :])
            em.flush(blk)
        with c.nc.Block() as blk:
            emit_p3(c, em, r_out, qn_dram)
            em.flush(blk)
        finish(c, em)
        c.nc.finalize()
    return c


def build_p4():
    c = Ctx()
    declare_consts(c)
    declare_inputs_p4(c)
    r_in = c.din("r_in", (D, V))
    y_out = c.dout("y_out", (1024, D))
    with c.es:
        em = Em(c)
        emit_const_loads(c, em)
        with c.nc.Block() as blk:
            emit_p4(c, em, r_in, y_out)
            em.flush(blk)
        finish(c, em)
        c.nc.finalize()
    return c


def build_fused():
    c = Ctx()
    declare_consts(c)
    c.din("xqT", (D, V))
    c.din("xkvT", (D, S0))
    c.din("w_dq", (D, QP))
    c.din("w_uq", (QP, D))
    c.din("w_dkv", (D, QP))
    c.din("w_ukv", (QP, 2 * D))
    c.din("wo_t", (16, 128, D))
    c.din("qgb", (2, QP))
    c.din("kvgb", (2, QP))
    c.din("l0_bnd", (3, 8, 128, 512), BF16)
    c.din("l0_mcol", (3, 16, 128))
    c.din("l0_mcol_bf", (3, 16, 128), BF16)
    declare_inputs_p2(c)
    declare_inputs_p3(c)
    declare_inputs_p4(c)
    r_int = c.dint("r_int", (D, V))
    qn_dram = c.dint("qn_scratch", (H, V))
    y1a = c.dint("y1a", (D, V))
    y1b = c.dint("y1b", (D, V))
    y2a = c.dint("y2a", (D, V))
    y2b = c.dint("y2b", (D, V))
    y3 = c.dint("y3", (D, 1024))
    y_out = c.dout("y_out", (1024, D))
    r_bufs = [Buf(f"r{i}") for i in range(DC)]
    with c.es:
        em = Em(c)
        emit_const_loads(c, em)
        with c.nc.Block() as blk:
            for dc in range(DC):
                em.dma("sy", "rcp", r_int[dc * 128:(dc + 1) * 128, :],
                       c.t["xqT"][dc * 128:(dc + 1) * 128, :],
                       writes=[r_bufs[dc]])
            em.seal(r_bufs, "rcp")
            em.flush(blk)
        f12 = [lambda ch: y1a[ch * 128:(ch + 1) * 128, :],
               lambda ch: y1b[ch * 128:(ch + 1) * 128, :]]
        f23 = [lambda ch: y2a[ch * 128:(ch + 1) * 128, :],
               lambda ch: y2b[ch * 128:(ch + 1) * 128, :]]
        f34 = [lambda ch: y3[ch * 128:(ch + 1) * 128, :]]
        for emitfn in (lambda: emit_p1(c, em, qn_dram, y1a, y1b),
                       lambda: emit_p2(c, em, r_int, r_bufs, f12, y2a, y2b),
                       lambda: emit_p3(c, em, r_int, r_bufs, f23, y3, qn_dram),
                       lambda: emit_p4(c, em, r_int, r_bufs, f34, y_out)):
            with c.nc.Block() as blk:
                emitfn()
                em.flush(blk)
        finish(c, em)
        c.nc.finalize()
    return c


FUSED_INPUTS = (["xqT", "xkvT", "w_dq", "w_uq", "w_dkv", "w_ukv", "wo_t",
                 "qgb", "kvgb", "l0_bnd", "l0_mcol", "l0_mcol_bf"]
                + P2_INPUTS + P3_INPUTS + P4_INPUTS)



FUSED = True

_BUILD_CACHE = {}


def _get(name, fn):
    if name not in _BUILD_CACHE:
        _BUILD_CACHE[name] = fn()
    return _BUILD_CACHE[name]


def kernel(**inputs):
    shared = stage_shared(inputs)
    cores = [stage_core(inputs, core) for core in range(NCORES)]

    if FUSED:
        cf = _get("fused", build_fused)
        in_maps = []
        for core in range(NCORES):
            m = {}
            for k in FUSED_INPUTS:
                m[k] = shared[k] if k in shared else cores[core][k]
            for nm, _, _ in CONSTS:
                m[nm] = shared[nm]
            in_maps.append(m)
        res = run_bass_kernel_spmd(cf.nc, in_maps, core_ids=list(range(NCORES)))
        out = np.empty((4, 2048, 2048), dtype=np.float32)
        for core in range(NCORES):
            b, par = core // 2, core % 2
            out[b, par * 1024:(par + 1) * 1024, :] = res.results[core]["y_out"]
        return out

    def maps(inames, r_key):
        out = []
        for core in range(NCORES):
            m = {}
            for k in inames:
                m[k] = shared[k] if k in shared else cores[core][k]
            for nm, _, _ in CONSTS:
                m[nm] = shared[nm]
            m["r_in"] = cores[core][r_key] if isinstance(r_key, str) else r_key[core]
            out.append(m)
        return out

    c1 = _get("p1", build_p1)
    res = run_bass_kernel_spmd(c1.nc, maps(P1_INPUTS, "xqT"),
                               core_ids=list(range(NCORES)))
    r = [res.results[i]["r_out"] for i in range(NCORES)]
    c2 = _get("p2", build_p2)
    res = run_bass_kernel_spmd(c2.nc, maps(P2_INPUTS, r),
                               core_ids=list(range(NCORES)))
    r = [res.results[i]["r_out"] for i in range(NCORES)]
    c3 = _get("p3", build_p3)
    res = run_bass_kernel_spmd(c3.nc, maps(P3_INPUTS, r),
                               core_ids=list(range(NCORES)))
    r = [res.results[i]["r_out"] for i in range(NCORES)]
    c4 = _get("p4", build_p4)
    res = run_bass_kernel_spmd(c4.nc, maps(P4_INPUTS, r),
                               core_ids=list(range(NCORES)))
    out = np.empty((4, 2048, 2048), dtype=np.float32)
    for core in range(NCORES):
        b, par = core // 2, core % 2
        out[b, par * 1024:(par + 1) * 1024, :] = res.results[core]["y_out"]
    return out


# version 09
# speedup vs baseline: 1.3390x; 1.3390x over previous
import math
import sys
from contextlib import ExitStack

import numpy as np

if "/opt/trn_rl_repo" not in sys.path:
    sys.path.insert(0, "/opt/trn_rl_repo")
import concourse.bass as bass
import concourse.mybir as mybir
from concourse.bass_utils import run_bass_kernel_spmd

F32 = mybir.dt.float32
BF16 = mybir.dt.bfloat16
AF = mybir.ActivationFunctionType
ALU = mybir.AluOpType
AX = mybir.AxisListType

D = 2048
DC = 16
V = 1536
VT = 12
S0 = 2048
H = 16
DH = 128
QP = 512
FF0 = 8192
E0 = 2
E1 = 8
I1 = 1408
LN_EPS = 1e-5
RMS_EPS = 1e-6
NEG = -1e30
INVSD = 1.0 / math.sqrt(DH)
NCORES = 8

NG_V = [(0, 512), (512, 512), (1024, 512)]



def _retile_lhst(wT, blk):
    K, M = wT.shape
    kc = K // 128
    mb = M // blk
    a = wT.reshape(kc, 128, mb, blk).transpose(2, 1, 0, 3)
    return np.ascontiguousarray(a.reshape(mb, 128, kc * blk))


def _rope_perm():
    p = np.empty(DH, dtype=np.int64)
    p[0:64] = np.arange(0, DH, 2)
    p[64:128] = np.arange(1, DH, 2)
    return p


def stage_shared(I):
    import ml_dtypes
    bf = ml_dtypes.bfloat16
    f32 = np.float32
    asc = np.ascontiguousarray
    s = {}
    s["w_dq"] = asc(np.asarray(I["L0_attn_W_dq"], dtype=f32))
    s["w_uq"] = asc(np.asarray(I["L0_attn_W_uq"], dtype=f32))
    s["w_dkv"] = asc(np.asarray(I["L0_attn_W_dkv"], dtype=f32))
    s["w_ukv"] = asc(np.asarray(I["L0_attn_W_ukv"], dtype=f32))
    w_oT = np.asarray(I["L0_attn_W_o"], dtype=f32).T
    s["wo_t"] = asc(w_oT.reshape(16, 128, 16, 128).transpose(2, 1, 0, 3)
                    .reshape(16, 128, 2048))
    s["qgb"] = asc(np.stack([np.asarray(I["L0_attn_q_ln_g"], dtype=f32),
                             np.asarray(I["L0_attn_q_ln_b"], dtype=f32)]))
    s["kvgb"] = asc(np.stack([np.asarray(I["L0_attn_kv_ln_g"], dtype=f32),
                              np.asarray(I["L0_attn_kv_ln_b"], dtype=f32)]))
    s["gate_wT"] = asc(np.asarray(I["L0_ffn_gate_w"], dtype=f32).T)
    wg = np.asarray(I["L0_ffn_w_gate"], dtype=f32)
    wu = np.asarray(I["L0_ffn_w_up"], dtype=f32)
    wd = np.asarray(I["L0_ffn_w_down"], dtype=f32)
    s["wg_t"] = np.stack([_retile_lhst(asc(wg[e].T), 128) for e in range(E0)])
    s["wu_t"] = np.stack([_retile_lhst(asc(wu[e].T), 128) for e in range(E0)])
    s["wd_t"] = np.stack([_retile_lhst(asc(wd[e].T), 128) for e in range(E0)])
    s["lnw"] = asc(np.asarray(I["L1_attn_ln_w"], dtype=f32))
    perm = _rope_perm()
    wq = np.asarray(I["L1_attn_wq"], dtype=f32)
    wq_p = np.concatenate([wq[:, h * DH:(h + 1) * DH][:, perm] for h in range(H)], axis=1)
    wk = np.asarray(I["L1_attn_wk"], dtype=f32)
    wk_p = np.concatenate([wk[:, kh * DH:(kh + 1) * DH][:, perm] for kh in range(4)], axis=1)
    s["wq_t"] = _retile_lhst(asc(wq_p), DH)
    s["wk_t"] = _retile_lhst(asc(wk_p), DH)
    s["wv"] = asc(np.asarray(I["L1_attn_wv"], dtype=f32))
    wo1 = np.asarray(I["L1_attn_wo"], dtype=f32)
    s["wo1_t"] = asc(wo1.reshape(16, 128, 16, 128).transpose(2, 1, 0, 3)
                     .reshape(16, 128, 2048))
    s["rwT"] = asc(np.asarray(I["L1_ffn_router_w"], dtype=f32).T)
    w1 = np.asarray(I["L1_ffn_w1"], dtype=f32)
    w2 = np.asarray(I["L1_ffn_w2"], dtype=f32)
    s["w1_t"] = np.stack([_retile_lhst(asc(w1[e].T), 128) for e in range(E1)])
    s["w2_t"] = np.stack([_retile_lhst(asc(w2[e].T), 128) for e in range(E1)])
    s["ident_bf"] = np.eye(128, dtype=bf)
    s["ident_f32"] = np.eye(128, dtype=f32)
    s["onesrow_bf"] = np.ones((1, 128), dtype=bf)
    s["onesrow_f32"] = np.ones((1, 128), dtype=f32)
    s["onescol_f32"] = np.ones((128, 1), dtype=f32)
    s["onescol_bf"] = np.ones((128, 1), dtype=bf)
    q_ = np.arange(128)[None, :]
    k_ = np.arange(128)[:, None]
    triL = np.where(q_ > k_, NEG, 0.0).astype(f32)
    triU = np.where(q_ < k_, NEG, 0.0).astype(f32)
    full = np.full((128, 128), NEG, dtype=f32)
    zero = np.zeros((128, 128), dtype=f32)
    l1m = np.zeros((8, 128, 512), dtype=f32)
    for j in range(8):
        for i in range(4):
            dji = j - i
            t = full if (dji < 0 or dji > 4) else (triL if dji == 0 else (triU if dji == 4 else zero))
            l1m[j, :, i * 128:(i + 1) * 128] = t
    s["l1_mask"] = l1m.astype(bf)
    return s


def stage_core(I, core):
    import ml_dtypes
    bf = ml_dtypes.bfloat16
    f32 = np.float32
    b, par = core // 2, core % 2
    x = np.asarray(I["L0_attn_x"], dtype=f32)
    d = {}
    if par == 0:
        xq = np.zeros((V, D), dtype=f32)
        xq[512:, :] = x[b, 0:1024, :]
        xkv = np.zeros((S0, D), dtype=f32)
        xkv[0:1024, :] = x[b, 0:1024, :]
    else:
        xq = x[b, 512:2048, :]
        xkv = x[b, :, :]
    d["xqT"] = np.ascontiguousarray(xq.T)
    d["xkvT"] = np.ascontiguousarray(xkv.T)

    q_ = np.arange(128)[None, :]
    k_ = np.arange(128)[:, None]
    tri = np.where(k_ > q_, NEG, 0.0).astype(f32)
    fullm = np.full((128, 128), NEG, dtype=f32)
    zerom = np.zeros((128, 128), dtype=f32)
    l0_bnd = np.zeros((3, 8, 128, 512), dtype=f32)
    for g in range(3):
        for sl in range(8):
            kc = (4 * g - 4 + sl) if sl < 4 else (4 * g + sl)
            if kc < 0:
                continue
            for i in range(4):
                qt = 4 * g + i
                bnd = (qt - 4) if par == 0 else (qt + 4)
                t = tri if kc == bnd else (fullm if kc > bnd else zerom)
                l0_bnd[g, sl, :, i * 128:(i + 1) * 128] = t
    d["l0_bnd"] = l0_bnd.astype(bf)
    l0_mcol = np.ones((3, 16), dtype=f32)
    for g in range(3):
        nk = 4 * g + 8
        for kc in range(16):
            if kc >= nk:
                l0_mcol[g, kc] = 0.0
            elif par == 0:
                if g == 0:
                    l0_mcol[g, kc] = 1.0 if kc == 0 else 0.0
                elif kc >= 4 * g:
                    l0_mcol[g, kc] = 0.0
    d["l0_mcol"] = np.ascontiguousarray(
        np.broadcast_to(l0_mcol[:, :, None], (3, 16, 128)).astype(f32))
    d["l0_mcol_bf"] = d["l0_mcol"].astype(bf)

    l1_mcol = np.ones((2, 8), dtype=f32)
    if par == 0:
        l1_mcol[0, 0:4] = 0.0
    d["l1_mcol"] = np.ascontiguousarray(
        np.broadcast_to(l1_mcol[:, :, None], (2, 8, 128)).astype(f32))
    d["l1_mcol_bf"] = d["l1_mcol"].astype(bf)

    inv = 1.0 / (10000.0 ** (np.arange(0, DH, 2, dtype=np.float64) / DH))
    gpos = np.arange(V, dtype=np.float64) + (512.0 if par == 1 else -512.0)
    ang = inv[:, None] * gpos[None, :]
    cos = np.cos(ang).astype(f32)
    sin = np.sin(ang).astype(f32)
    if par == 0:
        cos[:, 0:512] = 1.0
        sin[:, 0:512] = 0.0
    d["ropec"] = np.ascontiguousarray(cos)
    d["ropes"] = np.ascontiguousarray(sin)
    return d



ENGS = ("sy", "pe", "ve", "se", "gp")
_ENG_ATTR = {"sy": "sync", "pe": "tensor", "ve": "vector", "se": "scalar", "gp": "gpsimd"}


class MatNode:
    __slots__ = ("eng", "seq", "val")

    def __init__(self, eng, seq):
        self.eng = eng
        self.seq = seq
        self.val = None


class Rec:
    __slots__ = ("fn", "ewaits", "dwaits", "nodes", "dinc")

    def __init__(self, fn):
        self.fn = fn
        self.ewaits = []
        self.dwaits = []
        self.nodes = []
        self.dinc = None


class Buf:
    __slots__ = ("w", "r", "name")

    def __init__(self, name=""):
        self.w = None
        self.r = {}
        self.name = name


class Em:
    def __init__(self, c):
        self.c = c
        self.q = {e: [] for e in ENGS}
        self.esem = {e: c.sem("E_" + e) for e in ENGS}
        self.mats = {e: {} for e in ENGS}
        self.flushed_mats = {e: [] for e in ENGS}
        self.base = {e: 0 for e in ENGS}
        self.dsem = {}
        self.settled = {}
        self.last_dma = {}
        self.pending = {}
        self.fence_seen = {}

    def fence(self):
        toks = []
        for e in ENGS:
            loc = len(self.q[e]) - 1
            while loc >= 0 and self.q[e][loc].dinc is not None:
                loc -= 1
            if loc >= 0:
                node = self.mats[e].get(loc)
                if node is None:
                    node = MatNode(e, loc)
                    self.mats[e][loc] = node
                    self.q[e][loc].nodes.append(node)
                toks.append(("E", e, self.base[e] + loc))
        for name, sem in self.dsem.items():
            if sem.v > self.fence_seen.get(name, 0):
                toks.append(("D", sem, sem.v))
                self.fence_seen[name] = sem.v
        self.pending = {e: list(toks) for e in ENGS}

    def _fence_deps(self, rec, eng):
        toks = self.pending.pop(eng, None)
        if toks:
            for tk in toks:
                self._dep(rec, eng, tk)

    def seal(self, bufs, slot):
        sem = self.stream(slot)
        tok = ("D", sem, sem.v)
        for b in bufs:
            b.w = tok

    def _etok(self, eng):
        return ("E", eng, self.base[eng] + len(self.q[eng]) - 1)

    def stream(self, name):
        if name not in self.dsem:
            self.dsem[name] = self.c.sem("D_" + name)
        return self.dsem[name]

    def _dep(self, rec, eng, tok):
        if tok is None:
            return
        kind = tok[0]
        if kind == "D":
            rec.dwaits.append((tok[1], tok[2]))
            return
        peng, s = tok[1], tok[2]
        if peng == eng:
            if peng == "pe":
                return
            rec.ewaits.append(("DRAIN", peng, s))
            return
        loc = s - self.base[peng]
        if loc < 0:
            hist = self.flushed_mats[peng]
            cover = None
            for hs, hv in hist:
                if hs >= s:
                    cover = (self.esem[peng], hv)
                    break
            if cover is None:
                return
            rec.dwaits.append(cover)
            return
        prec = self.q[peng][loc]
        if prec.dinc is not None:
            rec.dwaits.append(prec.dinc)
            return
        node = self.mats[peng].get(loc)
        if node is None:
            node = MatNode(peng, loc)
            self.mats[peng][loc] = node
            self.q[peng][loc].nodes.append(node)
        rec.ewaits.append(("SEM", peng, node))

    def op(self, eng, fn, reads=(), writes=()):
        rec = Rec(fn)
        self.q[eng].append(rec)
        self._fence_deps(rec, eng)
        tok = self._etok(eng)
        for b in reads:
            self._dep(rec, eng, b.w)
        for b in writes:
            self._dep(rec, eng, b.w)
            for rt in b.r.values():
                self._dep(rec, eng, rt)
        for b in reads:
            b.r[eng] = tok
        for b in writes:
            b.w = tok
            b.r = {}
        return tok

    def dma(self, eng, slot, out, in_, reads=(), writes=(), **kw):
        sem = self.stream(slot)

        def fn(e, out=out, in_=in_, kw=kw):
            return e.dma_start(out=out, in_=in_, **kw)
        rec = Rec(fn)
        self.q[eng].append(rec)
        self._fence_deps(rec, eng)
        for b in reads:
            self._dep(rec, eng, b.w)
        for b in writes:
            if not (b.w is not None and b.w[0] == "D" and b.w[1] is sem):
                self._dep(rec, eng, b.w)
            for rt in b.r.values():
                if not (rt[0] == "D" and rt[1] is sem):
                    self._dep(rec, eng, rt)
        sem.v += 16
        rec.dinc = (sem, sem.v)
        self.last_dma[eng] = (sem, sem.v)
        tok = ("D", sem, sem.v)
        etok = self._etok(eng)
        for b in reads:
            b.r["D:" + slot] = tok
        for b in writes:
            b.w = tok
            b.r = {}
        return tok

    def barrier_tok(self, eng):
        n = self.base[eng] + len(self.q[eng])
        if n == 0:
            return None
        return ("E", eng, n - 1)

    def flush(self, blk):
        for e in ENGS:
            nodes = sorted(self.mats[e].values(), key=lambda n: n.seq)
            sem = self.esem[e]
            for n in nodes:
                sem.v += 1
                n.val = sem.v
                self.flushed_mats[e].append((self.base[e] + n.seq, n.val))
        wm = {e: {} for e in ENGS}

        tail = []
        for name, sem in self.dsem.items():
            if self.settled.get(name, 0) < sem.v:
                tail.append((sem, sem.v))
                self.settled[name] = sem.v
        if tail:
            def fn_settle(e, tail=tail):
                for sem, val in tail[:-1]:
                    e.wait_ge(sem.h, val)
                return e.wait_ge(tail[-1][0].h, tail[-1][1])
            rec = Rec(fn_settle)
            self.q["sy"].append(rec)

        for e in ENGS:
            recs = self.q[e]
            if not recs:
                continue

            def body(eng, recs=recs, e=e, wm=wm[e], base=self.base[e]):
                drained_upto = -1
                for ri, rec in enumerate(recs):
                    need_drain = False
                    for w in rec.ewaits:
                        if w[0] == "DRAIN":
                            s = w[2]
                            if drained_upto < s:
                                need_drain = True
                        else:
                            _, peng, node = w
                            sem = self.esem[peng]
                            if wm.get(id(sem), 0) < node.val:
                                eng.wait_ge(sem.h, node.val)
                                wm[id(sem)] = node.val
                    for sem, val in rec.dwaits:
                        if wm.get(id(sem), 0) < val:
                            eng.wait_ge(sem.h, val)
                            wm[id(sem)] = val
                    if need_drain:
                        eng.drain()
                        drained_upto = base + ri - 1
                    instr = rec.fn(eng)
                    for n in rec.nodes:
                        instr.then_inc(self.esem[e].h, 1)
                    if rec.dinc is not None:
                        instr.then_inc(rec.dinc[0].h, 16)
            getattr(blk, _ENG_ATTR[e])(body)
        for e in ENGS:
            self.base[e] += len(self.q[e])
            self.q[e] = []
            self.mats[e] = {}


class Ctx:
    def __init__(self):
        self.nc = bass.Bass("TRN2", target_bir_lowering=False, debug=False)
        self.es = ExitStack()
        self.sems = {}
        self.t = {}
        self.sb = {}
        self.nsb = 0

    def sem(self, name):
        if name not in self.sems:
            h = self.es.enter_context(self.nc.semaphore(name))

            class S:
                __slots__ = ("h", "v")
            s = S()
            s.h = h
            s.v = 0
            self.sems[name] = s
        return self.sems[name]

    def din(self, name, shape, dtype=F32):
        ap = self.nc.dram_tensor(name, list(shape), dtype, kind="ExternalInput").ap()
        self.t[name] = ap
        return ap

    def dout(self, name, shape, dtype=F32):
        ap = self.nc.dram_tensor(name, list(shape), dtype, kind="ExternalOutput").ap()
        self.t[name] = ap
        return ap

    def dint(self, name, shape, dtype=F32):
        ap = self.nc.dram_tensor(name, list(shape), dtype).ap()
        self.t[name] = ap
        return ap

    def sbuf(self, shape, dtype=F32, scope=None, name=None, side="left"):
        self.nsb += 1
        nm = name or f"sb{self.nsb}"
        return (scope or self.es).enter_context(
            self.nc.sbuf_tensor(nm, list(shape), dtype, side=side))

    def psum(self, shape, dtype=F32, scope=None, name=None):
        self.nsb += 1
        nm = name or f"ps{self.nsb}"
        return (scope or self.es).enter_context(self.nc.psum_tensor(nm, list(shape), dtype))



def ln_rows(c, em, src, srcbuf, nch, width, colstride, dim, eps, scope, rms=False):
    ng = [(o, min(512, width - o)) for o in range(0, width, 512)]
    rows = c.sbuf((1, 4 * width), scope=scope, side="right")
    inner = ExitStack()
    st_ps = c.psum((1, width), scope=inner) if not rms else None
    sq_ps = c.psum((1, width), scope=inner)
    b_st = Buf("st")
    b_sq = Buf("sq")
    tmp = [c.sbuf((128, width), scope=inner, side="right") for _ in range(2)]
    b_tmp = [Buf("t0"), Buf("t1")]
    onescol = c.sb["onescol_f32"]
    b_ones = c.sb["b_onescol_f32"]
    for ch in range(nch):
        co = ch * colstride
        tb = b_tmp[ch % 2]
        tt = tmp[ch % 2]
        em.op("se", lambda e, tt=tt, co=co, w=width: e.activation(
            tt[:, :], src[:, co:co + w], AF.Square), reads=[srcbuf], writes=[tb])
        for (o, w) in ng:
            if not rms:
                em.op("pe", lambda e, o=o, w=w, co=co, ch=ch: e.matmul(
                    st_ps[0:1, o:o + w], onescol[:, :], src[:, co + o:co + o + w],
                    start=(ch == 0), stop=(ch == nch - 1)),
                    reads=[srcbuf, b_ones], writes=[b_st])
            em.op("pe", lambda e, o=o, w=w, ch=ch, tt=tt: e.matmul(
                sq_ps[0:1, o:o + w], onescol[:, :], tt[:, o:o + w],
                start=(ch == 0), stop=(ch == nch - 1)),
                reads=[tb, b_ones], writes=[b_sq])
    return _ln_rows_finish(c, em, rows, st_ps, b_st, sq_ps, b_sq, inner, width,
                           ng, dim, eps, scope, rms)[:4]


def _ln_rows_finish(c, em, rows, st_ps, b_st, sq_ps, b_sq, inner, width, ng,
                    dim, eps, scope, rms):
    b_rows = Buf("rows")
    if not rms:
        em.op("ve", lambda e: e.tensor_scalar(rows[0:1, 0:width], st_ps[0:1, :],
                                              1.0 / dim, None, ALU.mult),
              reads=[b_st], writes=[b_rows])
        em.op("ve", lambda e: e.scalar_tensor_tensor(
            rows[0:1, width:2 * width], rows[0:1, 0:width], -1.0,
            rows[0:1, 0:width], ALU.mult, ALU.mult), reads=[b_rows], writes=[b_rows])
        em.op("ve", lambda e: e.scalar_tensor_tensor(
            rows[0:1, width:2 * width], sq_ps[0:1, :], 1.0 / dim,
            rows[0:1, width:2 * width], ALU.mult, ALU.add),
            reads=[b_sq, b_rows], writes=[b_rows])
    else:
        em.op("ve", lambda e: e.tensor_scalar(rows[0:1, width:2 * width], sq_ps[0:1, :],
                                              1.0 / dim, None, ALU.mult),
              reads=[b_sq], writes=[b_rows])
    em.op("ve", lambda e: e.tensor_scalar(rows[0:1, width:2 * width],
                                          rows[0:1, width:2 * width],
                                          float(eps), None, ALU.add),
          reads=[b_rows], writes=[b_rows])
    em.op("se", lambda e: e.activation(rows[0:1, 2 * width:3 * width],
                                       rows[0:1, width:2 * width], AF.Sqrt),
          reads=[b_rows], writes=[b_rows])
    em.op("ve", lambda e: e.reciprocal(rows[0:1, 3 * width:4 * width],
                                       rows[0:1, 2 * width:3 * width]),
          reads=[b_rows], writes=[b_rows])
    inner.close()
    inner2 = ExitStack()
    rs_ps = c.psum((128, width), scope=inner2)
    b_rsps = Buf("rsps")
    onesrow = c.sb["onesrow_f32"]
    b_or = c.sb["b_onesrow_f32"]
    for o, w in ng:
        em.op("pe", lambda e, o=o, w=w: e.matmul(
            rs_ps[:, o:o + w], onesrow[:, :],
            rows[0:1, 3 * width + o:3 * width + o + w],
            start=True, stop=True), reads=[b_rows, b_or], writes=[b_rsps])
    rs_sb = c.sbuf((128, width), scope=scope, side="right")
    b_rs = Buf("rs")
    em.op("ve", lambda e: e.tensor_copy(rs_sb[:, :], rs_ps[:, :]),
          reads=[b_rsps], writes=[b_rs])
    mrs_sb = None
    b_mrs = None
    if not rms:
        em.op("ve", lambda e: e.tensor_tensor(rows[0:1, width:2 * width],
                                              rows[0:1, 0:width],
                                              rows[0:1, 3 * width:4 * width], ALU.mult),
              reads=[b_rows], writes=[b_rows])
        for o, w in ng:
            em.op("pe", lambda e, o=o, w=w: e.matmul(
                rs_ps[:, o:o + w], onesrow[:, :], rows[0:1, width + o:width + o + w],
                start=True, stop=True), reads=[b_rows, b_or], writes=[b_rsps])
        mrs_sb = c.sbuf((128, width), scope=scope, side="right")
        b_mrs = Buf("mrs")
        em.op("ve", lambda e: e.tensor_copy(mrs_sb[:, :], rs_ps[:, :]),
              reads=[b_rsps], writes=[b_mrs])
    inner2.close()
    return rs_sb, b_rs, mrs_sb, b_mrs, b_rows


def ln_stream(c, em, src_row_ap, dst, dstbuf, nch, width, dim, eps, scope,
              rms=False, dst_stride=None, extra=None, fold=None):
    if dst_stride is None:
        dst_stride = width
    ng = [(o, min(512, width - o)) for o in range(0, width, 512)]
    stg = [c.sbuf((128, width), F32, scope=scope, side="right") for _ in range(2)]
    b_stg = [Buf("stg0"), Buf("stg1")]
    rows = c.sbuf((1, 4 * width), scope=scope, side="right")
    if fold is not None:
        slab_fns, rnew_fn, r_bufs, slab_bufs = fold
        fstg = [c.sbuf((128, width), F32, scope=scope, side="right")
                for _ in range(len(slab_fns))]
        b_fstg = [Buf("f%d" % i) for i in range(len(slab_fns))]
    inner = ExitStack()
    st_ps = c.psum((1, width), scope=inner) if not rms else None
    sq_ps = c.psum((1, width), scope=inner)
    b_st = Buf("st")
    b_sq = Buf("sq")
    tmp = [c.sbuf((128, width), scope=inner, side="right") for _ in range(2)]
    b_tmp = [Buf(), Buf()]
    onescol = c.sb["onescol_f32"]
    b_ones = c.sb["b_onescol_f32"]
    for ch in range(nch):
        ss, bs = stg[ch % 2], b_stg[ch % 2]
        tt, tb = tmp[ch % 2], b_tmp[ch % 2]
        em.dma("sy" if ch % 2 == 0 else "se",
               "lnA" if ch % 2 == 0 else "lnB", ss[:, :], src_row_ap(ch),
               writes=[bs], reads=[r_bufs[ch]] if fold is not None else ())
        if fold is not None:
            for si, sf in enumerate(slab_fns):
                fs, bfs = fstg[si], b_fstg[si]
                em.dma("sy" if si % 2 == 0 else "se",
                       "fs" + "ABCD"[2 * si + ch % 2], fs[:, :], sf(ch),
                       writes=[bfs])
                em.op("ve", lambda e, ss=ss, fs=fs: e.tensor_tensor(
                    ss[:, :], ss[:, :], fs[:, :], ALU.add),
                    reads=[bfs], writes=[bs])
            em.dma("se", "rnwA" if ch % 2 == 0 else "rnwB", rnew_fn(ch),
                   ss[:, :], reads=[bs], writes=[r_bufs[ch]])
        em.op("se", lambda e, tt=tt, ss=ss: e.activation(
            tt[:, :], ss[:, :], AF.Square), reads=[bs], writes=[tb])
        for (o, w) in ng:
            if not rms:
                em.op("pe", lambda e, o=o, w=w, ch=ch, ss=ss: e.matmul(
                    st_ps[0:1, o:o + w], onescol[:, :], ss[:, o:o + w],
                    start=(ch == 0), stop=(ch == nch - 1)),
                    reads=[bs, b_ones], writes=[b_st])
            em.op("pe", lambda e, o=o, w=w, ch=ch, tt=tt: e.matmul(
                sq_ps[0:1, o:o + w], onescol[:, :], tt[:, o:o + w],
                start=(ch == 0), stop=(ch == nch - 1)),
                reads=[tb, b_ones], writes=[b_sq])
    rs_sb, b_rs, mrs_sb, b_mrs, b_rows = _ln_rows_finish(
        c, em, rows, st_ps, b_st, sq_ps, b_sq, inner, width, ng, dim, eps,
        scope, rms)
    for ch in range(nch):
        ss, bs = stg[ch % 2], b_stg[ch % 2]
        co = ch * dst_stride
        em.dma("sy" if ch % 2 == 0 else "se",
               "lnA" if ch % 2 == 0 else "lnB", ss[:, :], src_row_ap(ch),
               writes=[bs], reads=[r_bufs[ch]] if fold is not None else ())
        if extra is not None:
            extra(ch, ss, bs)
        if rms:
            em.op("ve", lambda e, ss=ss, co=co: e.tensor_tensor(
                dst[:, co:co + width], ss[:, :], rs_sb[:, :], ALU.mult),
                reads=[bs, b_rs], writes=[dstbuf])
        else:
            em.op("ve", lambda e, ss=ss: e.tensor_tensor(
                ss[:, :], ss[:, :], rs_sb[:, :], ALU.mult),
                reads=[bs, b_rs], writes=[bs])
            em.op("ve", lambda e, ss=ss, co=co: e.tensor_tensor(
                dst[:, co:co + width], ss[:, :], mrs_sb[:, :], ALU.subtract),
                reads=[bs, b_mrs], writes=[dstbuf])
    return rs_sb, b_rs, mrs_sb, b_mrs, rows, b_rows


def ln_apply(c, em, src, srcbuf, dst, dstbuf, nch, width, colstride,
             rs_sb, b_rs, mrs_sb, b_mrs, scope):
    tmp = [c.sbuf((128, width), scope=scope, side="right") for _ in range(2)]
    btmp = [Buf(), Buf()]
    for ch in range(nch):
        co = ch * colstride
        if mrs_sb is None:
            em.op("ve", lambda e, co=co: e.tensor_tensor(
                dst[:, co:co + width], src[:, co:co + width], rs_sb[:, :], ALU.mult),
                reads=[srcbuf, b_rs], writes=[dstbuf])
        else:
            tb, tt = btmp[ch % 2], tmp[ch % 2]
            em.op("ve", lambda e, co=co, tt=tt: e.tensor_tensor(
                tt[:, :], src[:, co:co + width], rs_sb[:, :], ALU.mult),
                reads=[srcbuf, b_rs], writes=[tb])
            em.op("ve", lambda e, co=co, tt=tt: e.tensor_tensor(
                dst[:, co:co + width], tt[:, :], mrs_sb[:, :], ALU.subtract),
                reads=[tb, b_mrs], writes=[dstbuf])


def load_cast(c, em, dram_src, dst, dstbuf, col0, ncols, stage, b_stage,
              slot, scale_col=None, b_scale=None, scratch=None):
    em.dma("sy", slot, stage[:, 0:ncols], dram_src, writes=[b_stage])
    if scale_col is None:
        em.op("ve", lambda e: e.tensor_copy(dst[:, col0:col0 + ncols], stage[:, 0:ncols]),
              reads=[b_stage], writes=[dstbuf])
    else:
        em.op("ve", lambda e: e.tensor_scalar(dst[:, col0:col0 + ncols],
                                              stage[:, 0:ncols],
                                              scale_col, None, ALU.mult),
              reads=[b_stage, b_scale], writes=[dstbuf])
    if scratch is not None:
        em.dma("gp", slot + "_sc", scratch, dst[:, col0:col0 + ncols], reads=[dstbuf])



def emit_p1(c, em, qn_dram, y1a_ap, y1b_ap):
    t = c.t
    sb = c.sb
    nc = c.nc
    ident = sb["ident_bf"]
    b_ident = sb["b_ident_bf"]
    onesrow_bf = sb["onesrow_bf"]
    b_orb = sb["b_onesrow_bf"]
    onesrow_f32 = sb["onesrow_f32"]
    b_orf = sb["b_onesrow_f32"]
    onescol_bf = sb["onescol_bf"]
    b_ocb = sb["b_onescol_bf"]
    onescol = sb["onescol_f32"]
    b_ones = sb["b_onescol_f32"]

    p1 = ExitStack()
    with p1:
        ckv_bf = c.sbuf((128, 4 * S0), BF16, scope=p1)
        b_ckv = Buf("ckv")
        qT_bf = c.sbuf((128, H * V), BF16, scope=p1)
        b_qT = Buf("qT")
        mcol_f = c.sbuf((128, 48), F32, scope=p1)
        b_mcf = Buf("mcf")
        mcol_b = c.sbuf((128, 48), BF16, scope=p1)
        b_mcb = Buf("mcb")
        gq_sb = c.sbuf((128, 8), F32, scope=p1)
        b_gq = Buf("gq")
        gkv_sb = c.sbuf((128, 8), F32, scope=p1)
        b_gkv = Buf("gkv")

        if True:
            for g in range(3):
                for kc in range(16):
                    em.dma("sy", "msk", mcol_f[:, g * 16 + kc:g * 16 + kc + 1],
                           t["l0_mcol"][g, kc, :], writes=[b_mcf])
                    em.dma("sy", "msk", mcol_b[:, g * 16 + kc:g * 16 + kc + 1],
                           t["l0_mcol_bf"][g, kc, :], writes=[b_mcb])
            for i in range(2):
                for m in range(4):
                    em.dma("sy", "msk", gq_sb[:, i * 4 + m:i * 4 + m + 1],
                           t["qgb"][i, m * 128:(m + 1) * 128], writes=[b_gq])
                    em.dma("sy", "msk", gkv_sb[:, i * 4 + m:i * 4 + m + 1],
                           t["kvgb"][i, m * 128:(m + 1) * 128], writes=[b_gkv])
            em.seal([b_mcf, b_mcb, b_gq, b_gkv], "msk")
            em.fence()

        sq_scope = ExitStack()
        with sq_scope:
            cq_bf = c.sbuf((128, 4 * V), BF16, scope=sq_scope, side="right")
            b_cq = Buf("cq")
            cqraw = c.sbuf((128, 4 * V), F32, scope=sq_scope, side="right")
            b_cqr = Buf("cqraw")
            s2 = ExitStack()
            with s2:
                xq_bf = c.sbuf((128, DC * V), BF16, scope=s2, side="right")
                b_xq = Buf("xq")
                if True:
                    lnsc = ExitStack()
                    with lnsc:
                        ln_stream(c, em, lambda ch: t["xqT"][ch * 128:(ch + 1) * 128, :],
                                  xq_bf, b_xq, DC, V, D, LN_EPS, lnsc)
                        em.fence()

                s3 = ExitStack()
                with s3:
                    dq_bf = c.sbuf((128, DC * QP), BF16, scope=s3, side="right")
                    b_dq = Buf("dq")
                    wst = c.sbuf((128, QP), F32, scope=s3, side="right")
                    b_wst = Buf("wst")
                    if True:
                        ps_scope = ExitStack()
                        with ps_scope:
                            for dc in range(DC):
                                load_cast(c, em, t["w_dq"][dc * 128:(dc + 1) * 128, :],
                                          dq_bf, b_dq, dc * QP, QP, wst, b_wst, "wA")
                            ps = [c.psum((128, 512), scope=ps_scope) for _ in range(2)]
                            b_ps = [Buf("cqps0"), Buf("cqps1")]
                            k = 0
                            for m in range(4):
                                for (o, w) in NG_V:
                                    p, bp = ps[k % 2], b_ps[k % 2]
                                    k += 1
                                    for dc in range(DC):
                                        em.op("pe", lambda e, p=p, m=m, o=o, w=w, dc=dc: e.matmul(
                                            p[:, 0:w],
                                            dq_bf[:, dc * QP + m * 128:dc * QP + (m + 1) * 128],
                                            xq_bf[:, dc * V + o:dc * V + o + w],
                                            start=(dc == 0), stop=(dc == DC - 1)),
                                            reads=[b_dq, b_xq], writes=[bp])
                                    em.op("ve", lambda e, p=p, m=m, o=o, w=w: e.tensor_copy(
                                        cqraw[:, m * V + o:m * V + o + w], p[:, 0:w]),
                                        reads=[bp], writes=[b_cqr])
                            em.fence()
            if True:
                lnsc = ExitStack()
                with lnsc:
                    rs, b_rs, mrs, b_mrs = ln_rows(c, em, cqraw, b_cqr, 4, V, V,
                                                   QP, LN_EPS, lnsc)
                    tmp = [c.sbuf((128, V), scope=lnsc, side="right") for _ in range(2)]
                    btmp = [Buf(), Buf()]
                    for m in range(4):
                        tb, tt = btmp[m % 2], tmp[m % 2]
                        em.op("ve", lambda e, m=m, tt=tt: e.tensor_tensor(
                            tt[:, :], cqraw[:, m * V:(m + 1) * V], rs[:, :], ALU.mult),
                            reads=[b_cqr, b_rs], writes=[tb])
                        em.op("ve", lambda e, tt=tt: e.tensor_tensor(
                            tt[:, :], tt[:, :], mrs[:, :], ALU.subtract),
                            reads=[tb, b_mrs], writes=[tb])
                        em.op("ve", lambda e, m=m, tt=tt: e.tensor_scalar(
                            cq_bf[:, m * V:(m + 1) * V], tt[:, :],
                            gq_sb[:, m:m + 1], gq_sb[:, 4 + m:4 + m + 1],
                            ALU.mult, op1=ALU.add),
                            reads=[tb, b_gq], writes=[b_cq])
                    em.fence()
            s4 = ExitStack()
            with s4:
                uq_bf = c.sbuf((128, 4 * D), BF16, scope=s4, side="right")
                b_uq = Buf("uq")
                wst = c.sbuf((128, D), F32, scope=s4, side="right")
                b_wst = Buf("wstq")
                sqt = [c.sbuf((128, V), BF16, scope=s4, side="right") for _ in range(2)]
                b_sqt = [Buf(), Buf()]
                qnrow = [c.sbuf((1, V), F32, scope=s4, side="right") for _ in range(2)]
                b_qnr = [Buf(), Buf()]
                if True:
                    ps_scope = ExitStack()
                    with ps_scope:
                        q2_ps = c.psum((1, V), scope=ps_scope)
                        b_q2 = Buf("q2ps")
                        for kc in range(4):
                            load_cast(c, em, t["w_uq"][kc * 128:(kc + 1) * 128, :],
                                      uq_bf, b_uq, kc * D, D, wst, b_wst, "wA")
                        ps = [c.psum((128, 512), scope=ps_scope) for _ in range(2)]
                        b_ps = [Buf(), Buf()]
                        k = 0
                        for h in range(H):
                            for (o, w) in NG_V:
                                p, bp = ps[k % 2], b_ps[k % 2]
                                k += 1
                                for kc in range(4):
                                    em.op("pe", lambda e, p=p, h=h, o=o, w=w, kc=kc: e.matmul(
                                        p[:, 0:w],
                                        uq_bf[:, kc * D + h * 128:kc * D + (h + 1) * 128],
                                        cq_bf[:, kc * V + o:kc * V + o + w],
                                        start=(kc == 0), stop=(kc == 3)),
                                        reads=[b_uq, b_cq], writes=[bp])
                                em.op("ve", lambda e, p=p, h=h, o=o, w=w: e.tensor_copy(
                                    qT_bf[:, h * V + o:h * V + o + w], p[:, 0:w]),
                                    reads=[bp], writes=[b_qT])
                            stq, b_st = sqt[h % 2], b_sqt[h % 2]
                            em.op("se", lambda e, h=h, stq=stq: e.activation(
                                stq[:, :], qT_bf[:, h * V:(h + 1) * V], AF.Square),
                                reads=[b_qT], writes=[b_st])
                            for (o, w) in NG_V:
                                em.op("pe", lambda e, o=o, w=w, stq=stq: e.matmul(
                                    q2_ps[0:1, o:o + w], onescol_bf[:, :], stq[:, o:o + w],
                                    start=True, stop=True),
                                    reads=[b_st, b_ocb], writes=[b_q2])
                            qr, b_qr = qnrow[h % 2], b_qnr[h % 2]
                            em.op("se", lambda e, qr=qr: e.activation(
                                qr[0:1, :], q2_ps[0:1, :], AF.Sqrt),
                                reads=[b_q2], writes=[b_qr])
                            em.dma("sy", "qnstA" if h % 2 == 0 else "qnstB", qn_dram[h, :], qr[0:1, :], reads=[b_qr])
                        em.fence()

        s5 = ExitStack()
        with s5:
            dkv_bf = c.sbuf((128, DC * QP), BF16, scope=s5, side="right")
            b_dkv = Buf("dkv")
            wst = c.sbuf((128, QP), F32, scope=s5, side="right")
            b_wst = Buf("wstkv")
            if True:
                for dc in range(DC):
                    load_cast(c, em, t["w_dkv"][dc * 128:(dc + 1) * 128, :],
                              dkv_bf, b_dkv, dc * QP, QP, wst, b_wst, "wA")
                em.fence()
            def _ckv_tg(tg):
                s6 = ExitStack()
                with s6:
                    xkv_bf = c.sbuf((128, DC * 512), BF16, scope=s6, side="right")
                    b_xkb = Buf("xkvb")
                    ckraw = c.sbuf((128, 4 * 512), F32, scope=s6, side="right")
                    b_ckr = Buf("ckraw")
                    if True:
                        lnsc = ExitStack()
                        with lnsc:
                            _ = ln_stream(c, em,
                                          lambda ch: t["xkvT"][ch * 128:(ch + 1) * 128,
                                                               tg * 512:(tg + 1) * 512],
                                          xkv_bf, b_xkb, DC, 512, D, LN_EPS, lnsc)
                            ps = [c.psum((128, 512), scope=lnsc) for _ in range(2)]
                            b_ps = [Buf(), Buf()]
                            for m in range(4):
                                p, bp = ps[m % 2], b_ps[m % 2]
                                for dc in range(DC):
                                    em.op("pe", lambda e, p=p, m=m, dc=dc: e.matmul(
                                        p[:, :],
                                        dkv_bf[:, dc * QP + m * 128:dc * QP + (m + 1) * 128],
                                        xkv_bf[:, dc * 512:(dc + 1) * 512],
                                        start=(dc == 0), stop=(dc == DC - 1)),
                                        reads=[b_dkv, b_xkb], writes=[bp])
                                em.op("ve", lambda e, p=p, m=m: e.tensor_copy(
                                    ckraw[:, m * 512:(m + 1) * 512], p[:, :]),
                                    reads=[bp], writes=[b_ckr])
                            em.fence()
                    if True:
                        lnsc = ExitStack()
                        with lnsc:
                            rs, b_rs, mrs, b_mrs = ln_rows(c, em, ckraw, b_ckr, 4, 512,
                                                           512, QP, LN_EPS, lnsc)
                            tmp = [c.sbuf((128, 512), scope=lnsc, side="right") for _ in range(2)]
                            btmp = [Buf(), Buf()]
                            for m in range(4):
                                tb, tt = btmp[m % 2], tmp[m % 2]
                                em.op("ve", lambda e, m=m, tt=tt: e.tensor_tensor(
                                    tt[:, :], ckraw[:, m * 512:(m + 1) * 512], rs[:, :],
                                    ALU.mult), reads=[b_ckr, b_rs], writes=[tb])
                                em.op("ve", lambda e, tt=tt: e.tensor_tensor(
                                    tt[:, :], tt[:, :], mrs[:, :], ALU.subtract),
                                    reads=[tb, b_mrs], writes=[tb])
                                em.op("ve", lambda e, m=m, tt=tt, tg=tg: e.tensor_scalar(
                                    ckv_bf[:, m * S0 + tg * 512:m * S0 + (tg + 1) * 512],
                                    tt[:, :], gkv_sb[:, m:m + 1], gkv_sb[:, 4 + m:4 + m + 1],
                                    ALU.mult, op1=ALU.add),
                                    reads=[tb, b_gkv], writes=[b_ckv])
                            em.fence()

            for tg in range(4):
                _ckv_tg(tg)
        s7 = ExitStack()
        with s7:
            bnd_sb = c.sbuf((128, 24 * 512), BF16, scope=s7, side="right")
            b_bnd = Buf("bnd")
            ot_all = c.sbuf((128, 8 * V), BF16, scope=s7, side="right")
            b_otall = Buf("otall")
            if True:
                for g in range(3):
                    for sl in range(8):
                        em.dma("sy", "msk", bnd_sb[:, (g * 8 + sl) * 512:(g * 8 + sl + 1) * 512],
                               t["l0_bnd"][g, sl, :, :], writes=[b_bnd])
                em.seal([b_bnd], "msk")
                em.fence()

            def _hgrp1(hg):
                s8 = ExitStack()
                with s8:
                    kT_bf = c.sbuf((128, 4 * S0), BF16, scope=s8, side="right")
                    b_kT = Buf("kT")
                    v_bf = c.sbuf((128, 16 * 512), BF16, scope=s8, side="right")
                    b_v = Buf("v")
                    wst3 = c.sbuf((128, 512), F32, scope=s8, side="right")
                    b_wst3 = Buf("wst3")
                    pt_bf = c.sbuf((128, 16 * 512), BF16, scope=s8, side="right")
                    b_pt = Buf("pt")
                    k2tmp = c.sbuf((128, S0), BF16, scope=s8, side="right")
                    b_k2t = Buf("k2t")
                    krow = c.sbuf((1, 64), F32, scope=s8, side="right")
                    b_krow = Buf("krow")
                    qrow = c.sbuf((1, V), F32, scope=s8, side="right")
                    b_qrow = Buf("qrow")
                    negm = [c.sbuf((1, V), BF16, scope=s8, side="right") for _ in range(2)]
                    b_negm2 = [Buf("negm0"), Buf("negm1")]
                    rb_sb = c.sbuf((128, 512), F32, scope=s8, side="right")
                    b_rb = Buf("rb")
                    rrow = c.sbuf((1, 512), F32, scope=s8, side="right")
                    b_rrow = Buf("rrow")

                    ukv_bf = c.sbuf((128, 4 * 1024), BF16, scope=s8, side="right")
                    b_ukv = Buf("ukv")
                    if True:
                        kvsc = ExitStack()
                        with kvsc:
                            for kc in range(4):
                                load_cast(c, em,
                                          t["w_ukv"][kc * 128:(kc + 1) * 128,
                                                     hg * 512:(hg + 1) * 512],
                                          ukv_bf, b_ukv, kc * 1024, 512,
                                          wst3, b_wst3, "wA")
                                load_cast(c, em,
                                          t["w_ukv"][kc * 128:(kc + 1) * 128,
                                                     2048 + hg * 512:2048 + (hg + 1) * 512],
                                          ukv_bf, b_ukv, kc * 1024 + 512, 512,
                                          wst3, b_wst3, "wA")

                            pk = [c.psum((128, 512), scope=kvsc) for _ in range(2)]
                            b_pk = [Buf(), Buf()]
                            k2_ps = c.psum((1, S0), scope=kvsc)
                            b_k2 = Buf("k2ps")
                            ki = 0
                            for hh in range(4):
                                h = hg * 4 + hh
                                for ng in range(4):
                                    p, bp = pk[ki % 2], b_pk[ki % 2]
                                    ki += 1
                                    for m in range(4):
                                        em.op("pe", lambda e, p=p, hh=hh, ng=ng, m=m: e.matmul(
                                            p[:, :],
                                            ukv_bf[:, m * 1024 + hh * 128:m * 1024 + (hh + 1) * 128],
                                            ckv_bf[:, m * S0 + ng * 512:m * S0 + (ng + 1) * 512],
                                            start=(m == 0), stop=(m == 3)),
                                            reads=[b_ukv, b_ckv], writes=[bp])
                                    em.op("ve", lambda e, p=p, hh=hh, ng=ng: e.tensor_copy(
                                        kT_bf[:, hh * S0 + ng * 512:hh * S0 + (ng + 1) * 512],
                                        p[:, :]), reads=[bp], writes=[b_kT])
                                em.op("se", lambda e, hh=hh: e.activation(
                                    k2tmp[:, :], kT_bf[:, hh * S0:(hh + 1) * S0], AF.Square),
                                    reads=[b_kT], writes=[b_k2t])
                                for o in range(0, S0, 512):
                                    em.op("pe", lambda e, o=o: e.matmul(
                                        k2_ps[0:1, o:o + 512], onescol_bf[:, :],
                                        k2tmp[:, o:o + 512],
                                        start=True, stop=True),
                                        reads=[b_k2t, b_ocb], writes=[b_k2])
                                em.op("ve", lambda e, hh=hh: e.reduce_max(
                                    krow[0:1, hh * 16:hh * 16 + 1],
                                    k2_ps[0:1, :], axis=AX.X), reads=[b_k2], writes=[b_krow])
                                em.op("se", lambda e, hh=hh: e.activation(
                                    krow[0:1, hh * 16 + 1:hh * 16 + 2],
                                    krow[0:1, hh * 16:hh * 16 + 1], AF.Sqrt),
                                    reads=[b_krow], writes=[b_krow])
                                em.op("ve", lambda e, hh=hh: e.tensor_scalar(
                                    krow[0:1, hh * 16 + 2:hh * 16 + 3],
                                    krow[0:1, hh * 16 + 1:hh * 16 + 2],
                                    -1.0, None, ALU.mult), reads=[b_krow], writes=[b_krow])
                                em.dma("sy", "qnld", qrow[0:1, :], qn_dram[h, :],
                                       writes=[b_qrow])
                                em.op("ve", lambda e, hh=hh: e.tensor_scalar(
                                    negm[hh % 2][0:1, :], qrow[0:1, :],
                                    krow[0:1, hh * 16 + 2:hh * 16 + 3],
                                    None, ALU.mult),
                                    reads=[b_qrow, b_krow], writes=[b_negm2[hh % 2]])
                            for kc in range(16):
                                p, bp = pk[ki % 2], b_pk[ki % 2]
                                ki += 1
                                for m in range(4):
                                    em.op("pe", lambda e, p=p, kc=kc, m=m: e.matmul(
                                        p[:, :],
                                        ckv_bf[:, m * S0 + kc * 128:m * S0 + (kc + 1) * 128],
                                        ukv_bf[:, m * 1024 + 512:m * 1024 + 1024],
                                        start=(m == 0), stop=(m == 3)),
                                        reads=[b_ukv, b_ckv], writes=[bp])
                                em.op("ve", lambda e, p=p, kc=kc: e.tensor_copy(
                                    v_bf[:, kc * 512:(kc + 1) * 512], p[:, :]),
                                    reads=[bp], writes=[b_v])
                            em.fence()

                    if True:
                        atsc = ExitStack()
                        with atsc:
                            stp = [c.psum((128, 3 * 512), scope=atsc) for _ in range(2)]
                            b_stp = [Buf("stA"), Buf("stB")]
                            ot_ps = c.psum((128, 512), scope=atsc)
                            b_otp = Buf("otps")
                            row_ps = c.psum((128, 512), scope=atsc)
                            b_row = Buf("rowps")
                            for hh in range(4):
                                h = hg * 4 + hh
                                for g in range(3):
                                    nk = 4 * g + 8
                                    q0 = h * V + g * 512
                                    nbat = (nk + 2) // 3
                                    for bi in range(nbat):
                                        kcs = list(range(bi * 3, min(bi * 3 + 3, nk)))
                                        p, bp = stp[bi % 2], b_stp[bi % 2]
                                        for ii, kc in enumerate(kcs):
                                            sl = None
                                            if g > 0 and 4 * g - 4 <= kc <= 4 * g - 1:
                                                sl = kc - (4 * g - 4)
                                            elif 4 * g + 4 <= kc <= 4 * g + 7:
                                                sl = 4 + (kc - (4 * g + 4))
                                            em.op("pe", lambda e, p=p, ii=ii, kc=kc, hh=hh, q0=q0: e.matmul(
                                                p[:, ii * 512:(ii + 1) * 512],
                                                kT_bf[:, hh * S0 + kc * 128:hh * S0 + (kc + 1) * 128],
                                                qT_bf[:, q0:q0 + 512],
                                                start=True, stop=False),
                                                reads=[b_kT, b_qT], writes=[bp])
                                            em.op("pe", lambda e, p=p, ii=ii, hh=hh, g=g,
                                                  last=(sl is None): e.matmul(
                                                p[:, ii * 512:(ii + 1) * 512], onesrow_bf[:, :],
                                                negm[hh % 2][0:1, g * 512:(g + 1) * 512],
                                                start=False, stop=last),
                                                reads=[b_negm2[hh % 2], b_orb], writes=[bp])
                                            if sl is not None:
                                                em.op("pe", lambda e, p=p, ii=ii, g=g, sl=sl: e.matmul(
                                                    p[:, ii * 512:(ii + 1) * 512], ident[:, :],
                                                    bnd_sb[:, (g * 8 + sl) * 512:(g * 8 + sl + 1) * 512],
                                                    start=False, stop=True),
                                                    reads=[b_bnd, b_ident], writes=[bp])
                                        em.op("se", lambda e, p=p, kcs=kcs: e.activation(
                                            pt_bf[:, kcs[0] * 512:(kcs[0] + len(kcs)) * 512],
                                            p[:, 0:len(kcs) * 512], AF.Exp, scale=INVSD),
                                            reads=[bp], writes=[b_pt])
                                        for kc in kcs:
                                            em.op("ve", lambda e, kc=kc, g=g: e.tensor_scalar(
                                                pt_bf[:, kc * 512:(kc + 1) * 512],
                                                pt_bf[:, kc * 512:(kc + 1) * 512],
                                                mcol_f[:, g * 16 + kc:g * 16 + kc + 1],
                                                None, ALU.mult),
                                                reads=[b_mcf], writes=[b_pt])
                                    for kc in range(nk):
                                        em.op("pe", lambda e, kc=kc, hh=hh, nk=nk: e.matmul(
                                            ot_ps[:, :],
                                            v_bf[:, kc * 512 + hh * 128:kc * 512 + (hh + 1) * 128],
                                            pt_bf[:, kc * 512:(kc + 1) * 512],
                                            start=(kc == 0), stop=(kc == nk - 1)),
                                            reads=[b_v, b_pt], writes=[b_otp])
                                        em.op("pe", lambda e, kc=kc, g=g, nk=nk: e.matmul(
                                            row_ps[0:1, :],
                                            mcol_b[:, g * 16 + kc:g * 16 + kc + 1],
                                            pt_bf[:, kc * 512:(kc + 1) * 512],
                                            start=(kc == 0), stop=(kc == nk - 1)),
                                            reads=[b_mcb, b_pt], writes=[b_row])
                                    em.op("ve", lambda e: e.reciprocal(rrow[0:1, :], row_ps[0:1, :]),
                                          reads=[b_row], writes=[b_rrow])
                                    em.op("pe", lambda e: e.matmul(
                                        row_ps[:, :], onesrow_f32[:, :], rrow[0:1, :],
                                        start=True, stop=True),
                                        reads=[b_rrow, b_orf], writes=[b_row])
                                    em.op("ve", lambda e: e.tensor_copy(rb_sb[:, :], row_ps[:, :]),
                                          reads=[b_row], writes=[b_rb])
                                    em.op("ve", lambda e, hh=hh, g=g, hg=hg: e.tensor_tensor(
                                        ot_all[:, ((hg % 2) * 4 + hh) * V + g * 512:
                                               ((hg % 2) * 4 + hh) * V + (g + 1) * 512],
                                        ot_ps[:, :], rb_sb[:, :], ALU.mult),
                                        reads=[b_otp, b_rb], writes=[b_otall])
                            em.fence()

                    em.fence()


            def _wo_half1(half, y_ap):
                wsc = ExitStack()
                with wsc:
                    wst = [c.sbuf((128, 1024), F32, scope=wsc, side="right")
                           for _ in range(2)]
                    b_wst = [Buf(), Buf()]
                    wbf = [c.sbuf((128, 1024), BF16, scope=wsc, side="right")
                           for _ in range(2)]
                    b_wbf = [Buf(), Buf()]
                    ystg = [c.sbuf((128, 512), F32, scope=wsc, side="right")
                            for _ in range(4)]
                    b_ystg = [Buf(), Buf(), Buf(), Buf()]
                    yp = [c.psum((128, 512), scope=wsc) for _ in range(2)]
                    b_yp = [Buf(), Buf()]
                    k = 0
                    def _ldwo(j):
                        ws, bws = wst[j % 2], b_wst[j % 2]
                        wb, bwb = wbf[j % 2], b_wbf[j % 2]
                        jeng = "sy" if j % 2 == 0 else "se"
                        em.dma(jeng, "woA" if j % 2 == 0 else "woB", ws[:, :],
                               t["wo_t"][j, :, half * 1024:(half + 1) * 1024],
                               writes=[bws])
                        em.op("ve", lambda e, ws=ws, wb=wb: e.tensor_copy(
                            wb[:, :], ws[:, :]), reads=[bws], writes=[bwb])
                    _ldwo(0)
                    for j in range(16):
                        wb, bwb = wbf[j % 2], b_wbf[j % 2]
                        jeng = "sy" if j % 2 == 0 else "se"
                        if j < 15:
                            _ldwo(j + 1)
                        for (o, w) in NG_V:
                            p, bp = yp[k % 2], b_yp[k % 2]
                            ys, b_ys = ystg[k % 4], b_ystg[k % 4]
                            k += 1
                            for ic in range(8):
                                em.op("pe", lambda e, p=p, o=o, w=w, ic=ic, wb=wb: e.matmul(
                                    p[:, 0:w],
                                    wb[:, ic * 128:(ic + 1) * 128],
                                    ot_all[:, ic * V + o:ic * V + o + w],
                                    start=(ic == 0), stop=(ic == 7)),
                                    reads=[bwb, b_otall], writes=[bp])
                            em.op("ve", lambda e, p=p, ys=ys, w=w: e.tensor_copy(
                                ys[:, 0:w], p[:, 0:w]), reads=[bp], writes=[b_ys])
                            em.dma(jeng, "yst" + "ABCD"[(k - 1) % 4],
                                   y_ap[j * 128:(j + 1) * 128, o:o + w],
                                   ys[:, 0:w], reads=[b_ys])
                    em.fence()

            for half in range(2):
                _hgrp1(2 * half)
                _hgrp1(2 * half + 1)
                _wo_half1(half, y1a_ap if half == 0 else y1b_ap)

CONSTS = [("ident_bf", (128, 128), BF16),
          ("ident_f32", (128, 128), F32),
          ("onesrow_bf", (1, 128), BF16),
          ("onesrow_f32", (1, 128), F32),
          ("onescol_f32", (128, 1), F32),
          ("onescol_bf", (128, 1), BF16)]


def declare_consts(c):
    for nm, shape, dt in CONSTS:
        c.din(nm, shape, dt)


def emit_const_loads(c, em):
    sb = c.sb
    with c.nc.Block() as blk:
        for nm, shape, dt in CONSTS:
            sb[nm] = c.sbuf(shape, dt, name="c_" + nm)
            sb["b_" + nm] = Buf(nm)
            em.dma("sy", "cst", sb[nm][:, :], c.t[nm][:, :], writes=[sb["b_" + nm]])
        em.seal([sb["b_" + nm] for nm, _, _ in CONSTS], "cst")
        em.flush(blk)


def declare_inputs_p1(c):
    c.din("xqT", (D, V))
    c.din("xkvT", (D, S0))
    c.din("w_dq", (D, QP))
    c.din("w_uq", (QP, D))
    c.din("w_dkv", (D, QP))
    c.din("w_ukv", (QP, 2 * D))
    c.din("wo_t", (16, 128, D))
    c.din("qgb", (2, QP))
    c.din("kvgb", (2, QP))
    c.din("l0_bnd", (3, 8, 128, 512), BF16)
    c.din("l0_mcol", (3, 16, 128))
    c.din("l0_mcol_bf", (3, 16, 128), BF16)


P1_INPUTS = ["xqT", "xkvT", "w_dq", "w_uq", "w_dkv", "w_ukv", "wo_t", "qgb",
             "kvgb", "l0_bnd", "l0_mcol", "l0_mcol_bf"]


def finish(c, em):
    with c.nc.Block() as blk:
        def _(eng):
            for s in em.dsem.values():
                eng.wait_ge(s.h, s.v)
        blk.sync(_)


def build_p1():
    c = Ctx()
    declare_consts(c)
    declare_inputs_p1(c)
    r_in = c.din("r_in", (D, V))
    r_out = c.dout("r_out", (D, V))
    qn_dram = c.dint("qn_scratch", (H, V))
    with c.es:
        em = Em(c)
        emit_const_loads(c, em)
        with c.nc.Block() as blk:
            for dc in range(DC):
                em.dma("sy", "rcp", r_out[dc * 128:(dc + 1) * 128, :],
                       r_in[dc * 128:(dc + 1) * 128, :])
            em.flush(blk)
        with c.nc.Block() as blk:
            emit_p1(c, em, r_out, qn_dram)
            em.flush(blk)
        finish(c, em)
        c.nc.finalize()
    return c



NG_768 = [(0, 512), (512, 256)]


def emit_p2(c, em, r_ap, r_bufs, fold_fns, y2a_ap, y2b_ap):
    t = c.t
    sb = c.sb
    nc = c.nc
    onesrow_f32 = sb["onesrow_f32"]
    b_orf = sb["b_onesrow_f32"]
    ident_f = sb["ident_f32"]
    b_idf = sb["b_ident_f32"]

    p2 = ExitStack()
    with p2:
        x2_bf = c.sbuf((128, DC * V), BF16, scope=p2, side="right")
        b_x2 = Buf("x2")
        cb_bf = c.sbuf((128, 2 * V), BF16, scope=p2, side="right")
        b_cb = Buf("cb")

        rsc = ExitStack()
        gw_f = c.sbuf((128, DC * 2), F32, scope=rsc, side="right")
        b_gwf = Buf("gwf")
        b_raw = Buf("rawps")
        sgcol = c.sbuf((2, 1), F32, scope=rsc, side="right")
        b_sgc = Buf("sgc")
        onescol = sb["onescol_f32"]
        b_ones = sb["b_onescol_f32"]
        if True:
            for dc in range(DC):
                em.dma("sy", "gwl", gw_f[:, dc * 2:(dc + 1) * 2],
                       t["gate_wT"][dc * 128:(dc + 1) * 128, :], writes=[b_gwf])
            em.seal([b_gwf], "gwl")
            sgsc = ExitStack()
            sg_ps = c.psum((1, 2), scope=sgsc)
            b_sgp = Buf("sgp")
            for dc in range(DC):
                em.op("pe", lambda e, dc=dc: e.matmul(
                    sg_ps[0:1, :], onescol[:, :], gw_f[:, dc * 2:(dc + 1) * 2],
                    start=(dc == 0), stop=(dc == DC - 1)),
                    reads=[b_gwf, b_ones], writes=[b_sgp])
            sgrow = c.sbuf((1, 2), F32, scope=rsc, side="right")
            b_sgrow = Buf("sgrow")
            em.op("ve", lambda e: e.tensor_copy(sgrow[0:1, :], sg_ps[0:1, :]),
                  reads=[b_sgp], writes=[b_sgrow])
            sgc_ps = c.psum((2, 1), scope=sgsc)
            b_sgcp = Buf("sgcp")
            em.op("pe", lambda e: e.matmul(
                sgc_ps[:, :], sgrow[0:1, :], onesrow_f32[0:1, 0:1],
                start=True, stop=True), reads=[b_sgrow, b_orf], writes=[b_sgcp])
            em.op("ve", lambda e: e.tensor_copy(sgcol[:, :], sgc_ps[:, :]),
                  reads=[b_sgcp], writes=[b_sgc])
            em.fence()
            sgsc.close()

        rawT_holder = []

        def p2_extra(ch, ss, bs):
            rawT_ps = rawT_holder[0]
            for (o, w) in NG_V:
                em.op("pe", lambda e, ch=ch, o=o, w=w, ss=ss: e.matmul(
                    rawT_ps[:, o:o + w], gw_f[:, ch * 2:(ch + 1) * 2],
                    ss[:, o:o + w],
                    start=(ch == 0), stop=(ch == DC - 1)),
                    reads=[bs, b_gwf], writes=[b_raw])

        lsb = c.sbuf((128, 2 * VT), F32, scope=rsc, side="right")
        b_lsb2 = Buf("lsb2")
        if True:
            lnsc0 = ExitStack()
            with lnsc0:
                rs2, b_rs2, mrs2, b_mrs2, _, _ = ln_stream(
                    c, em, lambda ch: r_ap[ch * 128:(ch + 1) * 128, :],
                    x2_bf, b_x2, DC, V, D, LN_EPS, rsc,
                    fold=(fold_fns, lambda ch: r_ap[ch * 128:(ch + 1) * 128, :],
                          r_bufs, None))
                em.fence()
        if True:
            lnsc = ExitStack()
            with lnsc:
                rstg = [c.sbuf((128, V), F32, scope=lnsc, side="right") for _ in range(2)]
                b_rstg = [Buf(), Buf()]
                rawT_ps = c.psum((2, V), scope=lnsc)
                rawT_holder.append(rawT_ps)
                for ch in range(DC):
                    ss, bs = rstg[ch % 2], b_rstg[ch % 2]
                    em.dma("sy" if ch % 2 == 0 else "se",
                           "lnA" if ch % 2 == 0 else "lnB", ss[:, :],
                           r_ap[ch * 128:(ch + 1) * 128, :], writes=[bs],
                           reads=[r_bufs[ch]])
                    p2_extra(ch, ss, bs)
                lT = c.sbuf((2, V), F32, scope=lnsc, side="right")
                b_lT = Buf("lT")
                em.op("ve", lambda e: e.tensor_tensor(
                    lT[:, :], rawT_ps[:, :], rs2[0:2, :], ALU.mult),
                    reads=[b_raw, b_rs2], writes=[b_lT])
                tmpT = c.sbuf((2, V), F32, scope=lnsc, side="right")
                b_tmpT = Buf("tmpT")
                em.op("ve", lambda e: e.tensor_scalar(
                    tmpT[:, :], mrs2[0:2, :], sgcol[:, :], None, ALU.mult),
                    reads=[b_mrs2, b_sgc], writes=[b_tmpT])
                em.op("ve", lambda e: e.tensor_tensor(
                    lT[:, :], lT[:, :], tmpT[:, :], ALU.subtract),
                    reads=[b_tmpT], writes=[b_lT])
                tk_ps = c.psum((128, 2 * VT), scope=lnsc)
                b_tk = Buf("tkps")
                for tc in range(VT):
                    em.op("pe", lambda e, tc=tc: e.matmul(
                        tk_ps[:, tc * 2:(tc + 1) * 2],
                        lT[:, tc * 128:(tc + 1) * 128],
                        ident_f[0:2, 0:2], start=True, stop=True),
                        reads=[b_lT, b_idf], writes=[b_tk])
                em.op("ve", lambda e: e.tensor_copy(lsb[:, :], tk_ps[:, :]),
                      reads=[b_tk], writes=[b_lsb2])
                em.fence()

        if True:
            with rsc:
                rws = c.sbuf((128, 8 * VT), F32, scope=rsc, side="right")
                b_rws = Buf("rws")
                csb = c.sbuf((128, 2 * VT), F32, scope=rsc, side="right")
                b_csb = Buf("csb")
                for tc in range(VT):
                    em.op("ve", lambda e, tc=tc: e.tensor_tensor(
                        rws[:, tc * 8:tc * 8 + 1], lsb[:, tc * 2:tc * 2 + 1],
                        lsb[:, tc * 2 + 1:tc * 2 + 2], ALU.subtract),
                        reads=[b_lsb2], writes=[b_rws])
                for tc in range(VT):
                    em.op("se", lambda e, tc=tc: e.activation(
                        rws[:, tc * 8 + 1:tc * 8 + 2], rws[:, tc * 8:tc * 8 + 1],
                        AF.Sigmoid), reads=[b_rws], writes=[b_rws])
                    em.op("ve", lambda e, tc=tc: e.tensor_scalar(
                        rws[:, tc * 8 + 2:tc * 8 + 3], rws[:, tc * 8:tc * 8 + 1],
                        0.0, None, ALU.is_ge), reads=[b_rws], writes=[b_rws])
                    em.op("ve", lambda e, tc=tc: e.tensor_tensor(
                        csb[:, tc * 2:tc * 2 + 1], rws[:, tc * 8 + 1:tc * 8 + 2],
                        rws[:, tc * 8 + 2:tc * 8 + 3], ALU.mult),
                        reads=[b_rws], writes=[b_csb])
                    em.op("ve", lambda e, tc=tc: e.tensor_scalar(
                        rws[:, tc * 8 + 3:tc * 8 + 4], rws[:, tc * 8 + 1:tc * 8 + 2],
                        -1.0, 1.0, ALU.mult, op1=ALU.add),
                        reads=[b_rws], writes=[b_rws])
                    em.op("ve", lambda e, tc=tc: e.tensor_scalar(
                        rws[:, tc * 8 + 4:tc * 8 + 5], rws[:, tc * 8 + 2:tc * 8 + 3],
                        -1.0, 1.0, ALU.mult, op1=ALU.add),
                        reads=[b_rws], writes=[b_rws])
                    em.op("ve", lambda e, tc=tc: e.tensor_tensor(
                        csb[:, tc * 2 + 1:tc * 2 + 2], rws[:, tc * 8 + 3:tc * 8 + 4],
                        rws[:, tc * 8 + 4:tc * 8 + 5], ALU.mult),
                        reads=[b_rws], writes=[b_csb])
                crow_ps = c.psum((1, V), scope=rsc)
                b_crp = Buf("crp")
                crow = c.sbuf((1, V), F32, scope=rsc, side="right")
                b_crow = Buf("crow")
                cb_ps = c.psum((128, 512), scope=rsc)
                b_cbp = Buf("cbp")
                for e_ in range(2):
                    for tc in range(VT):
                        em.op("pe", lambda e, e_=e_, tc=tc: e.transpose(
                            crow_ps[0:1, tc * 128:(tc + 1) * 128],
                            csb[:, tc * 2 + e_:tc * 2 + e_ + 1], ident_f[:, :]),
                            reads=[b_csb, b_idf], writes=[b_crp])
                    em.op("ve", lambda e: e.tensor_copy(crow[0:1, :], crow_ps[0:1, :]),
                          reads=[b_crp], writes=[b_crow])
                    for (o, w) in NG_V:
                        em.op("pe", lambda e, o=o, w=w: e.matmul(
                            cb_ps[:, 0:w], onesrow_f32[:, :], crow[0:1, o:o + w],
                            start=True, stop=True), reads=[b_crow, b_orf], writes=[b_cbp])
                        em.op("ve", lambda e, e_=e_, o=o, w=w: e.tensor_copy(
                            cb_bf[:, e_ * V + o:e_ * V + o + w], cb_ps[:, 0:w]),
                            reads=[b_cbp], writes=[b_cb])
                em.fence()

        h_bf = c.sbuf((128, 32 * V), BF16, scope=p2, side="right")
        b_h = Buf("h")
        def _p2_hh(hh):
            def _p2_he(e_):
                if True:
                    gsc = ExitStack()
                    with gsc:
                        wst = c.sbuf((128, 2048), F32, scope=gsc, side="right")
                        b_wst = Buf("wst")
                        wstu = c.sbuf((128, 2048), F32, scope=gsc, side="right")
                        b_wstu = Buf("wstu")
                        wgbf = [c.sbuf((128, 2048), BF16, scope=gsc, side="right")
                                for _ in range(2)]
                        b_wgbf = [Buf(), Buf()]
                        wubf = [c.sbuf((128, 2048), BF16, scope=gsc, side="right")
                                for _ in range(2)]
                        b_wubf = [Buf(), Buf()]
                        sg = c.sbuf((128, V), BF16, scope=gsc, side="right")
                        b_sg = Buf("sg")
                        gps = c.psum((128, V), scope=gsc)
                        b_gps = Buf("gps")
                        ups = c.psum((128, V), scope=gsc)
                        b_ups = Buf("ups")
                        def _ldgu(fc):
                            gfc = hh * 32 + fc
                            wg_b, bwg = wgbf[fc % 2], b_wgbf[fc % 2]
                            wu_b, bwu = wubf[fc % 2], b_wubf[fc % 2]
                            em.dma("sy", "wgA" if fc % 2 == 0 else "wgB",
                                   wst[:, :], t["wg_t"][e_, gfc], writes=[b_wst])
                            em.op("ve", lambda e, wg_b=wg_b: e.tensor_copy(
                                wg_b[:, :], wst[:, :]), reads=[b_wst], writes=[bwg])
                            em.dma("se", "wuA" if fc % 2 == 0 else "wuB",
                                   wstu[:, :], t["wu_t"][e_, gfc], writes=[b_wstu])
                            em.op("ve", lambda e, wu_b=wu_b: e.tensor_copy(
                                wu_b[:, :], wstu[:, :]), reads=[b_wstu], writes=[bwu])
                        _ldgu(0)
                        for fc in range(32):
                            wg_b, bwg = wgbf[fc % 2], b_wgbf[fc % 2]
                            wu_b, bwu = wubf[fc % 2], b_wubf[fc % 2]
                            if fc < 31:
                                _ldgu(fc + 1)
                            for dc in range(DC):
                                for (o, w) in NG_V:
                                    em.op("pe", lambda e, dc=dc, o=o, w=w, wg_b=wg_b: e.matmul(
                                        gps[:, o:o + w],
                                        wg_b[:, dc * 128:(dc + 1) * 128],
                                        x2_bf[:, dc * V + o:dc * V + o + w],
                                        start=(dc == 0), stop=(dc == DC - 1)),
                                        reads=[bwg, b_x2], writes=[b_gps])
                            for dc in range(DC):
                                for (o, w) in NG_V:
                                    em.op("pe", lambda e, dc=dc, o=o, w=w, wu_b=wu_b: e.matmul(
                                        ups[:, o:o + w],
                                        wu_b[:, dc * 128:(dc + 1) * 128],
                                        x2_bf[:, dc * V + o:dc * V + o + w],
                                        start=(dc == 0), stop=(dc == DC - 1)),
                                        reads=[bwu, b_x2], writes=[b_ups])
                            em.op("se", lambda e: e.activation(
                                sg[:, :], gps[:, :], AF.Silu), reads=[b_gps], writes=[b_sg])
                            em.op("ve", lambda e, fc=fc: e.tensor_tensor(
                                h_bf[:, fc * V:(fc + 1) * V], sg[:, :], ups[:, :],
                                ALU.mult), reads=[b_sg, b_ups], writes=[b_h])
                            em.op("ve", lambda e, fc=fc, e_=e_: e.tensor_tensor(
                                h_bf[:, fc * V:(fc + 1) * V], h_bf[:, fc * V:(fc + 1) * V],
                                cb_bf[:, e_ * V:(e_ + 1) * V], ALU.mult),
                                reads=[b_cb], writes=[b_h])
                        em.fence()
                if True:
                    ysc = ExitStack()
                    with ysc:
                        wdst2 = [c.sbuf((128, 2048), F32, scope=ysc, side="right")
                                 for _ in range(2)]
                        b_wdst2 = [Buf("wdstA"), Buf("wdstB")]
                        wdbf2 = [c.sbuf((128, 4096), BF16, scope=ysc, side="right")
                                 for _ in range(2)]
                        b_wdbf2 = [Buf("wd0"), Buf("wd1")]
                        ys = c.sbuf((128, V), F32, scope=ysc, side="right")
                        b_ys = Buf("ys")
                        yold2 = [c.sbuf((128, V), F32, scope=ysc, side="right")
                                 for _ in range(2)]
                        b_yold2 = [Buf("yo0"), Buf("yo1")]
                        yps = [c.psum((128, V), scope=ysc) for _ in range(2)]
                        b_yps = [Buf(), Buf()]
                        y_ap = y2a_ap if e_ == 0 else y2b_ap
                        def _ldwd(dc):
                            wdbf, b_wdbf = wdbf2[dc % 2], b_wdbf2[dc % 2]
                            eng = "sy" if dc % 2 == 0 else "se"
                            for hf in range(2):
                                wdst, b_wdst = wdst2[hf], b_wdst2[hf]
                                em.dma(eng,
                                       "wdl" + "ABCD"[2 * hf + dc % 2], wdst[:, :],
                                       t["wd_t"][e_, dc,
                                                 :, hh * 4096 + hf * 2048:
                                                 hh * 4096 + (hf + 1) * 2048],
                                       writes=[b_wdst])
                                em.op("ve", lambda e, wdbf=wdbf, wdst=wdst,
                                      hf=hf, b=None: e.tensor_copy(
                                    wdbf[:, hf * 2048:(hf + 1) * 2048], wdst[:, :]),
                                    reads=[b_wdst], writes=[b_wdbf])
                            if hh == 1:
                                em.dma(eng, "yolA" if dc % 2 == 0 else "yolB",
                                       yold2[dc % 2][:, :],
                                       y_ap[dc * 128:(dc + 1) * 128, :],
                                       writes=[b_yold2[dc % 2]])
                        _ldwd(0)
                        for dc in range(DC):
                            wdbf, b_wdbf = wdbf2[dc % 2], b_wdbf2[dc % 2]
                            eng = "sy" if dc % 2 == 0 else "se"
                            yold, b_yold = yold2[dc % 2], b_yold2[dc % 2]
                            if dc < DC - 1:
                                _ldwd(dc + 1)
                            yp, byp = yps[dc % 2], b_yps[dc % 2]
                            for fc in range(32):
                                for (o, w) in NG_V:
                                    em.op("pe", lambda e, yp=yp, fc=fc, o=o, w=w,
                                          wdbf=wdbf, b=None: e.matmul(
                                        yp[:, o:o + w],
                                        wdbf[:, fc * 128:(fc + 1) * 128],
                                        h_bf[:, fc * V + o:fc * V + o + w],
                                        start=(fc == 0), stop=(fc == 31)),
                                        reads=[b_wdbf, b_h], writes=[byp])
                            if hh == 0:
                                em.op("ve", lambda e, yp=yp: e.tensor_copy(
                                    ys[:, :], yp[:, :]), reads=[byp], writes=[b_ys])
                            else:
                                em.op("ve", lambda e, yp=yp, yold=yold: e.tensor_tensor(
                                    ys[:, :], yp[:, :], yold[:, :], ALU.add),
                                    reads=[byp, b_yold], writes=[b_ys])
                            em.dma(eng, "ysA" if dc % 2 == 0 else "ysB",
                                   y_ap[dc * 128:(dc + 1) * 128, :],
                                   ys[:, :], reads=[b_ys])
                        em.fence()

            for e_ in range(2):
                _p2_he(e_)
        for hh in range(2):
            _p2_hh(hh)

DEBUG_P3 = False


def emit_p3(c, em, r_ap, r_bufs, fold_fns, y3_ap, qn_dram):
    t = c.t
    sb = c.sb
    nc = c.nc
    ident = sb["ident_bf"]
    b_ident = sb["b_ident_bf"]
    onesrow_bf = sb["onesrow_bf"]
    b_orb = sb["b_onesrow_bf"]
    onesrow_f32 = sb["onesrow_f32"]
    b_orf = sb["b_onesrow_f32"]
    onescol_bf = sb["onescol_bf"]
    b_ocb = sb["b_onescol_bf"]

    p3 = ExitStack()
    with p3:
        xn_bf = c.sbuf((128, DC * V), BF16, scope=p3, side="right")
        b_xn = Buf("xn")
        lnw_sb = c.sbuf((128, DC), F32, scope=p3, side="right")
        b_lnw = Buf("lnw")
        cos_sb = c.sbuf((64, V), F32, scope=p3, side="right")
        b_cos = Buf("cos")
        sin_sb = c.sbuf((64, V), F32, scope=p3, side="right")
        b_sin = Buf("sin")
        l1m_sb = c.sbuf((128, 8 * 512), BF16, scope=p3, side="right")
        b_l1m = Buf("l1m")
        mcol_f = c.sbuf((128, 16), F32, scope=p3, side="right")
        b_mcf = Buf("mcf3")
        mcol_b = c.sbuf((128, 16), BF16, scope=p3, side="right")
        b_mcb = Buf("mcb3")
        kT_bf = c.sbuf((128, 4 * V), BF16, scope=p3, side="right")
        b_kT = Buf("kT3")
        ot3_all = c.sbuf((128, H * 1024), BF16, scope=p3, side="right")
        b_otall3 = Buf("otall3")
        v_bf = c.sbuf((128, VT * 512), BF16, scope=p3, side="right")
        b_v = Buf("v3")
        krow = c.sbuf((1, 64), F32, scope=p3, side="right")
        b_krow = Buf("krow3")

        if True:
            for dc in range(DC):
                em.dma("sy", "msk", lnw_sb[:, dc:dc + 1],
                       t["lnw"][dc * 128:(dc + 1) * 128], writes=[b_lnw])
            em.dma("sy", "msk", cos_sb[:, :], t["ropec"][:, :], writes=[b_cos])
            em.dma("sy", "msk", sin_sb[:, :], t["ropes"][:, :], writes=[b_sin])
            for j in range(8):
                em.dma("sy", "msk", l1m_sb[:, j * 512:(j + 1) * 512],
                       t["l1_mask"][j, :, :], writes=[b_l1m])
            for qg in range(2):
                for kc in range(8):
                    em.dma("sy", "msk", mcol_f[:, qg * 8 + kc:qg * 8 + kc + 1],
                           t["l1_mcol"][qg, kc, :], writes=[b_mcf])
                    em.dma("sy", "msk", mcol_b[:, qg * 8 + kc:qg * 8 + kc + 1],
                           t["l1_mcol_bf"][qg, kc, :], writes=[b_mcb])
            em.seal([b_lnw, b_cos, b_sin, b_l1m, b_mcf, b_mcb], "msk")
            em.fence()

        if True:
            lnsc = ExitStack()
            with lnsc:
                _ = ln_stream(c, em, lambda ch: r_ap[ch * 128:(ch + 1) * 128, :],
                              xn_bf, b_xn, DC, V, D, RMS_EPS, lnsc, rms=True,
                              fold=(fold_fns,
                                    lambda ch: r_ap[ch * 128:(ch + 1) * 128, :],
                                    r_bufs, None))
                for dc in range(DC):
                    em.op("ve", lambda e, dc=dc: e.tensor_scalar(
                        xn_bf[:, dc * V:(dc + 1) * V], xn_bf[:, dc * V:(dc + 1) * V],
                        lnw_sb[:, dc:dc + 1], None, ALU.mult),
                        reads=[b_lnw], writes=[b_xn])
                em.fence()

        if True:
            kvsc = ExitStack()
            with kvsc:
                wkst = c.sbuf((128, 2048), F32, scope=kvsc, side="right")
                b_wkst = Buf("wkst")
                wkbf2 = [c.sbuf((128, 2048), BF16, scope=kvsc, side="right")
                         for _ in range(2)]
                b_wkbf2 = [Buf(), Buf()]
                rtmp = [c.sbuf((64, V), F32, scope=kvsc, side="right") for _ in range(2)]
                b_rt = [Buf(), Buf()]
                k2t = c.sbuf((128, V), BF16, scope=kvsc, side="right")
                b_k2t = Buf("k2t3")
                kps = c.psum((128, V), scope=kvsc)
                b_kps = Buf("kps")
                k2_ps = c.psum((1, V), scope=kvsc)
                b_k2 = Buf("k2ps3")
                def _ldwk(kh):
                    wkbf, b_wkbf = wkbf2[kh % 2], b_wkbf2[kh % 2]
                    em.dma("sy" if kh % 2 == 0 else "se",
                           "wkA" if kh % 2 == 0 else "wkB", wkst[:, :],
                           t["wk_t"][kh], writes=[b_wkst])
                    em.op("ve", lambda e, wkbf=wkbf: e.tensor_copy(
                        wkbf[:, :], wkst[:, :]), reads=[b_wkst], writes=[b_wkbf])
                _ldwk(0)
                for kh in range(4):
                    wkbf, b_wkbf = wkbf2[kh % 2], b_wkbf2[kh % 2]
                    if kh < 3:
                        _ldwk(kh + 1)
                    for dc in range(DC):
                        for (o, w) in NG_V:
                            em.op("pe", lambda e, dc=dc, o=o, w=w, wkbf=wkbf: e.matmul(
                                kps[:, o:o + w], wkbf[:, dc * 128:(dc + 1) * 128],
                                xn_bf[:, dc * V + o:dc * V + o + w],
                                start=(dc == 0), stop=(dc == DC - 1)),
                                reads=[b_wkbf, b_xn], writes=[b_kps])
                    ra, bra = rtmp[0], b_rt[0]
                    rb, brb = rtmp[1], b_rt[1]
                    em.op("ve", lambda e, ra=ra: e.tensor_tensor(
                        ra[:, :], kps[0:64, :], cos_sb[:, :], ALU.mult),
                        reads=[b_kps, b_cos], writes=[bra])
                    em.op("ve", lambda e, rb=rb: e.tensor_tensor(
                        rb[:, :], kps[64:128, :], sin_sb[:, :], ALU.mult),
                        reads=[b_kps, b_sin], writes=[brb])
                    em.op("ve", lambda e, kh=kh, ra=ra, rb=rb: e.tensor_tensor(
                        kT_bf[0:64, kh * V:(kh + 1) * V], ra[:, :], rb[:, :],
                        ALU.subtract), reads=[bra, brb], writes=[b_kT])
                    em.op("ve", lambda e, ra=ra: e.tensor_tensor(
                        ra[:, :], kps[0:64, :], sin_sb[:, :], ALU.mult),
                        reads=[b_kps, b_sin], writes=[bra])
                    em.op("ve", lambda e, rb=rb: e.tensor_tensor(
                        rb[:, :], kps[64:128, :], cos_sb[:, :], ALU.mult),
                        reads=[b_kps, b_cos], writes=[brb])
                    em.op("ve", lambda e, kh=kh, ra=ra, rb=rb: e.tensor_tensor(
                        kT_bf[64:128, kh * V:(kh + 1) * V], ra[:, :], rb[:, :],
                        ALU.add), reads=[bra, brb], writes=[b_kT])
                    em.op("se", lambda e, kh=kh: e.activation(
                        k2t[:, :], kT_bf[:, kh * V:(kh + 1) * V], AF.Square),
                        reads=[b_kT], writes=[b_k2t])
                    for (o, w) in NG_V:
                        em.op("pe", lambda e, o=o, w=w: e.matmul(
                            k2_ps[0:1, o:o + w], onescol_bf[:, :], k2t[:, o:o + w],
                            start=True, stop=True), reads=[b_k2t, b_ocb], writes=[b_k2])
                    em.op("ve", lambda e, kh=kh: e.reduce_max(
                        krow[0:1, kh * 16:kh * 16 + 1], k2_ps[0:1, :], axis=AX.X),
                        reads=[b_k2], writes=[b_krow])
                    em.op("se", lambda e, kh=kh: e.activation(
                        krow[0:1, kh * 16 + 1:kh * 16 + 2],
                        krow[0:1, kh * 16:kh * 16 + 1], AF.Sqrt),
                        reads=[b_krow], writes=[b_krow])
                    em.op("ve", lambda e, kh=kh: e.tensor_scalar(
                        krow[0:1, kh * 16 + 2:kh * 16 + 3],
                        krow[0:1, kh * 16 + 1:kh * 16 + 2],
                        -1.0, None, ALU.mult), reads=[b_krow], writes=[b_krow])
                em.fence()
        if True:
            vsc = ExitStack()
            with vsc:
                wvst = c.sbuf((128, 512), F32, scope=vsc, side="right")
                b_wvst = Buf("wvst")
                wvbf = c.sbuf((128, DC * 512), BF16, scope=vsc, side="right")
                b_wvbf = Buf("wvbf")
                for dc in range(DC):
                    load_cast(c, em, t["wv"][dc * 128:(dc + 1) * 128, :],
                              wvbf, b_wvbf, dc * 512, 512, wvst, b_wvst, "wvl")
                vps = [c.psum((128, 512), scope=vsc) for _ in range(2)]
                b_vps = [Buf(), Buf()]
                for tc in range(VT):
                    vp, bvp = vps[tc % 2], b_vps[tc % 2]
                    for dc in range(DC):
                        em.op("pe", lambda e, vp=vp, tc=tc, dc=dc: e.matmul(
                            vp[:, :],
                            xn_bf[:, dc * V + tc * 128:dc * V + (tc + 1) * 128],
                            wvbf[:, dc * 512:(dc + 1) * 512],
                            start=(dc == 0), stop=(dc == DC - 1)),
                            reads=[b_xn, b_wvbf], writes=[bvp])
                    em.op("ve", lambda e, vp=vp, tc=tc: e.tensor_copy(
                        v_bf[:, tc * 512:(tc + 1) * 512], vp[:, :]),
                        reads=[bvp], writes=[b_v])
                em.fence()

        if DEBUG_P3:
            if True:
                for nm, sbuf_t, bufo, wdt in [("dbg_xn", xn_bf, b_xn, DC * V),
                                              ("dbg_kT", kT_bf, b_kT, 4 * V),
                                              ("dbg_v", v_bf, b_v, VT * 512)]:
                    ap = c.dout(nm, (128, wdt), BF16)
                    em.dma("sy", "dbg", ap[:, :], sbuf_t[:, :], reads=[bufo])
                em.fence()

        def _hgrp3(hg):
            s8 = ExitStack()
            with s8:
                wqst = c.sbuf((128, 2048), F32, scope=s8, side="right")
                b_wqst = Buf("wqst")
                wqbf2 = [c.sbuf((128, 2048), BF16, scope=s8, side="right")
                         for _ in range(2)]
                b_wqbf2 = [Buf(), Buf()]
                q_bf = c.sbuf((128, 1024), BF16, scope=s8, side="right")
                b_q = Buf("qbf")
                rtq = [c.sbuf((64, 1024), F32, scope=s8, side="right") for _ in range(2)]
                b_rtq = [Buf(), Buf()]
                sqt = c.sbuf((128, 1024), BF16, scope=s8, side="right")
                b_sqt = Buf("sqt")
                qnr = c.sbuf((1, 1024), F32, scope=s8, side="right")
                b_qnr = Buf("qnr")
                qrow = c.sbuf((1, 1024), F32, scope=s8, side="right")
                b_qrow = Buf("qrow3")
                negm = c.sbuf((1, 1024), BF16, scope=s8, side="right")
                b_negm = Buf("negm3")
                pt_bf = c.sbuf((128, 8 * 512), BF16, scope=s8, side="right")
                b_pt = Buf("pt3")
                rb_sb = c.sbuf((128, 512), F32, scope=s8, side="right")
                b_rb = Buf("rb3")
                rrow = c.sbuf((1, 512), F32, scope=s8, side="right")
                b_rrow = Buf("rrow3")

                if True:
                    atsc = ExitStack()
                    with atsc:
                        qps = c.psum((128, 512), scope=atsc)
                        b_qps = Buf("qps")
                        q2_ps = c.psum((1, 512), scope=atsc)
                        b_q2 = Buf("q2ps3")
                        stp = [c.psum((128, 2 * 512), scope=atsc) for _ in range(2)]
                        b_stp = [Buf("stA3"), Buf("stB3")]
                        ot_ps = c.psum((128, 512), scope=atsc)
                        b_otp = Buf("otps3")
                        row_ps = c.psum((128, 512), scope=atsc)
                        b_row = Buf("rowps3")
                        def _ldwq(hh):
                            h = hg * 4 + hh
                            wqbf, b_wqbf = wqbf2[hh % 2], b_wqbf2[hh % 2]
                            em.dma("sy" if hh % 2 == 0 else "se",
                                   "wqA" if hh % 2 == 0 else "wqB", wqst[:, :],
                                   t["wq_t"][h], writes=[b_wqst])
                            em.op("ve", lambda e, wqbf=wqbf: e.tensor_copy(
                                wqbf[:, :], wqst[:, :]), reads=[b_wqst], writes=[b_wqbf])
                        _ldwq(0)
                        for hh in range(4):
                            h = hg * 4 + hh
                            kh = hg
                            wqbf, b_wqbf = wqbf2[hh % 2], b_wqbf2[hh % 2]
                            if hh < 3:
                                _ldwq(hh + 1)
                            for o in (0, 512):
                                for dc in range(DC):
                                    em.op("pe", lambda e, dc=dc, o=o, wqbf=wqbf: e.matmul(
                                        qps[:, :],
                                        wqbf[:, dc * 128:(dc + 1) * 128],
                                        xn_bf[:, dc * V + 512 + o:dc * V + 1024 + o],
                                        start=(dc == 0), stop=(dc == DC - 1)),
                                        reads=[b_wqbf, b_xn], writes=[b_qps])
                                ra, bra = rtq[0], b_rtq[0]
                                rb2, brb = rtq[1], b_rtq[1]
                                em.op("ve", lambda e, ra=ra, o=o: e.tensor_tensor(
                                    ra[:, 0:512], qps[0:64, :],
                                    cos_sb[:, 512 + o:1024 + o], ALU.mult),
                                    reads=[b_qps, b_cos], writes=[bra])
                                em.op("ve", lambda e, rb2=rb2, o=o: e.tensor_tensor(
                                    rb2[:, 0:512], qps[64:128, :],
                                    sin_sb[:, 512 + o:1024 + o], ALU.mult),
                                    reads=[b_qps, b_sin], writes=[brb])
                                em.op("ve", lambda e, ra=ra, rb2=rb2, o=o: e.tensor_tensor(
                                    q_bf[0:64, o:o + 512], ra[:, 0:512], rb2[:, 0:512],
                                    ALU.subtract), reads=[bra, brb], writes=[b_q])
                                em.op("ve", lambda e, ra=ra, o=o: e.tensor_tensor(
                                    ra[:, 0:512], qps[0:64, :],
                                    sin_sb[:, 512 + o:1024 + o], ALU.mult),
                                    reads=[b_qps, b_sin], writes=[bra])
                                em.op("ve", lambda e, rb2=rb2, o=o: e.tensor_tensor(
                                    rb2[:, 0:512], qps[64:128, :],
                                    cos_sb[:, 512 + o:1024 + o], ALU.mult),
                                    reads=[b_qps, b_cos], writes=[brb])
                                em.op("ve", lambda e, ra=ra, rb2=rb2, o=o: e.tensor_tensor(
                                    q_bf[64:128, o:o + 512], ra[:, 0:512], rb2[:, 0:512],
                                    ALU.add), reads=[bra, brb], writes=[b_q])
                            em.op("se", lambda e: e.activation(
                                sqt[:, :], q_bf[:, :], AF.Square),
                                reads=[b_q], writes=[b_sqt])
                            for o in (0, 512):
                                em.op("pe", lambda e, o=o: e.matmul(
                                    q2_ps[0:1, :], onescol_bf[:, :], sqt[:, o:o + 512],
                                    start=True, stop=True), reads=[b_sqt, b_ocb], writes=[b_q2])
                                em.op("se", lambda e, o=o: e.activation(
                                    qnr[0:1, o:o + 512], q2_ps[0:1, :], AF.Sqrt),
                                    reads=[b_q2], writes=[b_qnr])
                            em.op("ve", lambda e, kh=kh: e.tensor_scalar(
                                negm[0:1, :], qnr[0:1, :],
                                krow[0:1, kh * 16 + 2:kh * 16 + 3], None, ALU.mult),
                                reads=[b_qnr, b_krow], writes=[b_negm])
                            if DEBUG_P3 and hg == 1 and hh == 1:
                                ap = c.dout("dbg_q5", (128, 1024), BF16)
                                em.dma("sy", "dbg", ap[:, :], q_bf[:, :], reads=[b_q])
                            for qg in range(2):
                                qo = qg * 512
                                for bi in range(4):
                                    kcs = [bi * 2, bi * 2 + 1]
                                    p, bp = stp[bi % 2], b_stp[bi % 2]
                                    for ii, kc in enumerate(kcs):
                                        kcg = qg * 4 + kc
                                        em.op("pe", lambda e, p=p, ii=ii, kcg=kcg, kh=kh, qo=qo: e.matmul(
                                            p[:, ii * 512:(ii + 1) * 512],
                                            kT_bf[:, kh * V + kcg * 128:kh * V + (kcg + 1) * 128],
                                            q_bf[:, qo:qo + 512],
                                            start=True, stop=False),
                                            reads=[b_kT, b_q], writes=[bp])
                                        em.op("pe", lambda e, p=p, ii=ii, qo=qo: e.matmul(
                                            p[:, ii * 512:(ii + 1) * 512], onesrow_bf[:, :],
                                            negm[0:1, qo:qo + 512],
                                            start=False, stop=False),
                                            reads=[b_negm, b_orb], writes=[bp])
                                        em.op("pe", lambda e, p=p, ii=ii, kc=kc: e.matmul(
                                            p[:, ii * 512:(ii + 1) * 512], ident[:, :],
                                            l1m_sb[:, kc * 512:(kc + 1) * 512],
                                            start=False, stop=True),
                                            reads=[b_l1m, b_ident], writes=[bp])
                                    em.op("se", lambda e, p=p, kcs=kcs: e.activation(
                                        pt_bf[:, kcs[0] * 512:(kcs[0] + 2) * 512],
                                        p[:, 0:1024], AF.Exp, scale=INVSD),
                                        reads=[bp], writes=[b_pt])
                                    for kc in kcs:
                                        em.op("ve", lambda e, kc=kc, qg=qg: e.tensor_scalar(
                                            pt_bf[:, kc * 512:(kc + 1) * 512],
                                            pt_bf[:, kc * 512:(kc + 1) * 512],
                                            mcol_f[:, qg * 8 + kc:qg * 8 + kc + 1],
                                            None, ALU.mult),
                                            reads=[b_mcf], writes=[b_pt])
                                for kc in range(8):
                                    kcg = qg * 4 + kc
                                    em.op("pe", lambda e, kc=kc, kcg=kcg, kh=kh: e.matmul(
                                        ot_ps[:, :],
                                        v_bf[:, kcg * 512 + kh * 128:kcg * 512 + (kh + 1) * 128],
                                        pt_bf[:, kc * 512:(kc + 1) * 512],
                                        start=(kc == 0), stop=(kc == 7)),
                                        reads=[b_v, b_pt], writes=[b_otp])
                                    em.op("pe", lambda e, kc=kc, qg=qg: e.matmul(
                                        row_ps[0:1, :],
                                        mcol_b[:, qg * 8 + kc:qg * 8 + kc + 1],
                                        pt_bf[:, kc * 512:(kc + 1) * 512],
                                        start=(kc == 0), stop=(kc == 7)),
                                        reads=[b_mcb, b_pt], writes=[b_row])
                                em.op("ve", lambda e: e.reciprocal(rrow[0:1, :], row_ps[0:1, :]),
                                      reads=[b_row], writes=[b_rrow])
                                em.op("pe", lambda e: e.matmul(
                                    row_ps[:, :], onesrow_f32[:, :], rrow[0:1, :],
                                    start=True, stop=True),
                                    reads=[b_rrow, b_orf], writes=[b_row])
                                em.op("ve", lambda e: e.tensor_copy(rb_sb[:, :], row_ps[:, :]),
                                      reads=[b_row], writes=[b_rb])
                                em.op("ve", lambda e, h=h, qg=qg: e.tensor_tensor(
                                    ot3_all[:, h * 1024 + qg * 512:h * 1024 + (qg + 1) * 512],
                                    ot_ps[:, :], rb_sb[:, :], ALU.mult),
                                    reads=[b_otp, b_rb], writes=[b_otall3])
                        em.fence()

                em.fence()

        for hg in range(4):
            _hgrp3(hg)

        wsc = ExitStack()
        with wsc:
            wst = [c.sbuf((128, 2048), F32, scope=wsc, side="right") for _ in range(2)]
            b_wst = [Buf(), Buf()]
            wbf = [c.sbuf((128, 2048), BF16, scope=wsc, side="right") for _ in range(2)]
            b_wbf = [Buf(), Buf()]
            ystg = [c.sbuf((128, 512), F32, scope=wsc, side="right") for _ in range(4)]
            b_ystg = [Buf(), Buf(), Buf(), Buf()]
            wops = ExitStack()
            with wops:
                yp = [c.psum((128, 512), scope=wops) for _ in range(2)]
                b_yp = [Buf(), Buf()]
                k = 0
                def _ldwo3(j):
                    ws, bws = wst[j % 2], b_wst[j % 2]
                    wb, bwb = wbf[j % 2], b_wbf[j % 2]
                    jeng = "sy" if j % 2 == 0 else "se"
                    em.dma(jeng, "woA" if j % 2 == 0 else "woB", ws[:, :],
                           t["wo1_t"][j], writes=[bws])
                    em.op("ve", lambda e, ws=ws, wb=wb: e.tensor_copy(
                        wb[:, :], ws[:, :]), reads=[bws], writes=[bwb])
                _ldwo3(0)
                for j in range(16):
                    wb, bwb = wbf[j % 2], b_wbf[j % 2]
                    jeng = "sy" if j % 2 == 0 else "se"
                    if j < 15:
                        _ldwo3(j + 1)
                    for (o, w) in [(0, 512), (512, 512)]:
                        p, bp = yp[k % 2], b_yp[k % 2]
                        ys, bys = ystg[k % 4], b_ystg[k % 4]
                        k += 1
                        for ic in range(16):
                            em.op("pe", lambda e, p=p, o=o, w=w, ic=ic, wb=wb: e.matmul(
                                p[:, 0:w],
                                wb[:, ic * 128:(ic + 1) * 128],
                                ot3_all[:, ic * 1024 + o:ic * 1024 + o + w],
                                start=(ic == 0), stop=(ic == 15)),
                                reads=[bwb, b_otall3], writes=[bp])
                        em.op("ve", lambda e, p=p, ys=ys, w=w: e.tensor_copy(
                            ys[:, 0:w], p[:, 0:w]), reads=[bp], writes=[bys])
                        em.dma(jeng, "yst" + "ABCD"[(k - 1) % 4],
                               y3_ap[j * 128:(j + 1) * 128, o:o + w],
                               ys[:, 0:w], reads=[bys])
                em.fence()

def emit_p4(c, em, r_ap, r_bufs, fold_fns, y_ap):
    t = c.t
    sb = c.sb
    nc = c.nc
    onesrow_f32 = sb["onesrow_f32"]
    b_orf = sb["b_onesrow_f32"]
    ident_f = sb["ident_f32"]
    b_idf = sb["b_ident_f32"]
    W = 1024

    p4 = ExitStack()
    with p4:
        x3_bf = c.sbuf((128, DC * W), BF16, scope=p4, side="right")
        b_x3 = Buf("x3")
        yacc = c.sbuf((128, DC * W), F32, scope=p4, side="right")
        b_yacc = Buf("yacc")
        cb_bf = c.sbuf((128, 8 * W), BF16, scope=p4, side="right")
        b_cb = Buf("cb4")

        rsc = ExitStack()
        rw_f = c.sbuf((128, DC * 8), F32, scope=rsc, side="right")
        b_rwf = Buf("rwf")
        b_raw = Buf("rawps4")
        sgcol = c.sbuf((8, 1), F32, scope=rsc, side="right")
        b_sgc = Buf("sgc4")
        onescol = sb["onescol_f32"]
        b_ones = sb["b_onescol_f32"]
        if True:
            for dc in range(DC):
                em.dma("sy", "rwl", rw_f[:, dc * 8:(dc + 1) * 8],
                       t["rwT"][dc * 128:(dc + 1) * 128, :], writes=[b_rwf])
            em.seal([b_rwf], "rwl")
            sgsc = ExitStack()
            sg_ps = c.psum((1, 8), scope=sgsc)
            b_sgp = Buf("sgp4")
            for dc in range(DC):
                em.op("pe", lambda e, dc=dc: e.matmul(
                    sg_ps[0:1, :], onescol[:, :], rw_f[:, dc * 8:(dc + 1) * 8],
                    start=(dc == 0), stop=(dc == DC - 1)),
                    reads=[b_rwf, b_ones], writes=[b_sgp])
            sgrow = c.sbuf((1, 8), F32, scope=rsc, side="right")
            b_sgrow = Buf("sgrow4")
            em.op("ve", lambda e: e.tensor_copy(sgrow[0:1, :], sg_ps[0:1, :]),
                  reads=[b_sgp], writes=[b_sgrow])
            sgc_ps = c.psum((8, 1), scope=sgsc)
            b_sgcp = Buf("sgcp4")
            em.op("pe", lambda e: e.matmul(
                sgc_ps[:, :], sgrow[0:1, :], onesrow_f32[0:1, 0:1],
                start=True, stop=True), reads=[b_sgrow, b_orf], writes=[b_sgcp])
            em.op("ve", lambda e: e.tensor_copy(sgcol[:, :], sgc_ps[:, :]),
                  reads=[b_sgcp], writes=[b_sgc])
            em.fence()
            sgsc.close()

        rawT_holder = []

        def p4_extra(ch, ss, bs):
            rawT_ps = rawT_holder[0]
            for (o, w) in [(0, 512), (512, 512)]:
                em.op("pe", lambda e, ch=ch, o=o, w=w, ss=ss: e.matmul(
                    rawT_ps[:, o:o + w], rw_f[:, ch * 8:(ch + 1) * 8],
                    ss[:, o:o + w],
                    start=(ch == 0), stop=(ch == DC - 1)),
                    reads=[bs, b_rwf], writes=[b_raw])

        lsb = c.sbuf((128, 64), F32, scope=rsc, side="right")
        b_lsb = Buf("lsb")
        if True:
            lnsc0 = ExitStack()
            with lnsc0:
                rs2, b_rs2, mrs2, b_mrs2, _, _ = ln_stream(
                    c, em, lambda ch: r_ap[ch * 128:(ch + 1) * 128, 512:V],
                    x3_bf, b_x3, DC, W, D, LN_EPS, rsc,
                    fold=(fold_fns,
                          lambda ch: r_ap[ch * 128:(ch + 1) * 128, 512:V],
                          r_bufs, None))
                em.fence()
        if True:
            lnsc = ExitStack()
            with lnsc:
                rstg = [c.sbuf((128, W), F32, scope=lnsc, side="right") for _ in range(2)]
                b_rstg = [Buf(), Buf()]
                rawT_ps = c.psum((8, W), scope=lnsc)
                rawT_holder.append(rawT_ps)
                for ch in range(DC):
                    ss, bs = rstg[ch % 2], b_rstg[ch % 2]
                    em.dma("sy" if ch % 2 == 0 else "se",
                           "lnA" if ch % 2 == 0 else "lnB", ss[:, :],
                           r_ap[ch * 128:(ch + 1) * 128, 512:V], writes=[bs],
                           reads=[r_bufs[ch]])
                    p4_extra(ch, ss, bs)
                lT = c.sbuf((8, W), F32, scope=lnsc, side="right")
                b_lT = Buf("lT4")
                em.op("ve", lambda e: e.tensor_tensor(
                    lT[:, :], rawT_ps[:, :], rs2[0:8, :], ALU.mult),
                    reads=[b_raw, b_rs2], writes=[b_lT])
                tmpT = c.sbuf((8, W), F32, scope=lnsc, side="right")
                b_tmpT = Buf("tmpT4")
                em.op("ve", lambda e: e.tensor_scalar(
                    tmpT[:, :], mrs2[0:8, :], sgcol[:, :], None, ALU.mult),
                    reads=[b_mrs2, b_sgc], writes=[b_tmpT])
                em.op("ve", lambda e: e.tensor_tensor(
                    lT[:, :], lT[:, :], tmpT[:, :], ALU.subtract),
                    reads=[b_tmpT], writes=[b_lT])
                tk_ps = c.psum((128, 64), scope=lnsc)
                b_tk = Buf("tkps4")
                for tc in range(8):
                    em.op("pe", lambda e, tc=tc: e.matmul(
                        tk_ps[:, tc * 8:(tc + 1) * 8],
                        lT[:, tc * 128:(tc + 1) * 128],
                        ident_f[0:8, 0:8], start=True, stop=True),
                        reads=[b_lT, b_idf], writes=[b_tk])
                em.op("ve", lambda e: e.tensor_copy(lsb[:, :], tk_ps[:, :]),
                      reads=[b_tk], writes=[b_lsb])
                em.fence()

        if True:
            with rsc:
                ws = c.sbuf((128, 48), F32, scope=rsc, side="right")
                b_ws = Buf("ws4")
                wm = c.sbuf((128, 8), F32, scope=rsc, side="right")
                b_wm = Buf("wm4")
                csb = c.sbuf((128, 8 * 8), F32, scope=rsc, side="right")
                b_csb = Buf("csb4")
                for tc in range(8):
                    L = lambda: lsb[:, tc * 8:(tc + 1) * 8]
                    em.op("ve", lambda e, tc=tc: e.reduce_max(
                        ws[:, tc * 6:tc * 6 + 1], lsb[:, tc * 8:(tc + 1) * 8],
                        axis=AX.X, negate=True), reads=[b_lsb], writes=[b_ws])
                    em.op("ve", lambda e, tc=tc: e.tensor_scalar(
                        ws[:, tc * 6 + 1:tc * 6 + 2], ws[:, tc * 6:tc * 6 + 1],
                        -1.0, None, ALU.mult), reads=[b_ws], writes=[b_ws])
                    em.op("ve", lambda e, tc=tc: e.tensor_scalar(
                        wm[:, :], lsb[:, tc * 8:(tc + 1) * 8],
                        ws[:, tc * 6 + 1:tc * 6 + 2], None, ALU.is_ge),
                        reads=[b_lsb, b_ws], writes=[b_wm])
                    em.op("ve", lambda e, tc=tc: e.scalar_tensor_tensor(
                        wm[:, :], wm[:, :], NEG, lsb[:, tc * 8:(tc + 1) * 8],
                        ALU.mult, ALU.add), reads=[b_wm, b_lsb], writes=[b_wm])
                    em.op("ve", lambda e, tc=tc: e.reduce_max(
                        ws[:, tc * 6 + 2:tc * 6 + 3], wm[:, :], axis=AX.X),
                        reads=[b_wm], writes=[b_ws])
                    em.op("ve", lambda e, tc=tc: e.tensor_scalar(
                        wm[:, :], lsb[:, tc * 8:(tc + 1) * 8],
                        ws[:, tc * 6 + 2:tc * 6 + 3], None, ALU.is_ge),
                        reads=[b_lsb, b_ws], writes=[b_wm])
                    em.op("se", lambda e, tc=tc: e.activation(
                        csb[:, tc * 8:(tc + 1) * 8], lsb[:, tc * 8:(tc + 1) * 8],
                        AF.Exp, bias=ws[:, tc * 6:tc * 6 + 1]),
                        reads=[b_lsb, b_ws], writes=[b_csb])
                    em.op("ve", lambda e, tc=tc: e.tensor_tensor(
                        csb[:, tc * 8:(tc + 1) * 8], csb[:, tc * 8:(tc + 1) * 8],
                        wm[:, :], ALU.mult), reads=[b_wm], writes=[b_csb])
                    em.op("ve", lambda e, tc=tc: e.reduce_sum(
                        ws[:, tc * 6 + 3:tc * 6 + 4], csb[:, tc * 8:(tc + 1) * 8],
                        axis=AX.X), reads=[b_csb], writes=[b_ws])
                    em.op("ve", lambda e, tc=tc: e.reciprocal(
                        ws[:, tc * 6 + 4:tc * 6 + 5], ws[:, tc * 6 + 3:tc * 6 + 4]),
                        reads=[b_ws], writes=[b_ws])
                    em.op("ve", lambda e, tc=tc: e.tensor_scalar(
                        csb[:, tc * 8:(tc + 1) * 8], csb[:, tc * 8:(tc + 1) * 8],
                        ws[:, tc * 6 + 4:tc * 6 + 5], None, ALU.mult),
                        reads=[b_ws], writes=[b_csb])
                crow_ps = c.psum((1, W), scope=rsc)
                b_crp = Buf("crp4")
                crow = c.sbuf((1, W), F32, scope=rsc, side="right")
                b_crow = Buf("crow4")
                cb_ps = c.psum((128, 512), scope=rsc)
                b_cbp = Buf("cbp4")
                for e_ in range(8):
                    for tc in range(8):
                        em.op("pe", lambda e, e_=e_, tc=tc: e.transpose(
                            crow_ps[0:1, tc * 128:(tc + 1) * 128],
                            csb[:, tc * 8 + e_:tc * 8 + e_ + 1], ident_f[:, :]),
                            reads=[b_csb, b_idf], writes=[b_crp])
                    em.op("ve", lambda e: e.tensor_copy(crow[0:1, :], crow_ps[0:1, :]),
                          reads=[b_crp], writes=[b_crow])
                    for (o, w) in [(0, 512), (512, 512)]:
                        em.op("pe", lambda e, o=o, w=w: e.matmul(
                            cb_ps[:, 0:w], onesrow_f32[:, :], crow[0:1, o:o + w],
                            start=True, stop=True), reads=[b_crow, b_orf], writes=[b_cbp])
                        em.op("ve", lambda e, e_=e_, o=o, w=w: e.tensor_copy(
                            cb_bf[:, e_ * W + o:e_ * W + o + w], cb_ps[:, 0:w]),
                            reads=[b_cbp], writes=[b_cb])
                em.fence()

        def _p4_e(e_):
            if True:
                esc = ExitStack()
                with esc:
                    wstg = c.sbuf((128, 2048), F32, scope=esc, side="right")
                    b_wstg = Buf()
                    wstu = c.sbuf((128, 2048), F32, scope=esc, side="right")
                    b_wstu = Buf()
                    wbfg = [c.sbuf((128, 2048), BF16, scope=esc, side="right")
                            for _ in range(2)]
                    b_wbfg = [Buf(), Buf()]
                    wbfu = [c.sbuf((128, 2048), BF16, scope=esc, side="right")
                            for _ in range(2)]
                    b_wbfu = [Buf(), Buf()]
                    h_bf = c.sbuf((128, 11 * W), BF16, scope=esc, side="right")
                    b_h = Buf("h4")
                    sg = c.sbuf((128, W), BF16, scope=esc, side="right")
                    b_sg = Buf("sg4")
                    htmp = c.sbuf((128, W), BF16, scope=esc, side="right")
                    b_htmp = Buf("ht4")
                    gps = c.psum((128, W), scope=esc)
                    b_gps = Buf("gps4")
                    ups = c.psum((128, W), scope=esc)
                    b_ups = Buf("ups4")
                    def _ld1(fc):
                        for which in (fc, fc + 11):
                            gate = which < 11
                            wb, bwb = (wbfg if gate else wbfu)[fc % 2], \
                                      (b_wbfg if gate else b_wbfu)[fc % 2]
                            wstx, b_wstx = (wstg, b_wstg) if gate else (wstu, b_wstu)
                            em.dma("sy" if gate else "se",
                                   ("w1A" if fc % 2 == 0 else "w1B") if gate
                                   else ("w1C" if fc % 2 == 0 else "w1D"),
                                   wstx[:, :], t["w1_t"][e_, which], writes=[b_wstx])
                            em.op("ve", lambda e, wb=wb, wstx=wstx: e.tensor_copy(
                                wb[:, :], wstx[:, :]), reads=[b_wstx], writes=[bwb])
                    _ld1(0)
                    for fc in range(11):
                        if fc < 10:
                            _ld1(fc + 1)
                        for which, ps, bps in [(fc, gps, b_gps), (fc + 11, ups, b_ups)]:
                            gate = which < 11
                            wb, bwb = (wbfg if gate else wbfu)[fc % 2], \
                                      (b_wbfg if gate else b_wbfu)[fc % 2]
                            for dc in range(DC):
                                for (o, w) in [(0, 512), (512, 512)]:
                                    em.op("pe", lambda e, ps=ps, dc=dc, o=o, w=w, wb=wb: e.matmul(
                                        ps[:, o:o + w],
                                        wb[:, dc * 128:(dc + 1) * 128],
                                        x3_bf[:, dc * W + o:dc * W + o + w],
                                        start=(dc == 0), stop=(dc == DC - 1)),
                                        reads=[bwb, b_x3], writes=[bps])
                        em.op("se", lambda e: e.activation(
                            sg[:, :], gps[:, :], AF.Silu), reads=[b_gps], writes=[b_sg])
                        em.op("ve", lambda e: e.tensor_tensor(
                            htmp[:, :], sg[:, :], ups[:, :], ALU.mult),
                            reads=[b_sg, b_ups], writes=[b_htmp])
                        em.op("ve", lambda e, fc=fc, e_=e_: e.tensor_tensor(
                            h_bf[:, fc * W:(fc + 1) * W], htmp[:, :],
                            cb_bf[:, e_ * W:(e_ + 1) * W], ALU.mult),
                            reads=[b_htmp, b_cb], writes=[b_h])
                    w2stl = [c.sbuf((128, 1408), F32, scope=esc, side="right")
                             for _ in range(2)]
                    b_w2stl = [Buf(), Buf()]
                    w2bf = [c.sbuf((128, 1408), BF16, scope=esc, side="right")
                            for _ in range(2)]
                    b_w2bf = [Buf(), Buf()]
                    y2ps = [c.psum((128, W), scope=esc) for _ in range(2)]
                    b_y2ps = [Buf(), Buf()]
                    def _ld2(dc):
                        wb, bwb = w2bf[dc % 2], b_w2bf[dc % 2]
                        w2st, b_w2st = w2stl[dc % 2], b_w2stl[dc % 2]
                        em.dma("sy" if dc % 2 == 0 else "se",
                               "w2A" if dc % 2 == 0 else "w2B",
                               w2st[:, :], t["w2_t"][e_, dc], writes=[b_w2st])
                        em.op("ve", lambda e, wb=wb, w2st=w2st: e.tensor_copy(
                            wb[:, :], w2st[:, :]), reads=[b_w2st], writes=[bwb])
                    _ld2(0)
                    for dc in range(DC):
                        wb, bwb = w2bf[dc % 2], b_w2bf[dc % 2]
                        if dc < DC - 1:
                            _ld2(dc + 1)
                        yp, byp = y2ps[dc % 2], b_y2ps[dc % 2]
                        for fc in range(11):
                            for (o, w) in [(0, 512), (512, 512)]:
                                em.op("pe", lambda e, yp=yp, fc=fc, o=o, w=w, wb=wb: e.matmul(
                                    yp[:, o:o + w],
                                    wb[:, fc * 128:(fc + 1) * 128],
                                    h_bf[:, fc * W + o:fc * W + o + w],
                                    start=(fc == 0), stop=(fc == 10)),
                                    reads=[bwb, b_h], writes=[byp])
                        if e_ == 0:
                            em.op("ve", lambda e, yp=yp, dc=dc: e.tensor_copy(
                                yacc[:, dc * W:(dc + 1) * W], yp[:, :]),
                                reads=[byp], writes=[b_yacc])
                        else:
                            em.op("ve", lambda e, yp=yp, dc=dc: e.tensor_tensor(
                                yacc[:, dc * W:(dc + 1) * W],
                                yacc[:, dc * W:(dc + 1) * W], yp[:, :], ALU.add),
                                reads=[byp], writes=[b_yacc])
                    em.fence()

        for e_ in range(E1):
            _p4_e(e_)
        if True:
            osc = ExitStack()
            with osc:
                rst = [c.sbuf((128, W), F32, scope=osc, side="right") for _ in range(2)]
                b_rst = [Buf(), Buf()]
                outsb = [c.sbuf((128, D), F32, scope=osc, side="right") for _ in range(2)]
                b_out = [Buf(), Buf()]
                tps = [c.psum((128, 128), scope=osc) for _ in range(2)]
                b_tps = [Buf(), Buf()]
                for dc in range(DC):
                    rs, brs = rst[dc % 2], b_rst[dc % 2]
                    em.dma("sy", "orA" if dc % 2 == 0 else "orB", rs[:, :],
                           r_ap[dc * 128:(dc + 1) * 128, 512:V], writes=[brs],
                           reads=[r_bufs[dc]])
                    em.op("ve", lambda e, rs=rs, dc=dc: e.tensor_tensor(
                        yacc[:, dc * W:(dc + 1) * W], yacc[:, dc * W:(dc + 1) * W],
                        rs[:, :], ALU.add), reads=[brs], writes=[b_yacc])
                kk = 0
                for tc in range(8):
                    ob, bob = outsb[tc % 2], b_out[tc % 2]
                    for dc in range(DC):
                        tp, btp = tps[kk % 2], b_tps[kk % 2]
                        kk += 1
                        em.op("pe", lambda e, tp=tp, dc=dc, tc=tc: e.transpose(
                            tp[:, :], yacc[:, dc * W + tc * 128:dc * W + (tc + 1) * 128],
                            ident_f[:, :]), reads=[b_yacc, b_idf], writes=[btp])
                        em.op("ve", lambda e, tp=tp, ob=ob, dc=dc: e.tensor_copy(
                            ob[:, dc * 128:(dc + 1) * 128], tp[:, :]),
                            reads=[btp], writes=[bob])
                    em.dma("sy", "ostA" if tc % 2 == 0 else "ostB",
                           y_ap[tc * 128:(tc + 1) * 128, :], ob[:, :], reads=[bob])
                em.fence()



P2_INPUTS = ["gate_wT", "wg_t", "wu_t", "wd_t"]
P3_INPUTS = ["lnw", "wq_t", "wk_t", "wv", "wo1_t", "ropec", "ropes", "l1_mask",
             "l1_mcol", "l1_mcol_bf"]
P4_INPUTS = ["rwT", "w1_t", "w2_t"]


def declare_inputs_p2(c):
    c.din("gate_wT", (D, 2))
    c.din("wg_t", (E0, 64, 128, 2048))
    c.din("wu_t", (E0, 64, 128, 2048))
    c.din("wd_t", (E0, 16, 128, 8192))


def declare_inputs_p3(c):
    c.din("lnw", (D,))
    c.din("wq_t", (H, 128, 2048))
    c.din("wk_t", (4, 128, 2048))
    c.din("wv", (D, 512))
    c.din("wo1_t", (16, 128, D))
    c.din("ropec", (64, V))
    c.din("ropes", (64, V))
    c.din("l1_mask", (8, 128, 512), BF16)
    c.din("l1_mcol", (2, 8, 128))
    c.din("l1_mcol_bf", (2, 8, 128), BF16)


def declare_inputs_p4(c):
    c.din("rwT", (D, 8))
    c.din("w1_t", (E1, 22, 128, 2048))
    c.din("w2_t", (E1, 16, 128, 1408))


def build_p2():
    c = Ctx()
    declare_consts(c)
    declare_inputs_p2(c)
    r_in = c.din("r_in", (D, V))
    r_out = c.dout("r_out", (D, V))
    with c.es:
        em = Em(c)
        emit_const_loads(c, em)
        with c.nc.Block() as blk:
            for dc in range(DC):
                em.dma("sy", "rcp", r_out[dc * 128:(dc + 1) * 128, :],
                       r_in[dc * 128:(dc + 1) * 128, :])
            em.flush(blk)
        with c.nc.Block() as blk:
            emit_p2(c, em, r_out)
            em.flush(blk)
        finish(c, em)
        c.nc.finalize()
    return c


def build_p3():
    c = Ctx()
    declare_consts(c)
    declare_inputs_p3(c)
    r_in = c.din("r_in", (D, V))
    r_out = c.dout("r_out", (D, V))
    qn_dram = c.dint("qn_scratch", (H, V))
    with c.es:
        em = Em(c)
        emit_const_loads(c, em)
        with c.nc.Block() as blk:
            for dc in range(DC):
                em.dma("sy", "rcp", r_out[dc * 128:(dc + 1) * 128, :],
                       r_in[dc * 128:(dc + 1) * 128, :])
            em.flush(blk)
        with c.nc.Block() as blk:
            emit_p3(c, em, r_out, qn_dram)
            em.flush(blk)
        finish(c, em)
        c.nc.finalize()
    return c


def build_p4():
    c = Ctx()
    declare_consts(c)
    declare_inputs_p4(c)
    r_in = c.din("r_in", (D, V))
    y_out = c.dout("y_out", (1024, D))
    with c.es:
        em = Em(c)
        emit_const_loads(c, em)
        with c.nc.Block() as blk:
            emit_p4(c, em, r_in, y_out)
            em.flush(blk)
        finish(c, em)
        c.nc.finalize()
    return c


def build_fused():
    c = Ctx()
    declare_consts(c)
    c.din("xqT", (D, V))
    c.din("xkvT", (D, S0))
    c.din("w_dq", (D, QP))
    c.din("w_uq", (QP, D))
    c.din("w_dkv", (D, QP))
    c.din("w_ukv", (QP, 2 * D))
    c.din("wo_t", (16, 128, D))
    c.din("qgb", (2, QP))
    c.din("kvgb", (2, QP))
    c.din("l0_bnd", (3, 8, 128, 512), BF16)
    c.din("l0_mcol", (3, 16, 128))
    c.din("l0_mcol_bf", (3, 16, 128), BF16)
    declare_inputs_p2(c)
    declare_inputs_p3(c)
    declare_inputs_p4(c)
    r_int = c.dint("r_int", (D, V))
    qn_dram = c.dint("qn_scratch", (H, V))
    y1a = c.dint("y1a", (D, V))
    y1b = c.dint("y1b", (D, V))
    y2a = c.dint("y2a", (D, V))
    y2b = c.dint("y2b", (D, V))
    y3 = c.dint("y3", (D, 1024))
    y_out = c.dout("y_out", (1024, D))
    r_bufs = [Buf(f"r{i}") for i in range(DC)]
    with c.es:
        em = Em(c)
        emit_const_loads(c, em)
        with c.nc.Block() as blk:
            for dc in range(DC):
                em.dma("sy", "rcp", r_int[dc * 128:(dc + 1) * 128, :],
                       c.t["xqT"][dc * 128:(dc + 1) * 128, :],
                       writes=[r_bufs[dc]])
            em.seal(r_bufs, "rcp")
            em.flush(blk)
        f12 = [lambda ch: y1a[ch * 128:(ch + 1) * 128, :],
               lambda ch: y1b[ch * 128:(ch + 1) * 128, :]]
        f23 = [lambda ch: y2a[ch * 128:(ch + 1) * 128, :],
               lambda ch: y2b[ch * 128:(ch + 1) * 128, :]]
        f34 = [lambda ch: y3[ch * 128:(ch + 1) * 128, :]]
        for emitfn in (lambda: emit_p1(c, em, qn_dram, y1a, y1b),
                       lambda: emit_p2(c, em, r_int, r_bufs, f12, y2a, y2b),
                       lambda: emit_p3(c, em, r_int, r_bufs, f23, y3, qn_dram),
                       lambda: emit_p4(c, em, r_int, r_bufs, f34, y_out)):
            with c.nc.Block() as blk:
                emitfn()
                em.flush(blk)
        finish(c, em)
        c.nc.finalize()
    return c


FUSED_INPUTS = (["xqT", "xkvT", "w_dq", "w_uq", "w_dkv", "w_ukv", "wo_t",
                 "qgb", "kvgb", "l0_bnd", "l0_mcol", "l0_mcol_bf"]
                + P2_INPUTS + P3_INPUTS + P4_INPUTS)



FUSED = True

_BUILD_CACHE = {}


def _get(name, fn):
    if name not in _BUILD_CACHE:
        _BUILD_CACHE[name] = fn()
    return _BUILD_CACHE[name]


def kernel(**inputs):
    shared = stage_shared(inputs)
    cores = [stage_core(inputs, core) for core in range(NCORES)]

    if FUSED:
        cf = _get("fused", build_fused)
        in_maps = []
        for core in range(NCORES):
            m = {}
            for k in FUSED_INPUTS:
                m[k] = shared[k] if k in shared else cores[core][k]
            for nm, _, _ in CONSTS:
                m[nm] = shared[nm]
            in_maps.append(m)
        res = run_bass_kernel_spmd(cf.nc, in_maps, core_ids=list(range(NCORES)))
        out = np.empty((4, 2048, 2048), dtype=np.float32)
        for core in range(NCORES):
            b, par = core // 2, core % 2
            out[b, par * 1024:(par + 1) * 1024, :] = res.results[core]["y_out"]
        return out

    def maps(inames, r_key):
        out = []
        for core in range(NCORES):
            m = {}
            for k in inames:
                m[k] = shared[k] if k in shared else cores[core][k]
            for nm, _, _ in CONSTS:
                m[nm] = shared[nm]
            m["r_in"] = cores[core][r_key] if isinstance(r_key, str) else r_key[core]
            out.append(m)
        return out

    c1 = _get("p1", build_p1)
    res = run_bass_kernel_spmd(c1.nc, maps(P1_INPUTS, "xqT"),
                               core_ids=list(range(NCORES)))
    r = [res.results[i]["r_out"] for i in range(NCORES)]
    c2 = _get("p2", build_p2)
    res = run_bass_kernel_spmd(c2.nc, maps(P2_INPUTS, r),
                               core_ids=list(range(NCORES)))
    r = [res.results[i]["r_out"] for i in range(NCORES)]
    c3 = _get("p3", build_p3)
    res = run_bass_kernel_spmd(c3.nc, maps(P3_INPUTS, r),
                               core_ids=list(range(NCORES)))
    r = [res.results[i]["r_out"] for i in range(NCORES)]
    c4 = _get("p4", build_p4)
    res = run_bass_kernel_spmd(c4.nc, maps(P4_INPUTS, r),
                               core_ids=list(range(NCORES)))
    out = np.empty((4, 2048, 2048), dtype=np.float32)
    for core in range(NCORES):
        b, par = core // 2, core % 2
        out[b, par * 1024:(par + 1) * 1024, :] = res.results[core]["y_out"]
    return out
